# version 01
# baseline (speedup 1.0000x reference)
import numpy as np
import concourse.bass as bass
import concourse.mybir as mybir
from concourse.bass_utils import run_bass_kernel_spmd
from contextlib import ExitStack

F32 = mybir.dt.float32
BF16 = mybir.dt.bfloat16
AF = mybir.ActivationFunctionType
ALU = mybir.AluOpType
AX = mybir.AxisListType

T = 4096
D = 1024
NT = 32
NH = 16
DH = 64
NP = 8
NE = 16
DI = 512
NTB = 8
NCH = 8
LCLIP = float(np.log(np.float32(1e-7)))

PHASES = ("A", "F1", "B", "F2")
A2_QBS = None
DEV_TAPS = {}


class Sem:
    def __init__(self, ctx, nc, name, step=1):
        self.h = ctx.enter_context(nc.semaphore(name))
        self.n = 0
        self.step = step

    def inc(self, ins, k=1):
        ins.then_inc(self.h, k * self.step)
        self.n += k * self.step
        return self.n

    def wait(self, eng, at=None):
        eng.wait_ge(self.h, self.n if at is None else at)


class Builder:
    def __init__(self):
        nc = bass.Bass("TRN2", target_bir_lowering=False, dynamic_dma_scratch_size=8192)
        self.nc = nc
        self.te, self.ve, self.se, self.ge, self.sy = (
            nc.tensor, nc.vector, nc.scalar, nc.gpsimd, nc.sync)
        self.ctx = ExitStack()
        self.sLD = Sem(self.ctx, nc, "sLD", 16)
        self.sST = Sem(self.ctx, nc, "sST", 16)
        self.sPE = Sem(self.ctx, nc, "sPE")
        self.sVE = Sem(self.ctx, nc, "sVE")
        self.sAC = Sem(self.ctx, nc, "sAC")
        self.sGE = Sem(self.ctx, nc, "sGE")
        self._dram()
        self.taps = {}
        for name, (shape, dt) in DEV_TAPS.items():
            self.taps[name] = nc.dram_tensor(name, list(shape), dt,
                                             kind="ExternalOutput").ap()

    def sb(self, ctx, name, shape, dtype):
        return ctx.enter_context(self.nc.sbuf_tensor(name, shape, dtype))

    def psum(self, ctx, name, shape, dtype):
        return ctx.enter_context(self.nc.psum_tensor(name, shape, dtype))

    def _dram(self):
        nc = self.nc
        self.X = nc.dram_tensor("x", [T, D], F32, kind="ExternalInput").ap()
        self.WQKVT = nc.dram_tensor("wqkvT", [D, 3 * D], F32, kind="ExternalInput").ap()
        self.BQKV = nc.dram_tensor("bqkv", [1, 3 * D], F32, kind="ExternalInput").ap()
        self.WOT = nc.dram_tensor("woT", [D, D], F32, kind="ExternalInput").ap()
        self.BO = nc.dram_tensor("bo", [1, D], F32, kind="ExternalInput").ap()
        self.WQKROW = nc.dram_tensor("wqkrow", [1, 2 * D], F32, kind="ExternalInput").ap()
        self.IGW = nc.dram_tensor("igw", [NCH, 128, 128], F32, kind="ExternalInput").ap()
        self.IGB = nc.dram_tensor("igb", [NCH, 128], F32, kind="ExternalInput").ap()
        self.AGW = nc.dram_tensor("agw", [NCH, 128, 128], F32, kind="ExternalInput").ap()
        self.AGB = nc.dram_tensor("agb", [NCH, 128], F32, kind="ExternalInput").ap()
        self.APAR = nc.dram_tensor("apar", [NCH, 128], F32, kind="ExternalInput").ap()
        self.GWT = nc.dram_tensor("gwT", [D, NE], F32, kind="ExternalInput").ap()
        self.W1 = nc.dram_tensor("w1", [NE, NCH, 128, DI], F32, kind="ExternalInput").ap()
        self.W3 = nc.dram_tensor("w3", [NE, NCH, 128, DI], F32, kind="ExternalInput").ap()
        self.W2 = nc.dram_tensor("w2", [NE * 4, 128, D], F32, kind="ExternalInput").ap()
        self.OUT = nc.dram_tensor("out", [T, D], F32, kind="ExternalOutput").ap()
        self.XR = nc.dram_tensor("xres", [T, D], F32, kind="Internal").ap()
        self.QS = nc.dram_tensor("qsc", [NT, 128, D], BF16, kind="Internal").ap()
        self.ZS = nc.dram_tensor("zsc", [NP, 128, T], BF16, kind="Internal").ap()
        self.W1B = nc.dram_tensor("w1bd", [NE, NCH, 128, DI], BF16, kind="Internal").ap()
        self.W3B = nc.dram_tensor("w3bd", [NE, NCH, 128, DI], BF16, kind="Internal").ap()
        self.W2B = nc.dram_tensor("w2bd", [NE * 4, 128, D], BF16, kind="Internal").ap()
        self.MSC = nc.dram_tensor("msc", [8, 32], F32, kind="Internal").ap()
        self.SFD = nc.dram_tensor("sfd", [NE, T], BF16, kind="Internal").ap()

    def tap(self, name, src_ap):
        if name in self.taps:
            self.sVE.wait(self.sy)
            self.sAC.wait(self.sy)
            self.sPE.wait(self.sy)
            self.sST.inc(self.sy.dma_start(out=self.taps[name], in_=src_ap))

    def prelude(self):
        ctx, nc = self.ctx, self.nc
        te, ve, se, ge, sy = self.te, self.ve, self.se, self.ge, self.sy
        sLD, sVE, sAC, sPE, sGE = self.sLD, self.sVE, self.sAC, self.sPE, self.sGE

        self.ident_b = self.sb(ctx, "ident_b", [128, 128], BF16)
        self.ident_f = self.sb(ctx, "ident_f", [128, 128], F32)
        self.ones_b = self.sb(ctx, "ones_b", [1, 512], BF16)
        self.ones_f = self.sb(ctx, "ones_f", [1, 128], F32)
        self.c_eps_ln = self.sb(ctx, "c_eps_ln", [128, 1], F32)
        self.c_eps_rms = self.sb(ctx, "c_eps_rms", [128, 1], F32)
        self.c_one = self.sb(ctx, "c_one", [128, 1], F32)
        self.sp8 = self.sb(ctx, "sp8", [128, NCH], F32)
        self.lnst = self.sb(ctx, "lnst", [128, 12], F32)
        self.lnmv = self.sb(ctx, "lnmv", [128, 2], F32)
        self.lnr = self.sb(ctx, "lnr", [128, 2], F32)

        with ExitStack() as pctx:
            iscr = self.sb(pctx, "iscr", [128, 128], F32)
            stg1 = self.sb(pctx, "stg1", [1, 3 * D], F32)
            stg2 = self.sb(pctx, "stg2", [128, NCH], F32)
            sp_t = self.sb(pctx, "sp_t", [128, NCH], F32)
            ps_pre = self.psum(pctx, "ps_pre", [128, 512], F32)

            sGE.inc(ge.iota(out=iscr[:], pattern=[[-1, 128]], base=0,
                            channel_multiplier=1,
                            allow_small_or_imprecise_dtypes=True))
            ve.memset(self.ones_b[:], 1.0)
            ve.memset(self.ones_f[:], 1.0)
            ve.memset(self.c_eps_ln[:], 1e-5)
            ve.memset(self.c_eps_rms[:], 1e-6)
            sVE.inc(ve.memset(self.c_one[:], 1.0))
            sGE.wait(ve)
            ve.tensor_scalar(out=self.ident_b[:], in0=iscr[:], scalar1=0.0,
                             scalar2=None, op0=ALU.is_equal)
            sVE.inc(ve.tensor_scalar(out=self.ident_f[:], in0=iscr[:],
                                     scalar1=0.0, scalar2=None, op0=ALU.is_equal))

            for c in range(NCH):
                sLD.inc(sy.dma_start(
                    out=stg2[:, c:c + 1],
                    in_=self.APAR[c:c + 1, :].rearrange("a b -> b a")))
            sLD.wait(se)
            sAC.inc(se.activation(out=sp_t[:], in_=stg2[:], func=AF.Abs))
            sAC.wait(se)
            sAC.inc(se.activation(out=sp_t[:], in_=sp_t[:], func=AF.Exp, scale=-1.0))
            sAC.wait(se)
            sVE.wait(se)
            sAC.inc(se.activation(out=sp_t[:], in_=sp_t[:], func=AF.Ln,
                                  bias=self.c_one[:, 0:1], scale=1.0))
            sLD.wait(ve)
            sAC.wait(ve)
            sVE.inc(ve.tensor_scalar(out=self.sp8[:], in0=stg2[:], scalar1=0.0,
                             scalar2=None, op0=ALU.max))
            sVE.wait(ve)
            sVE.inc(ve.tensor_tensor(out=self.sp8[:], in0=self.sp8[:], in1=sp_t[:], op=ALU.add))
            sVE.wait(ve)
            sVE.inc(ve.tensor_scalar(out=self.sp8[:], in0=self.sp8[:],
                                     scalar1=-8.0, scalar2=None, op0=ALU.mult))
            sVE.wait(te)
            sVE.wait(ge)
            sVE.wait(sy)

    def emit_ln(self, xbuf, out_bf, eps_ap):
        ve, se = self.ve, self.se
        sVE, sAC = self.sVE, self.sAC
        ve.bn_stats(out=self.lnst[:, 0:6], in_=xbuf[:, 0:512])
        sVE.inc(ve.bn_stats(out=self.lnst[:, 6:12], in_=xbuf[:, 512:1024]))
        sVE.wait(ve)
        sVE.inc(ve.bn_aggr(out=self.lnmv[:], in_=self.lnst[:]))
        sVE.wait(ve)
        sVE.wait(se)
        sAC.inc(se.activation(out=self.lnr[:, 0:1], in_=self.lnmv[:, 1:2], func=AF.Ln,
                      bias=eps_ap, scale=1.0))
        sAC.wait(se)
        sAC.inc(se.activation(out=self.lnr[:, 0:1], in_=self.lnr[:, 0:1],
                              func=AF.Exp, scale=-0.5))
        sAC.wait(ve)
        sVE.inc(ve.scalar_tensor_tensor(out=self.lnr[:, 1:2], in0=self.lnmv[:, 0:1],
                                        scalar=-1.0, in1=self.lnr[:, 0:1],
                                        op0=ALU.mult, op1=ALU.mult))
        sVE.wait(ve)
        sVE.inc(ve.tensor_scalar(out=out_bf, in0=xbuf,
                                 scalar1=self.lnr[:, 0:1], scalar2=self.lnr[:, 1:2],
                                 op0=ALU.mult, op1=ALU.add))

    def phase_a(self):
        ctx = self.ctx
        nc = self.nc
        te, ve, se, ge, sy = self.te, self.ve, self.se, self.ge, self.sy
        sLD, sST, sPE, sVE, sAC = self.sLD, self.sST, self.sPE, self.sVE, self.sAC

        with ExitStack() as actx:
            self.wqkb = self.sb(actx, "wqkb", [128, 2 * D], BF16)
            self.bqkv_b = self.sb(actx, "bqkv_b", [1, 3 * D], BF16)
            self.bo_b = self.sb(actx, "bo_b", [1, D], BF16)
            with ExitStack() as wctx:
                stg1 = self.sb(wctx, "stg1a", [1, 3 * D], F32)
                ps_pre = self.psum(wctx, "ps_prea", [128, 512], F32)
                sLD.inc(sy.dma_start(out=stg1[0:1, 0:2 * D], in_=self.WQKROW))
                sLD.wait(te)
                self.sVE.wait(te)
                for nb in range(4):
                    if nb:
                        self.sVE.wait(te)
                    sPE.inc(te.matmul(ps_pre[:], self.ones_f[0:1, :],
                                      stg1[0:1, nb * 512:(nb + 1) * 512],
                                      start=True, stop=True))
                    sPE.wait(ve)
                    sVE.inc(ve.tensor_copy(out=self.wqkb[:, nb * 512:(nb + 1) * 512],
                                           in_=ps_pre[:]))
                sVE.wait(sy)
                sLD.inc(sy.dma_start(out=stg1[0:1, :], in_=self.BQKV))
                sLD.wait(ve)
                sVE.inc(ve.tensor_copy(out=self.bqkv_b[:], in_=stg1[0:1, :]))
                sVE.wait(sy)
                sLD.inc(sy.dma_start(out=stg1[0:1, 0:D], in_=self.BO))
                sLD.wait(ve)
                sVE.inc(ve.tensor_copy(out=self.bo_b[:], in_=stg1[0:1, 0:D]))
                sVE.wait(sy)
                sVE.wait(te)

            kvctx = ExitStack()
            K_one = self.sb(kvctx, "K_one", [128, NP * T], BF16)
            V_aug = self.sb(kvctx, "V_aug", [128, NT * NH * 65], BF16)
            r_fm = self.sb(kvctx, "r_fm", [32, T], BF16)

            with ExitStack() as a1:
                xls = self.sb(a1, "xls", [128, 2, D], F32)
                xlnb = self.sb(a1, "xlnb", [128, D], BF16)
                xlf_t = self.sb(a1, "xlf_t", [128, D], BF16)
                wst = self.sb(a1, "wst", [128, 2048], F32)
                q2sb = wst
                s2h = self.sb(a1, "s2h", [128, 32], F32)
                rstdh = self.sb(a1, "rstdh", [128, 32], F32)
                qn1 = self.sb(a1, "qn1", [128, 512], F32)
                qkn = self.sb(a1, "qkn", [128, 2048], BF16)
                rtm = self.sb(a1, "rtm", [128, 32], F32)
                qstg = self.sb(a1, "qstg", [128, 2, D], BF16)
                ps_t = self.psum(a1, "ps_t", [128, 1024], BF16)
                ps_qk = self.psum(a1, "ps_qk", [128, 2048], F32)
                ps_qt = self.psum(a1, "ps_qt", [128, 1024], BF16)
                ps_kt = self.psum(a1, "ps_kt", [128, 1024], BF16)
                ps_r = self.psum(a1, "ps_r", [32, 128], F32)

                with ExitStack() as a1a:
                    wqk_b = self.sb(a1a, "wqk_b", [128, NCH * 2048], BF16)
                    wqk_bv = wqk_b.ap().rearrange("p (c n) -> p c n", c=NCH)
                    cast_done = []
                    for c in range(NCH):
                        if c >= 1:
                            sy.wait_ge(sVE.h, cast_done[c - 1])
                        sLD.inc(sy.dma_start(out=wst[:],
                                             in_=self.WQKVT[c * 128:(c + 1) * 128, 0:2048]))
                        sLD.wait(ve)
                        sVE.inc(ve.tensor_copy(out=wqk_bv[:, c, :], in_=wst[:]))
                        cast_done.append(sVE.n)

                    xld = []
                    sLD.inc(sy.dma_start(out=xls[:, 0, :], in_=self.X[0:128, :]))
                    xld.append(sLD.n)
                    sLD.inc(sy.dma_start(out=xls[:, 1, :], in_=self.X[128:256, :]))
                    xld.append(sLD.n)

                    sn_x = [0] * NT
                    sn_t = [0] * NT
                    sn_qk_ve = [0] * NT
                    sn_qk_ac = [0] * NT
                    sn_r = [0] * NT
                    sn_qt = [0] * NT
                    sn_qst = [0] * NT
                    sn_xtr_pe = [0] * NT
                    sn_qtr_pe = [0] * NT

                    for tt in range(NT):
                        xbuf = xls[:, tt % 2, :]
                        sLD.wait(ve, at=xld[tt])
                        if tt >= 1:
                            ve.wait_ge(sPE.h, sn_xtr_pe[tt - 1])
                        self.emit_ln(xbuf, xlnb[:], self.c_eps_ln[:, 0:1])
                        sn_x[tt] = self.sVE.n
                        if tt + 2 < NT:
                            sy.wait_ge(sVE.h, sn_x[tt])
                            sLD.inc(sy.dma_start(out=xls[:, tt % 2, :],
                                                 in_=self.X[(tt + 2) * 128:(tt + 3) * 128, :]))
                            xld.append(sLD.n)
                        if tt >= 1:
                            te.wait_ge(sVE.h, sn_t[tt - 1])
                        sVE.wait(te)
                        for c in range(NCH):
                            ins = te.transpose(ps_t[:, c * 128:(c + 1) * 128],
                                               xlnb[:, c * 128:(c + 1) * 128],
                                               self.ident_b[:])
                        sPE.inc(ins)
                        sn_xtr_pe[tt] = sPE.n
                        sPE.wait(ve)
                        sVE.inc(ve.tensor_copy(out=xlf_t[:], in_=ps_t[:]))
                        sn_t[tt] = sVE.n
                        xlf_ready = sVE.n

                        for nb in range(4):
                            te.wait_ge(sVE.h, xlf_ready)
                            if tt >= 1:
                                te.wait_ge(sVE.h, sn_qk_ve[tt - 1])
                                te.wait_ge(sAC.h, sn_qk_ac[tt - 1])
                            if nb == 0 and tt >= 1:
                                ve.wait_ge(sPE.h, sn_qtr_pe[tt - 1])
                            bank = ps_qk[:, nb * 512:(nb + 1) * 512]
                            for c in range(NCH):
                                te.matmul(bank, xlf_t[:, c * 128:(c + 1) * 128],
                                          wqk_bv[:, c, nb * 512:(nb + 1) * 512],
                                          start=(c == 0), stop=False)
                            sPE.inc(te.matmul(bank, self.ones_b[0:1, 0:128],
                                              self.bqkv_b[0:1, nb * 512:(nb + 1) * 512],
                                              start=False, stop=True))
                            sPE.wait(se)
                            sAC.inc(se.activation(out=q2sb[:, nb * 512:(nb + 1) * 512],
                                                  in_=bank, func=AF.Square))
                            sAC.wait(ve)
                            sVE.inc(ve.tensor_reduce(
                                out=s2h[:, nb * 8:(nb + 1) * 8],
                                in_=q2sb.ap().rearrange("p (h d) -> p h d", d=DH)[:, nb * 8:(nb + 1) * 8, :],
                                op=ALU.add, axis=AX.X))
                            sVE.wait(se)
                            sAC.inc(se.activation(out=rstdh[:, nb * 8:(nb + 1) * 8],
                                          in_=s2h[:, nb * 8:(nb + 1) * 8], func=AF.Ln,
                                          bias=self.c_eps_rms[:, 0:1], scale=1.0 / DH))
                            sAC.wait(se)
                            sAC.inc(se.activation(out=rstdh[:, nb * 8:(nb + 1) * 8],
                                                  in_=rstdh[:, nb * 8:(nb + 1) * 8],
                                                  func=AF.Exp, scale=-0.5))
                            sAC.wait(ve)
                            ve.tensor_tensor(
                                out=qn1.ap().rearrange("p (h d) -> p h d", d=DH),
                                in0=ps_qk.ap().rearrange("p (h d) -> p h d", d=DH)[:, nb * 8:(nb + 1) * 8, :],
                                in1=rstdh[:, nb * 8:(nb + 1) * 8].unsqueeze(2).to_broadcast([128, 8, DH]),
                                op=ALU.mult)
                            sVE.inc(ve.tensor_tensor(
                                out=qkn[:, nb * 512:(nb + 1) * 512], in0=qn1[:],
                                in1=self.wqkb[:, nb * 512:(nb + 1) * 512], op=ALU.mult))
                        sn_qk_ve[tt] = sVE.n
                        sn_qk_ac[tt] = sAC.n

                        sVE.wait(se)
                        sAC.inc(se.activation(out=q2sb[:], in_=qkn[:], func=AF.Square))
                        sAC.wait(ve)
                        sVE.inc(ve.tensor_reduce(
                            out=rtm[:],
                            in_=q2sb.ap().rearrange("p (h d) -> p h d", d=DH),
                            op=ALU.add, axis=AX.X))
                        sVE.wait(se)
                        sAC.inc(se.activation(out=rtm[:], in_=rtm[:], func=AF.Ln,
                                      bias=self.c_eps_rms[:, 0:1], scale=1.0))
                        sAC.wait(se)
                        sAC.inc(se.activation(out=rtm[:], in_=rtm[:], func=AF.Exp,
                                              scale=0.5))
                        if tt >= 1:
                            te.wait_ge(sVE.h, sn_r[tt - 1])
                        sAC.wait(te)
                        sPE.inc(te.transpose(ps_r[:], rtm[:, 0:32], self.ident_f[:]))
                        sPE.wait(ve)
                        sVE.inc(ve.tensor_copy(out=r_fm[:, tt * 128:(tt + 1) * 128],
                                               in_=ps_r[:]))
                        sn_r[tt] = sVE.n

                        if tt >= 1:
                            te.wait_ge(sVE.h, sn_qt[tt - 1])
                        for p in range(NP):
                            te.transpose(ps_qt[:, p * 128:(p + 1) * 128],
                                         qkn[:, p * 128:(p + 1) * 128], self.ident_b[:])
                            ins = te.transpose(ps_kt[:, p * 128:(p + 1) * 128],
                                               qkn[:, 1024 + p * 128:1024 + (p + 1) * 128],
                                               self.ident_b[:])
                        sPE.inc(ins)
                        sn_qtr_pe[tt] = sPE.n
                        sPE.wait(ve)
                        if tt >= 2:
                            ve.wait_ge(sST.h, sn_qst[tt - 2])
                        ve.tensor_copy(out=qstg[:, tt % 2, :], in_=ps_qt[:])
                        sVE.inc(ve.tensor_copy(
                            out=K_one.ap().rearrange("p (c t) -> p c t", c=NP)[:, :, tt * 128:(tt + 1) * 128],
                            in_=ps_kt.ap().rearrange("p (c t) -> p c t", c=NP)))
                        sn_qt[tt] = sVE.n
                        sVE.wait(sy)
                        sST.inc(sy.dma_start(out=self.QS[tt, :, :],
                                             in_=qstg.ap()[:, tt % 2, :]))
                        sn_qst[tt] = sST.n
                    sVE.wait(sy)
                    sVE.wait(te)
                    sAC.wait(te)

                with ExitStack() as a1b:
                    wv_b = self.sb(a1b, "wv_b", [128, NCH * 1024], BF16)
                    wv_bv = wv_b.ap().rearrange("p (c n) -> p c n", c=NCH)
                    ps_v = ps_qk
                    cast_done = []
                    for c in range(NCH):
                        if c >= 1:
                            sy.wait_ge(sVE.h, cast_done[c - 1])
                        sLD.inc(sy.dma_start(out=wst[:, 0:1024],
                                             in_=self.WQKVT[c * 128:(c + 1) * 128, 2048:3072]))
                        sLD.wait(ve)
                        sVE.inc(ve.tensor_copy(out=wv_bv[:, c, :], in_=wst[:, 0:1024]))
                        cast_done.append(sVE.n)
                    sVE.inc(ve.memset(V_aug[:], 1.0))
                    vones_ready = sVE.n

                    xld = []
                    sLD.inc(sy.dma_start(out=xls[:, 0, :], in_=self.X[0:128, :]))
                    xld.append(sLD.n)
                    sLD.inc(sy.dma_start(out=xls[:, 1, :], in_=self.X[128:256, :]))
                    xld.append(sLD.n)
                    sn_x = [0] * NT
                    sn_t = [0] * NT
                    sn_v_ac = [0] * NT
                    for tt in range(NT):
                        xbuf = xls[:, tt % 2, :]
                        sLD.wait(ve, at=xld[tt])
                        self.emit_ln(xbuf, xlnb[:], self.c_eps_ln[:, 0:1])
                        sn_x[tt] = sVE.n
                        if tt + 2 < NT:
                            sy.wait_ge(sVE.h, sn_x[tt])
                            sLD.inc(sy.dma_start(out=xls[:, tt % 2, :],
                                                 in_=self.X[(tt + 2) * 128:(tt + 3) * 128, :]))
                            xld.append(sLD.n)
                        if tt >= 1:
                            te.wait_ge(sVE.h, sn_t[tt - 1])
                        sVE.wait(te)
                        for c in range(NCH):
                            ins = te.transpose(ps_t[:, c * 128:(c + 1) * 128],
                                               xlnb[:, c * 128:(c + 1) * 128],
                                               self.ident_b[:])
                        sPE.inc(ins)
                        sPE.wait(ve)
                        sVE.inc(ve.tensor_copy(out=xlf_t[:], in_=ps_t[:]))
                        sn_t[tt] = sVE.n
                        xlf_ready = sVE.n
                        for nb in range(2):
                            te.wait_ge(sVE.h, xlf_ready)
                            if tt == 0:
                                te.wait_ge(sVE.h, vones_ready)
                            if tt >= 1:
                                te.wait_ge(sVE.h, sn_v_ac[tt - 1])
                            bank = ps_v[:, nb * 512:(nb + 1) * 512]
                            for c in range(NCH):
                                te.matmul(bank, xlf_t[:, c * 128:(c + 1) * 128],
                                          wv_bv[:, c, nb * 512:(nb + 1) * 512],
                                          start=(c == 0), stop=False)
                            sPE.inc(te.matmul(bank, self.ones_b[0:1, 0:128],
                                              self.bqkv_b[0:1, 2048 + nb * 512:2048 + (nb + 1) * 512],
                                              start=False, stop=True))
                            sPE.wait(ve)
                            sVE.inc(ve.tensor_copy(
                                out=V_aug.ap().rearrange("p (t h e) -> p t h e", t=NT, h=NH)[:, tt, nb * 8:(nb + 1) * 8, 0:64],
                                in_=bank.rearrange("p (h e) -> p h e", e=DH)))
                        sn_v_ac[tt] = sVE.n
                    sVE.wait(sy)
                    sVE.wait(te)
                    sAC.wait(te)

            if self.taps:
                self.tap("t_kone", K_one.ap())
                self.tap("t_vaug", V_aug.ap())
                self.tap("t_rfm", r_fm.ap())
                self.tap("t_qs", self.QS)

            with ExitStack() as a2:
                P = self.sb(a2, "P", [128, NT * 512], BF16)
                Qb = self.sb(a2, "Qb", [128, 2, 512], BF16)
                rk_col = self.sb(a2, "rk_col", [32, 1], F32)
                rqb32 = self.sb(a2, "rqb32", [32, 8], F32)
                rqbT = self.sb(a2, "rqbT", [8, 32], F32)
                rkT = self.sb(a2, "rkT", [1, 32], F32)
                rq_row = self.sb(a2, "rq_row", [1, 256], F32)
                negm_row = self.sb(a2, "negm_row", [1, 128], F32)
                negm_mat = self.sb(a2, "negm_mat", [128, 128], F32)
                rl = self.sb(a2, "rl", [1, 512], F32)
                bc_sb = self.sb(a2, "bc_sb", [64, 512], F32)
                zstg = self.sb(a2, "zstg", [128, 2, 512], BF16)
                ps_sA = self.psum(a2, "ps_sA", [128, 1536], F32)
                ps_sB = self.psum(a2, "ps_sB", [128, 1536], F32)
                ps_z = self.psum(a2, "ps_z", [128, 512], F32)
                ps_m = self.psum(a2, "ps_m", [128, 512], F32)

                Vv = V_aug.ap().rearrange("p (t h e) -> p t h e", t=NT, h=NH)
                Kv = K_one.ap().rearrange("p (c t) -> p c t", c=NP)

                sVE.inc(ve.tensor_reduce(out=rk_col[:], in_=r_fm.ap(),
                                         op=ALU.max, axis=AX.X))
                sVE.inc(ve.tensor_reduce(
                    out=rqb32[:],
                    in_=r_fm.ap().rearrange("h (q t) -> h q t", q=8),
                    op=ALU.max, axis=AX.X))
                sVE.wait(te)
                te.transpose(ps_m[0:8, 0:32], rqb32[:], self.ident_f[0:32, 0:32])
                sPE.inc(te.transpose(ps_m[0:1, 32:64], rk_col[:], self.ident_f[0:32, 0:32]))
                sPE.wait(ve)
                ve.tensor_copy(out=rqbT[:], in_=ps_m[0:8, 0:32])
                sVE.inc(ve.tensor_copy(out=rkT[:], in_=ps_m[0:1, 32:64]))
                sVE.wait(sy)
                sST.inc(sy.dma_start(out=self.MSC, in_=rqbT[:]))
                sST.wait(sy)
                sLD.inc(sy.dma_start(out=rq_row[:],
                                     in_=self.MSC.rearrange("a b -> (a b)").unsqueeze(0)))
                sLD.wait(ve)
                sVE.inc(ve.scalar_tensor_tensor(
                    out=negm_row.ap().rearrange("a (q h) -> a q h", h=16),
                    in0=rq_row.ap().rearrange("a (q h) -> a q h", h=32)[:, :, 0:16],
                    scalar=-0.5,
                    in1=rkT.ap()[0:1, 16:32].unsqueeze(1).to_broadcast([1, 8, 16]),
                    op0=ALU.mult, op1=ALU.mult))
                sVE.wait(te)
                sPE.inc(te.matmul(ps_m[:, 0:128], self.ones_f[0:1, :],
                                  negm_row[:], start=True, stop=True))
                sPE.wait(ve)
                sVE.inc(ve.tensor_copy(out=negm_mat[:], in_=ps_m[:, 0:128]))
                self.tap("t_negm", negm_mat.ap())

                groups = [(g * 3, min(3, NT - g * 3)) for g in range((NT + 2) // 3)]
                sn_grp = {0: 0, 1: 0}
                sn_z_ve = 0
                sn_m_ve = sVE.n
                sn_q = {}
                sn_zst = {}
                sn_zstg_ve = {}
                for qb in (range(NTB) if A2_QBS is None else A2_QBS):
                    for pc in range(NP):
                        prev = (qb * NP + pc) - 2
                        if prev >= 0:
                            pq, pp = divmod(prev, NP)
                            sy.wait_ge(sVE.h, sn_zstg_ve.get((pq, pp), 0))
                        sLD.inc(sy.dma_start(
                            out=Qb.ap()[:, pc % 2, :].rearrange("p (t d) -> p t d", t=4),
                            in_=self.QS.rearrange("t p d -> p t d")[:, qb * 4:(qb + 1) * 4, pc * 128:(pc + 1) * 128]))
                        sn_q[(qb, pc)] = sLD.n
                        for hh in range(2):
                            h = 2 * pc + hh
                            base = 64 * hh
                            bias_ap = negm_mat[:, qb * 16 + h:qb * 16 + h + 1]
                            te.wait_ge(sAC.h, self.sAC.n)

                            sLD.wait(te, at=sn_q[(qb, pc)])
                            for gi, (k0, glen) in enumerate(groups):
                                pss = ps_sA if gi % 2 == 0 else ps_sB
                                te.wait_ge(sAC.h, sn_grp[gi % 2])
                                for kk in range(glen):
                                    kt = k0 + kk
                                    ins = te.matmul(
                                        pss[:, kk * 512:(kk + 1) * 512],
                                        Kv[base:base + 64, pc, kt * 128:(kt + 1) * 128],
                                        Qb[base:base + 64, pc % 2, :],
                                        start=True, stop=True)
                                sPE.inc(ins)
                                sPE.wait(se)
                                sAC.inc(se.activation(
                                    out=P[:, k0 * 512:(k0 + glen) * 512],
                                    in_=pss[:, 0:glen * 512],
                                    func=AF.Exp, bias=bias_ap, scale=1.0))
                                sn_grp[gi % 2] = sAC.n
                            te.wait_ge(sVE.h, sn_z_ve)
                            te.wait_ge(sAC.h, sn_grp[0])
                            te.wait_ge(sAC.h, sn_grp[1])
                            for kt in range(NT):
                                ins = te.matmul(ps_z[0:65, :],
                                                Vv[:, kt, h, :],
                                                P[:, kt * 512:(kt + 1) * 512],
                                                start=(kt == 0), stop=(kt == NT - 1))
                            sPE.inc(ins)
                            sn_av = self.sPE.n
                            sPE.wait(ve)
                            sVE.inc(ve.tensor_scalar(out=rl[:], in0=ps_z[64:65, :],
                                             scalar1=1e-30, scalar2=None, op0=ALU.add))
                            sVE.wait(ve)
                            sVE.inc(ve.reciprocal(out=rl[:], in_=rl[:]))
                            sn_rl = sVE.n
                            te.wait_ge(sVE.h, sn_rl)
                            sPE.inc(te.matmul(ps_m[0:64, :], self.ones_f[0:1, 0:64],
                                              rl[:], start=True, stop=True))
                            sPE.wait(se)
                            se.wait_ge(sVE.h, sn_z_ve)
                            sAC.inc(se.activation(out=bc_sb[:], in_=ps_m[0:64, :],
                                                  func=AF.Identity))
                            sAC.wait(ve)
                            prev2 = (qb * NP + pc) - 2
                            if prev2 >= 0:
                                ve.wait_ge(sST.h, sn_zst.get(divmod(prev2, NP), 0))
                            sVE.inc(ve.tensor_tensor(out=zstg[base:base + 64, pc % 2, :],
                                                     in0=ps_z[0:64, :], in1=bc_sb[:],
                                                     op=ALU.mult))
                            sn_z_ve = sVE.n
                            sn_m_ve = sVE.n
                        sn_zstg_ve[(qb, pc)] = sVE.n
                        sVE.wait(sy)
                        sST.inc(sy.dma_start(out=self.ZS[pc, :, qb * 512:(qb + 1) * 512],
                                             in_=zstg.ap()[:, pc % 2, :]))
                        sn_zst[(qb, pc)] = sST.n
                sVE.wait(te)
                sAC.wait(te)
                sVE.wait(sy)
                self.tap("t_p", P.ap())
                self.tap("t_qb", Qb.ap())

            if self.taps:
                self.tap("t_zs", self.ZS)
            kvctx.close()
            with ExitStack() as a3:
                wo_b = self.sb(a3, "wo_b", [128, NCH * 1024], BF16)
                wo_bv = wo_b.ap().rearrange("p (c n) -> p c n", c=NCH)
                wstA3 = self.sb(a3, "wstA3", [128, 1024], F32)
                zbuf = self.sb(a3, "zbuf", [128, 2, NP, 512], BF16)
                xrt = self.sb(a3, "xrt", [128, 4, D], F32)
                ofm = self.sb(a3, "ofm", [128, 4096], BF16)
                xstg = self.sb(a3, "xstg", [128, 2, D], F32)
                ps_o = self.psum(a3, "ps_o", [128, 2048], F32)
                ps_T = self.psum(a3, "ps_T", [128, 1024], BF16)

                cast_done = []
                for c in range(NCH):
                    if c >= 1:
                        sy.wait_ge(sVE.h, cast_done[c - 1])
                    sLD.inc(sy.dma_start(out=wstA3[:],
                                         in_=self.WOT[c * 128:(c + 1) * 128, :]))
                    sLD.wait(ve)
                    sVE.inc(ve.tensor_copy(out=wo_bv[:, c, :], in_=wstA3[:]))
                    cast_done.append(sVE.n)

                zld = []
                for pc in range(NP):
                    sLD.inc(sy.dma_start(out=zbuf[:, 0, pc, :],
                                         in_=self.ZS[pc, :, 0:512]))
                zld.append(sLD.n)
                sn_o_ac = [0] * NTB
                sn_T_ve = [0] * NTB
                xst_list = []
                for tb in range(NTB):
                    if tb + 1 < NTB:
                        if tb >= 1:
                            sy.wait_ge(sPE.h, self.sPE.n)
                        for pc in range(NP):
                            sLD.inc(sy.dma_start(
                                out=zbuf[:, (tb + 1) % 2, pc, :],
                                in_=self.ZS[pc, :, (tb + 1) * 512:(tb + 2) * 512]))
                        zld.append(sLD.n)
                    if tb >= 1:
                        sy.wait_ge(sVE.h, sn_T_ve[tb - 1])
                    for ts4 in range(4):
                        sLD.inc(sy.dma_start(
                            out=xrt[:, ts4, :],
                            in_=self.X[tb * 512 + ts4 * 128:tb * 512 + (ts4 + 1) * 128, :]))
                    xrld = sLD.n
                    for half in range(2):
                        sLD.wait(te, at=zld[tb])
                        if tb >= 1 or half >= 1:
                            te.wait_ge(sAC.h, sn_o_ac[tb - 1] if half == 0 else self.sAC.n)
                        for oc4 in range(4):
                            oc = half * 4 + oc4
                            bank = ps_o[:, oc4 * 512:(oc4 + 1) * 512]
                            for zc in range(NP):
                                te.matmul(bank, wo_bv[:, zc, oc * 128:(oc + 1) * 128],
                                          zbuf[:, tb % 2, zc, :],
                                          start=(zc == 0), stop=False)
                            sPE.inc(te.matmul(bank, self.bo_b[0:1, oc * 128:(oc + 1) * 128],
                                              self.ones_b[0:1, :],
                                              start=False, stop=True))
                            sPE.wait(se)
                            sAC.inc(se.activation(out=ofm[:, oc * 512:(oc + 1) * 512],
                                                  in_=bank, func=AF.Identity))
                        sn_o_ac[tb] = sAC.n
                    for ts4 in range(4):
                        if tb >= 1 and ts4 == 0:
                            te.wait_ge(sVE.h, sn_T_ve[tb - 1])
                        if ts4 >= 1:
                            te.wait_ge(sVE.h, self.sVE.n)
                        sAC.wait(te)
                        for oc in range(NCH):
                            ins = te.transpose(ps_T[:, oc * 128:(oc + 1) * 128],
                                               ofm[:, oc * 512 + ts4 * 128:oc * 512 + (ts4 + 1) * 128],
                                               self.ident_b[:])
                        sPE.inc(ins)
                        sPE.wait(ve)
                        sLD.wait(ve, at=xrld)
                        if len(xst_list) >= 2:
                            ve.wait_ge(sST.h, xst_list[-2])
                        sVE.inc(ve.tensor_tensor(out=xstg[:, ts4 % 2, :], in0=ps_T[:],
                                                 in1=xrt[:, ts4, :], op=ALU.add))
                        sn_T_ve[tb] = sVE.n
                        sVE.wait(sy)
                        r0 = tb * 512 + ts4 * 128
                        sST.inc(sy.dma_start(out=self.XR[r0:r0 + 128, :],
                                             in_=xstg.ap()[:, ts4 % 2, :]))
                        xst_list.append(sST.n)
                sVE.wait(te)
                sAC.wait(te)
                sVE.wait(sy)
                sVE.wait(ge)

    def emit_ln_pass(self, src_dram, xlf, fstage, xlnb, ps_t):
        te, ve, se, sy = self.te, self.ve, self.se, self.sy
        sLD, sPE, sVE, sAC = self.sLD, self.sPE, self.sVE, self.sAC
        xlf_v = xlf.ap().rearrange("p (c t) -> p c t", c=NCH)
        xld = []
        sVE.wait(sy)
        sLD.inc(sy.dma_start(out=fstage[:, 0, :], in_=src_dram[0:128, :]))
        xld.append(sLD.n)
        sLD.inc(sy.dma_start(out=fstage[:, 1, :], in_=src_dram[128:256, :]))
        xld.append(sLD.n)
        sn_x = [0] * NT
        sn_t = [0] * NT
        sn_pe = [0] * NT
        for tt in range(NT):
            xbuf = fstage[:, tt % 2, :]
            sLD.wait(ve, at=xld[tt])
            if tt >= 1:
                ve.wait_ge(sPE.h, sn_pe[tt - 1])
            self.emit_ln(xbuf, xlnb[:], self.c_eps_ln[:, 0:1])
            sn_x[tt] = sVE.n
            if tt + 2 < NT:
                sy.wait_ge(sVE.h, sn_x[tt])
                sLD.inc(sy.dma_start(out=fstage[:, tt % 2, :],
                                     in_=src_dram[(tt + 2) * 128:(tt + 3) * 128, :]))
                xld.append(sLD.n)
            if tt >= 1:
                te.wait_ge(sVE.h, sn_t[tt - 1])
            sVE.wait(te)
            for c in range(NCH):
                ins = te.transpose(ps_t[:, c * 128:(c + 1) * 128],
                                   xlnb[:, c * 128:(c + 1) * 128], self.ident_b[:])
            sPE.inc(ins)
            sn_pe[tt] = sPE.n
            sPE.wait(ve)
            sVE.inc(ve.tensor_copy(
                out=xlf_v[:, :, tt * 128:(tt + 1) * 128],
                in_=ps_t.rearrange("p (c t) -> p c t", c=NCH)))
            sn_t[tt] = sVE.n

    def phase_ffn(self, src_dram, dst_dram, first_call):
        sfx = "a" if first_call else "b"
        te, ve, se, ge, sy = self.te, self.ve, self.se, self.ge, self.sy
        sLD, sST, sPE, sVE, sAC = self.sLD, self.sST, self.sPE, self.sVE, self.sAC
        NCK = NE * 4
        W1Bv = self.W1B.rearrange("e c p i -> e p c i")
        W3Bv = self.W3B.rearrange("e c p i -> e p c i")

        with ExitStack() as f:
            xlf = self.sb(f, "xlf" + sfx, [128, NCH * T], BF16)
            act_all = self.sb(f, "act_all" + sfx, [128, NCK * 512], BF16)
            w1b = self.sb(f, "w1b" + sfx, [128, 2, NCH * 512], BF16)
            w3b = self.sb(f, "w3b" + sfx, [128, 2, NCH * 512], BF16)
            w2b = self.sb(f, "w2b" + sfx, [128, 2, D], BF16)
            sbc = self.sb(f, "sbc" + sfx, [128, 512], F32)
            s_t = self.sb(f, "s_t" + sfx, [16, 128], BF16)
            s_row = self.sb(f, "s_row" + sfx, [1, 2, 512], BF16)
            fstage = self.sb(f, "fstage" + sfx, [128, 4, D], F32)
            xlnb = self.sb(f, "xlnb2" + sfx, [128, D], BF16)
            gw_b = self.sb(f, "gw_b" + sfx, [128, NCH * 16], BF16)
            rsc = self.sb(f, "rsc" + sfx, [128, 64], F32)
            ofm = self.sb(f, "ofm2" + sfx, [128, 4096], BF16)
            ps_g = self.psum(f, "ps_g" + sfx, [128, 2048], F32)
            ps_u = self.psum(f, "ps_u" + sfx, [128, 2048], F32)
            ps_t = ps_g.ap()[:, 0:512].bitcast(BF16)
            ps_o_banks = ([ps_g.ap()[:, i * 512:(i + 1) * 512] for i in range(4)]
                          + [ps_u.ap()[:, i * 512:(i + 1) * 512] for i in range(4)])

            if first_call:
                stg = act_all.ap().bitcast(F32)
                wsb = ofm.ap()
                for e in range(NE):
                    for (WSRC, WDST) in (
                        (self.W1.rearrange("e c p i -> e p c i"), W1Bv),
                        (self.W3.rearrange("e c p i -> e p c i"), W3Bv),
                    ):
                        sVE.wait(sy)
                        self.sST.wait(sy)
                        sLD.inc(sy.dma_start(
                            out=stg[:, 0:4096].rearrange("p (c i) -> p c i", c=NCH),
                            in_=WSRC[e]))
                        sLD.wait(ve)
                        sVE.inc(ve.tensor_copy(out=wsb[:, 0:4096], in_=stg[:, 0:4096]))
                        sVE.wait(sy)
                        sST.inc(sy.dma_start(
                            out=WDST[e],
                            in_=wsb[:, 0:4096].rearrange("p (c i) -> p c i", c=NCH)))
                for ck in range(NCK):
                    sVE.wait(sy)
                    self.sST.wait(sy)
                    sLD.inc(sy.dma_start(out=stg[:, 0:1024], in_=self.W2[ck]))
                    sLD.wait(ve)
                    sVE.inc(ve.tensor_copy(out=wsb[:, 0:1024], in_=stg[:, 0:1024]))
                    sVE.wait(sy)
                    sST.inc(sy.dma_start(out=self.W2B[ck], in_=wsb[:, 0:1024]))
                sST.wait(sy)
                sST.wait(ve)

            self.emit_ln_pass(src_dram, xlf, fstage, xlnb,
                              ps_t=ps_u.ap()[:, 0:512].bitcast(BF16))
            xlf_v = xlf.ap().rearrange("p (c t) -> p c t", c=NCH)

            with ExitStack() as g:
                gstg = self.sb(g, "gstg" + sfx, [128, NCH * 16], F32)
                sVE.wait(sy)
                sLD.inc(sy.dma_start(
                    out=gstg.ap().rearrange("p (c n) -> p c n", c=NCH),
                    in_=self.GWT.rearrange("(c p) n -> p c n", p=128)))
                sLD.wait(ve)
                sVE.inc(ve.tensor_copy(out=gw_b[:], in_=gstg[:]))
            gw_v = gw_b.ap().rearrange("p (c n) -> p c n", c=NCH)

            ps_rt = ps_g.ap()[:, 0:512]
            lg = rsc[:, 0:16]
            t8 = rsc[:, 16:24]
            neg = rsc[:, 24:25]
            thr = rsc[:, 25:26]
            s4 = rsc[:, 26:27]
            rcp = rsc[:, 27:28]
            es = rsc[:, 32:48]
            sct = rsc[:, 48:64]
            sn_rt = 0
            sn_st1 = sn_st2 = 0
            for tt in range(NT):
                te.wait_ge(sVE.h, self.sVE.n)
                for c in range(NCH):
                    ins = te.matmul(ps_rt[:, 0:16], xlf_v[:, c, tt * 128:(tt + 1) * 128],
                                    gw_v[:, c, :], start=(c == 0), stop=(c == NCH - 1))
                sPE.inc(ins)
                sPE.wait(ve)
                sVE.inc(ve.tensor_copy(out=lg, in_=ps_rt[:, 0:16]))
                sVE.wait(ve)
                sVE.inc(ve.max(out=t8, in_=lg))
                sVE.wait(ve)
                sVE.inc(ve.tensor_scalar_mul(neg, t8[:, 0:1], -1.0))
                sVE.wait(se)
                sAC.inc(se.activation(out=es, in_=lg, func=AF.Exp, bias=neg[:, 0:1],
                                      scale=1.0))
                sAC.wait(se)
                sAC.inc(se.activation(out=thr, in_=t8[:, 3:4], func=AF.Exp,
                                      bias=neg[:, 0:1], scale=1.0))
                sAC.wait(ve)
                sVE.inc(ve.tensor_scalar(out=sct, in0=es, scalar1=thr[:, 0:1],
                                         scalar2=None, op0=ALU.is_ge))
                sVE.wait(ve)
                sVE.inc(ve.tensor_tensor(out=es, in0=es, in1=sct, op=ALU.mult))
                sVE.wait(ve)
                sVE.inc(ve.tensor_reduce(out=s4, in_=es, op=ALU.add, axis=AX.X))
                sVE.wait(ve)
                sVE.inc(ve.reciprocal(out=rcp, in_=s4))
                sVE.wait(ve)
                sVE.inc(ve.tensor_scalar(out=sct, in0=es, scalar1=rcp[:, 0:1],
                                         scalar2=None, op0=ALU.mult))
                te.wait_ge(sVE.h, self.sVE.n)
                if tt >= 1:
                    te.wait_ge(sVE.h, sn_rt)
                if tt >= 2:
                    ve.wait_ge(sST.h, sn_st2)
                sPE.inc(te.transpose(ps_rt[0:16, 16:144], sct, self.ident_f[:]))
                sPE.wait(ve)
                sVE.inc(ve.tensor_copy(out=s_t[:], in_=ps_rt[0:16, 16:144]))
                sn_rt = sVE.n
                sVE.wait(sy)
                sST.inc(sy.dma_start(out=self.SFD[:, tt * 128:(tt + 1) * 128],
                                     in_=s_t[:]))
                sn_st2 = sn_st1
                sn_st1 = sST.n

            w1v = w1b.ap().rearrange("p b (c n) -> p b c n", c=NCH)
            w3v = w3b.ap().rearrange("p b (c n) -> p b c n", c=NCH)
            sn_w13 = {}
            sn_gu_pe = {}
            sn_act_ve = {}
            sn_silu_ac = {}
            sn_w2 = {}
            sn_w2_pe = {}
            sn_o_ac = 0
            sn_T_ve = 0
            sn_xst = []
            for gi0 in (0, 1):
                sLD.inc(sy.dma_start(out=w1v[:, gi0, :, :], in_=W1Bv[gi0 % NE]))
                sLD.inc(sy.dma_start(out=w3v[:, gi0, :, :], in_=W3Bv[gi0 % NE]))
                sn_w13[gi0] = sLD.n

            for tb in range(NTB):
                for e in range(NE):
                    gi = tb * NE + e
                    sy.wait_ge(sPE.h, self.sPE.n)
                    sLD.inc(sy.dma_start(out=s_row.ap()[:, gi % 2, :],
                                         in_=self.SFD[e:e + 1, tb * 512:(tb + 1) * 512]))
                    sn_srow = sLD.n
                    te.wait_ge(sAC.h, self.sAC.n)
                    te.wait_ge(sVE.h, self.sVE.n)
                    sLD.wait(te, at=sn_srow)
                    sPE.inc(te.matmul(ps_g.ap()[:, 0:512], self.ones_b[0:1, 0:128],
                                      s_row.ap()[:, gi % 2, :],
                                      start=True, stop=True))
                    sPE.wait(ve)
                    sVE.inc(ve.tensor_copy(out=sbc[:], in_=ps_g.ap()[:, 0:512]))
                    sn_sbc = sVE.n

                    sLD.wait(te, at=sn_w13[gi])
                    te.wait_ge(sVE.h, sn_sbc)
                    te.wait_ge(sVE.h, sn_act_ve.get(gi - 1, 0))
                    te.wait_ge(sAC.h, max(sn_silu_ac.get(gi - 1, 0), sn_o_ac))
                    for ic in range(4):
                        gbank = ps_g.ap()[:, ic * 512:(ic + 1) * 512]
                        ubank = ps_u.ap()[:, ic * 512:(ic + 1) * 512]
                        for c in range(NCH):
                            te.matmul(gbank, w1v[:, gi % 2, c, ic * 128:(ic + 1) * 128],
                                      xlf_v[:, c, tb * 512:(tb + 1) * 512],
                                      start=(c == 0), stop=(c == NCH - 1))
                        for c in range(NCH):
                            ins = te.matmul(ubank, w3v[:, gi % 2, c, ic * 128:(ic + 1) * 128],
                                            xlf_v[:, c, tb * 512:(tb + 1) * 512],
                                            start=(c == 0), stop=(c == NCH - 1))
                    sPE.inc(ins)
                    sn_gu_pe[gi] = sPE.n
                    ni = gi + 2
                    if ni < NTB * NE:
                        sy.wait_ge(sPE.h, sn_gu_pe[gi])
                        sLD.inc(sy.dma_start(out=w1v[:, ni % 2, :, :], in_=W1Bv[ni % NE]))
                        sLD.inc(sy.dma_start(out=w3v[:, ni % 2, :, :], in_=W3Bv[ni % NE]))
                        sn_w13[ni] = sLD.n
                    sPE.wait(se)
                    sAC.inc(se.activation(out=ofm[:, 0:2048], in_=ps_g.ap()[:, 0:2048],
                                          func=AF.Silu))
                    sn_silu_ac[gi] = sAC.n
                    sPE.wait(ve)
                    sVE.inc(ve.tensor_tensor(
                        out=ofm.ap()[:, 2048:4096].rearrange("p (i n) -> p i n", i=4),
                        in0=ps_u.ap().rearrange("p (i n) -> p i n", i=4),
                        in1=sbc.ap().unsqueeze(1).to_broadcast([128, 4, 512]),
                        op=ALU.mult))
                    sAC.wait(ve)
                    sVE.inc(ve.tensor_tensor(
                        out=act_all[:, e * 2048:(e + 1) * 2048],
                        in0=ofm[:, 0:2048], in1=ofm[:, 2048:4096], op=ALU.mult))
                    sn_act_ve[gi] = sVE.n

                if tb >= 1:
                    sy.wait_ge(sPE.h, self.sPE.n)
                sLD.inc(sy.dma_start(out=w2b[:, 0, :], in_=self.W2B[0]))
                sn_w2[0] = sLD.n
                for ck in range(NCK):
                    if ck + 1 < NCK:
                        if ck >= 1:
                            sy.wait_ge(sPE.h, sn_w2_pe.get(ck - 1, 0))
                        sLD.inc(sy.dma_start(out=w2b[:, (ck + 1) % 2, :],
                                             in_=self.W2B[ck + 1]))
                        sn_w2[ck + 1] = sLD.n
                    sLD.wait(te, at=sn_w2[ck])
                    if ck == 0:
                        te.wait_ge(sVE.h, sn_act_ve[tb * NE + NE - 1])
                        te.wait_ge(sAC.h, self.sAC.n)
                        te.wait_ge(sVE.h, self.sVE.n)
                    for oc in range(NCH):
                        ins = te.matmul(ps_o_banks[oc],
                                        w2b[:, ck % 2, oc * 128:(oc + 1) * 128],
                                        act_all[:, ck * 512:(ck + 1) * 512],
                                        start=(ck == 0), stop=(ck == NCK - 1))
                    sPE.inc(ins)
                    sn_w2_pe[ck] = sPE.n
                sPE.wait(se)
                for oc in range(NCH):
                    ins = se.activation(out=ofm[:, oc * 512:(oc + 1) * 512],
                                        in_=ps_o_banks[oc], func=AF.Identity)
                sAC.inc(ins)
                sn_o_ac = self.sAC.n
                for ts4 in range(4):
                    r0 = tb * 512 + ts4 * 128
                    sy.wait_ge(sVE.h, sn_T_ve)
                    sLD.inc(sy.dma_start(out=fstage[:, ts4 % 2, :],
                                         in_=src_dram[r0:r0 + 128, :]))
                    xrld = sLD.n
                    sAC.wait(te)
                    te.wait_ge(sVE.h, sn_T_ve)
                    for oc in range(NCH):
                        ins = te.transpose(ps_t[:, oc * 128:(oc + 1) * 128],
                                           ofm[:, oc * 512 + ts4 * 128:oc * 512 + (ts4 + 1) * 128],
                                           self.ident_b[:])
                    sPE.inc(ins)
                    sPE.wait(ve)
                    sLD.wait(ve, at=xrld)
                    if len(sn_xst) >= 2:
                        ve.wait_ge(sST.h, sn_xst[-2])
                    sVE.inc(ve.tensor_tensor(out=fstage[:, 2 + ts4 % 2, :],
                                             in0=ps_t, in1=fstage[:, ts4 % 2, :],
                                             op=ALU.add))
                    sn_T_ve = sVE.n
                    sVE.wait(sy)
                    sST.inc(sy.dma_start(out=dst_dram[r0:r0 + 128, :],
                                         in_=fstage.ap()[:, 2 + ts4 % 2, :]))
                    sn_xst.append(sST.n)
            sVE.wait(te)
            sAC.wait(te)
            sVE.wait(sy)
            sVE.wait(ge)
            self.tap("t_sfd", self.SFD)
            self.tap("t_act", act_all.ap())
            self.tap("t_xlf", xlf.ap())

    def phase_b(self, src_dram, dst_dram):
        te, ve, se, ge, sy = self.te, self.ve, self.se, self.ge, self.sy
        sLD, sST, sPE, sVE, sAC = self.sLD, self.sST, self.sPE, self.sVE, self.sAC

        with ExitStack() as b:
            xlf = self.sb(b, "xlf3", [128, NCH * T], BF16)
            B1 = self.sb(b, "B1", [128, T], F32)
            B2 = self.sb(b, "B2", [128, T], F32)
            B3 = self.sb(b, "B3", [128, T], F32)
            B4 = self.sb(b, "B4", [128, T], F32)
            fstage = self.sb(b, "fstage3", [128, 4, D], F32)
            xlnb = self.sb(b, "xlnb3", [128, D], BF16)
            wg_b = self.sb(b, "wg_b", [128, 2, 128], BF16)
            wgstg = self.sb(b, "wgstg", [128, 128], F32)
            bcol = self.sb(b, "bcol", [128, 2], F32)
            hstage = self.sb(b, "hstage", [128, 2, 512], F32)
            xq = self.sb(b, "xq", [128, 2, 512], F32)
            ps_gx = self.psum(b, "ps_gx", [128, 2048], F32)
            ps_ht = self.psum(b, "ps_ht", [128, 512], F32)

            self.emit_ln_pass(src_dram, xlf, fstage, xlnb,
                              ps_t=ps_gx.ap()[:, 0:512].bitcast(BF16))
            xlf_v = xlf.ap().rearrange("p (c t) -> p c t", c=NCH)

            for c in range(NCH):
                for i, (WSRC, BSRC) in enumerate(((self.IGW, self.IGB),
                                                  (self.AGW, self.AGB))):
                    sVE.wait(sy)
                    sLD.inc(sy.dma_start(out=wgstg[:], in_=WSRC[c]))
                    sLD.inc(sy.dma_start(out=bcol[:, i:i + 1],
                                         in_=BSRC[c:c + 1, :].rearrange("a b -> b a")))
                    sLD.wait(ve)
                    sVE.inc(ve.tensor_copy(out=wg_b[:, i, :], in_=wgstg[:]))
                sn_wg = sVE.n
                for i, dstbuf in ((0, B1), (1, B2)):
                    for tb in range(NTB):
                        bank = ps_gx.ap()[:, (tb % 2) * 512:(tb % 2 + 1) * 512]
                        te.wait_ge(sVE.h, sn_wg)
                        te.wait_ge(sAC.h, self.sAC.n)
                        sPE.inc(te.matmul(bank, wg_b[:, i, :],
                                          xlf_v[:, c, tb * 512:(tb + 1) * 512],
                                          start=True, stop=True))
                        sPE.wait(se)
                        sAC.inc(se.activation(out=dstbuf[:, tb * 512:(tb + 1) * 512],
                                              in_=bank, func=AF.Sigmoid,
                                              bias=bcol[:, i:i + 1], scale=1.0))
                sAC.wait(ve)
                sVE.inc(ve.tensor_scalar(out=B2[:], in0=B2[:],
                                         scalar1=self.sp8[:, c:c + 1], scalar2=None,
                                         op0=ALU.mult))
                sVE.inc(ve.tensor_tensor_scan(out=B3[:], data0=B2[:], data1=B2[:],
                                              initial=0.0, op0=ALU.add, op1=ALU.bypass))
                sVE.inc(ve.tensor_scalar(out=B3[:], in0=B3[:], scalar1=LCLIP,
                                         scalar2=None, op0=ALU.max))
                ve.memset(B4[:, 0:1], 0.0)
                ve.tensor_copy(out=B4[:, 1:2], in_=B3[:, 0:1])
                sVE.inc(ve.tensor_tensor(out=B4[:, 2:T], in0=B3[:, 1:T - 1],
                                         in1=B3[:, 0:T - 2], op=ALU.subtract))
                sVE.wait(se)
                sAC.inc(se.activation(out=B4[:], in_=B4[:], func=AF.Exp))
                sAC.inc(se.activation(out=B2[:], in_=B2[:], func=AF.Exp, scale=2.0))
                sAC.wait(ve)
                ve.tensor_scalar(out=B2[:], in0=B2[:], scalar1=-1.0, scalar2=1.0,
                                 op0=ALU.mult, op1=ALU.add)
                sVE.inc(ve.tensor_scalar(out=B2[:], in0=B2[:], scalar1=1e-7,
                                         scalar2=None, op0=ALU.max))
                sVE.wait(se)
                se.activation(out=B2[:], in_=B2[:], func=AF.Ln)
                sAC.inc(se.activation(out=B2[:], in_=B2[:], func=AF.Exp, scale=0.5))
                sAC.wait(ve)
                ve.tensor_tensor(out=B1[:], in0=xlf_v[:, c, :], in1=B1[:], op=ALU.mult)
                sVE.inc(ve.tensor_tensor(out=B1[:], in0=B1[:], in1=B2[:], op=ALU.mult))
                sVE.inc(ve.tensor_tensor_scan(out=B3[:], data0=B4[:], data1=B1[:],
                                              initial=0.0, op0=ALU.mult, op1=ALU.add))
                sn_hst = []
                sn_hT = 0
                xqld = {}
                for q4 in range(NCH):
                    r0 = q4 * 512
                    dcol = dst_dram[r0:r0 + 512, c * 128:(c + 1) * 128] \
                        .rearrange("(q p) d -> p q d", p=128)
                    if q4 >= 2:
                        sy.wait_ge(sVE.h, sn_hT)
                    sLD.inc(sy.dma_start(out=xq.ap()[:, q4 % 2, :]
                                         .rearrange("p (q d) -> p q d", q=4),
                                         in_=dcol))
                    xqld[q4] = sLD.n
                    if q4 >= 1:
                        te.wait_ge(sVE.h, sn_hT)
                    sVE.wait(te)
                    for j in range(4):
                        tt = q4 * 4 + j
                        ins = te.transpose(ps_ht[:, j * 128:(j + 1) * 128],
                                           B3[:, tt * 128:(tt + 1) * 128],
                                           self.ident_f[:])
                    sPE.inc(ins)
                    sPE.wait(ve)
                    sLD.wait(ve, at=xqld[q4])
                    if len(sn_hst) >= 2:
                        ve.wait_ge(sST.h, sn_hst[-2])
                    sVE.inc(ve.tensor_tensor(out=hstage.ap()[:, q4 % 2, :],
                                             in0=ps_ht.ap(),
                                             in1=xq.ap()[:, q4 % 2, :], op=ALU.add))
                    sn_hT = sVE.n
                    sVE.wait(sy)
                    sST.inc(sy.dma_start(
                        out=dcol,
                        in_=hstage.ap()[:, q4 % 2, :].rearrange("p (q d) -> p q d", q=4)))
                    sn_hst.append(sST.n)
                sST.wait(sy)
            sVE.wait(te)
            sAC.wait(te)
            sVE.wait(sy)
            self.tap("t_B1", B1.ap())
            self.tap("t_B2", B2.ap())
            self.tap("t_B3", B3.ap())
            self.tap("t_B4", B4.ap())
            self.tap("t_xlf3", xlf.ap())

    def finish(self, copy_xr_to_out=False):
        ge, sy, ve = self.ge, self.sy, self.ve
        if copy_xr_to_out:
            self.sST.wait(sy)
            self.sST.inc(sy.dma_start(out=self.OUT, in_=self.XR))
        self.sST.wait(sy)
        self.sLD.wait(sy)
        self.sST.wait(ge)

    def build(self):
        self.prelude()
        if "A" in PHASES:
            self.phase_a()
        first = True
        if "F1" in PHASES:
            self.phase_ffn(self.XR, self.XR, first_call=first)
            first = False
        if "B" in PHASES:
            self.phase_b(self.XR, self.XR)
        if "F2" in PHASES:
            last = ("A" in PHASES and "F1" in PHASES and "B" in PHASES)
            self.phase_ffn(self.XR, self.OUT if last else self.XR,
                           first_call=first)
            first = False
            done_full = last
        else:
            done_full = False
        self.finish(copy_xr_to_out=not done_full)
        self.ctx.close()
        return self.nc


def _shard_inputs(inputs):
    f = np.ascontiguousarray
    x = inputs["L0_attn_a_x"]
    wq = np.asarray(inputs["L0_attn_a_wq"])
    wk = np.asarray(inputs["L0_attn_a_wk"])
    common = {
        "wqkvT": f(np.asarray(inputs["L0_attn_a_Wqkv"]).T),
        "bqkv": f(np.asarray(inputs["L0_attn_a_bqkv"]).reshape(1, 3 * D)),
        "woT": f(np.asarray(inputs["L0_attn_a_Wo"]).T),
        "bo": f(np.asarray(inputs["L0_attn_a_bo"]).reshape(1, D)),
        "wqkrow": f(np.concatenate([np.tile(wq, NH), np.tile(wk, NH)]).reshape(1, 2 * D)),
        "igw": f(np.asarray(inputs["L0_attn_b_ig_w"])),
        "igb": f(np.asarray(inputs["L0_attn_b_ig_b"])),
        "agw": f(np.asarray(inputs["L0_attn_b_ag_w"])),
        "agb": f(np.asarray(inputs["L0_attn_b_ag_b"])),
        "apar": f(np.asarray(inputs["L0_attn_b_a_param"]).reshape(NCH, 128)),
        "gwT": f(np.asarray(inputs["L0_ffn_gate_w"]).T),
        "w1": f(np.asarray(inputs["L0_ffn_w1"]).reshape(NE, NCH, 128, DI)),
        "w3": f(np.asarray(inputs["L0_ffn_w3"]).reshape(NE, NCH, 128, DI)),
        "w2": f(np.asarray(inputs["L0_ffn_w2"]).reshape(NE * 4, 128, D)),
    }
    in_maps = []
    for i in range(8):
        m = dict(common)
        m["x"] = f(np.asarray(x[i]))
        in_maps.append(m)
    return in_maps


def kernel(**inputs):
    in_maps = _shard_inputs(inputs)
    nc = Builder().build()
    res = run_bass_kernel_spmd(nc, in_maps, core_ids=list(range(8)))
    out = np.stack([np.asarray(r["out"]) for r in res.results], axis=0)
    return out.astype(np.float32)


# version 02
# speedup vs baseline: 1.0301x; 1.0301x over previous
import numpy as np
import concourse.bass as bass
import concourse.mybir as mybir
from concourse.bass_utils import run_bass_kernel_spmd
from contextlib import ExitStack

F32 = mybir.dt.float32
BF16 = mybir.dt.bfloat16
AF = mybir.ActivationFunctionType
ALU = mybir.AluOpType
AX = mybir.AxisListType

T = 4096
D = 1024
NT = 32
NH = 16
DH = 64
NP = 8
NE = 16
DI = 512
NTB = 8
NCH = 8
LCLIP = float(np.log(np.float32(1e-7)))

PHASES = ("A", "F1", "B", "F2")
A2_QBS = None
DEV_TAPS = {}


class Sem:
    def __init__(self, ctx, nc, name, step=1):
        self.h = ctx.enter_context(nc.semaphore(name))
        self.n = 0
        self.step = step

    def inc(self, ins, k=1):
        ins.then_inc(self.h, k * self.step)
        self.n += k * self.step
        return self.n

    def wait(self, eng, at=None):
        eng.wait_ge(self.h, self.n if at is None else at)


class Builder:
    def __init__(self):
        nc = bass.Bass("TRN2", target_bir_lowering=False, dynamic_dma_scratch_size=8192)
        self.nc = nc
        self.te, self.ve, self.se, self.ge, self.sy = (
            nc.tensor, nc.vector, nc.scalar, nc.gpsimd, nc.sync)
        self.ctx = ExitStack()
        self.sLD = Sem(self.ctx, nc, "sLD", 16)
        self.sST = Sem(self.ctx, nc, "sST", 16)
        self.sPE = Sem(self.ctx, nc, "sPE")
        self.sVE = Sem(self.ctx, nc, "sVE")
        self.sAC = Sem(self.ctx, nc, "sAC")
        self.sGE = Sem(self.ctx, nc, "sGE")
        self._dram()
        self.taps = {}
        for name, (shape, dt) in DEV_TAPS.items():
            self.taps[name] = nc.dram_tensor(name, list(shape), dt,
                                             kind="ExternalOutput").ap()

    def sb(self, ctx, name, shape, dtype):
        return ctx.enter_context(self.nc.sbuf_tensor(name, shape, dtype))

    def psum(self, ctx, name, shape, dtype):
        return ctx.enter_context(self.nc.psum_tensor(name, shape, dtype))

    def _dram(self):
        nc = self.nc
        self.X = nc.dram_tensor("x", [T, D], F32, kind="ExternalInput").ap()
        self.WQKVT = nc.dram_tensor("wqkvT", [D, 3 * D], F32, kind="ExternalInput").ap()
        self.BQKV = nc.dram_tensor("bqkv", [1, 3 * D], F32, kind="ExternalInput").ap()
        self.WOT = nc.dram_tensor("woT", [D, D], F32, kind="ExternalInput").ap()
        self.BO = nc.dram_tensor("bo", [1, D], F32, kind="ExternalInput").ap()
        self.WQKROW = nc.dram_tensor("wqkrow", [1, 2 * D], F32, kind="ExternalInput").ap()
        self.IGW = nc.dram_tensor("igw", [NCH, 128, 128], F32, kind="ExternalInput").ap()
        self.IGB = nc.dram_tensor("igb", [NCH, 128], F32, kind="ExternalInput").ap()
        self.AGW = nc.dram_tensor("agw", [NCH, 128, 128], F32, kind="ExternalInput").ap()
        self.AGB = nc.dram_tensor("agb", [NCH, 128], F32, kind="ExternalInput").ap()
        self.APAR = nc.dram_tensor("apar", [NCH, 128], F32, kind="ExternalInput").ap()
        self.GWT = nc.dram_tensor("gwT", [D, NE], F32, kind="ExternalInput").ap()
        self.W1 = nc.dram_tensor("w1", [NE, NCH, 128, DI], F32, kind="ExternalInput").ap()
        self.W3 = nc.dram_tensor("w3", [NE, NCH, 128, DI], F32, kind="ExternalInput").ap()
        self.W2 = nc.dram_tensor("w2", [NE * 4, 128, D], F32, kind="ExternalInput").ap()
        self.OUT = nc.dram_tensor("out", [T, D], F32, kind="ExternalOutput").ap()
        self.XR = nc.dram_tensor("xres", [T, D], F32, kind="Internal").ap()
        self.QS = nc.dram_tensor("qsc", [NT, 128, D], BF16, kind="Internal").ap()
        self.ZS = nc.dram_tensor("zsc", [NP, 128, T], BF16, kind="Internal").ap()
        self.W1B = nc.dram_tensor("w1bd", [NE, NCH, 128, DI], BF16, kind="Internal").ap()
        self.W3B = nc.dram_tensor("w3bd", [NE, NCH, 128, DI], BF16, kind="Internal").ap()
        self.W2B = nc.dram_tensor("w2bd", [NE * 4, 128, D], BF16, kind="Internal").ap()
        self.MSC = nc.dram_tensor("msc", [8, 32], F32, kind="Internal").ap()
        self.SFD = nc.dram_tensor("sfd", [NE, T], BF16, kind="Internal").ap()

    def tap(self, name, src_ap):
        if name in self.taps:
            self.sVE.wait(self.sy)
            self.sAC.wait(self.sy)
            self.sPE.wait(self.sy)
            self.sST.inc(self.sy.dma_start(out=self.taps[name], in_=src_ap))

    def prelude(self):
        ctx, nc = self.ctx, self.nc
        te, ve, se, ge, sy = self.te, self.ve, self.se, self.ge, self.sy
        sLD, sVE, sAC, sPE, sGE = self.sLD, self.sVE, self.sAC, self.sPE, self.sGE

        self.ident_b = self.sb(ctx, "ident_b", [128, 128], BF16)
        self.ident_f = self.sb(ctx, "ident_f", [128, 128], F32)
        self.ones_b = self.sb(ctx, "ones_b", [1, 512], BF16)
        self.ones_f = self.sb(ctx, "ones_f", [1, 128], F32)
        self.c_eps_ln = self.sb(ctx, "c_eps_ln", [128, 1], F32)
        self.c_eps_rms = self.sb(ctx, "c_eps_rms", [128, 1], F32)
        self.c_one = self.sb(ctx, "c_one", [128, 1], F32)
        self.sp8 = self.sb(ctx, "sp8", [128, NCH], F32)
        self.lnst = self.sb(ctx, "lnst", [128, 12], F32)
        self.lnmv = self.sb(ctx, "lnmv", [128, 2], F32)
        self.lnr = self.sb(ctx, "lnr", [128, 2], F32)

        with ExitStack() as pctx:
            iscr = self.sb(pctx, "iscr", [128, 128], F32)
            stg1 = self.sb(pctx, "stg1", [1, 3 * D], F32)
            stg2 = self.sb(pctx, "stg2", [128, NCH], F32)
            sp_t = self.sb(pctx, "sp_t", [128, NCH], F32)
            ps_pre = self.psum(pctx, "ps_pre", [128, 512], F32)

            sGE.inc(ge.iota(out=iscr[:], pattern=[[-1, 128]], base=0,
                            channel_multiplier=1,
                            allow_small_or_imprecise_dtypes=True))
            ve.memset(self.ones_b[:], 1.0)
            ve.memset(self.ones_f[:], 1.0)
            ve.memset(self.c_eps_ln[:], 1e-5)
            ve.memset(self.c_eps_rms[:], 1e-6)
            sVE.inc(ve.memset(self.c_one[:], 1.0))
            sGE.wait(ve)
            ve.tensor_scalar(out=self.ident_b[:], in0=iscr[:], scalar1=0.0,
                             scalar2=None, op0=ALU.is_equal)
            sVE.inc(ve.tensor_scalar(out=self.ident_f[:], in0=iscr[:],
                                     scalar1=0.0, scalar2=None, op0=ALU.is_equal))

            for c in range(NCH):
                sLD.inc(sy.dma_start(
                    out=stg2[:, c:c + 1],
                    in_=self.APAR[c:c + 1, :].rearrange("a b -> b a")))
            sLD.wait(se)
            sAC.inc(se.activation(out=sp_t[:], in_=stg2[:], func=AF.Abs))
            sAC.wait(se)
            sAC.inc(se.activation(out=sp_t[:], in_=sp_t[:], func=AF.Exp, scale=-1.0))
            sAC.wait(se)
            sVE.wait(se)
            sAC.inc(se.activation(out=sp_t[:], in_=sp_t[:], func=AF.Ln,
                                  bias=self.c_one[:, 0:1], scale=1.0))
            sLD.wait(ve)
            sAC.wait(ve)
            sVE.inc(ve.tensor_scalar(out=self.sp8[:], in0=stg2[:], scalar1=0.0,
                             scalar2=None, op0=ALU.max))
            sVE.wait(ve)
            sVE.inc(ve.tensor_tensor(out=self.sp8[:], in0=self.sp8[:], in1=sp_t[:], op=ALU.add))
            sVE.wait(ve)
            sVE.inc(ve.tensor_scalar(out=self.sp8[:], in0=self.sp8[:],
                                     scalar1=-8.0, scalar2=None, op0=ALU.mult))
            sVE.wait(te)
            sVE.wait(ge)
            sVE.wait(sy)

    def emit_ln(self, xbuf, out_bf, eps_ap):
        ve, se = self.ve, self.se
        sVE, sAC = self.sVE, self.sAC
        ve.bn_stats(out=self.lnst[:, 0:6], in_=xbuf[:, 0:512])
        sVE.inc(ve.bn_stats(out=self.lnst[:, 6:12], in_=xbuf[:, 512:1024]))
        sVE.wait(ve)
        sVE.inc(ve.bn_aggr(out=self.lnmv[:], in_=self.lnst[:]))
        sVE.wait(ve)
        sVE.wait(se)
        sAC.inc(se.activation(out=self.lnr[:, 0:1], in_=self.lnmv[:, 1:2], func=AF.Ln,
                      bias=eps_ap, scale=1.0))
        sAC.wait(se)
        sAC.inc(se.activation(out=self.lnr[:, 0:1], in_=self.lnr[:, 0:1],
                              func=AF.Exp, scale=-0.5))
        sAC.wait(ve)
        sVE.inc(ve.scalar_tensor_tensor(out=self.lnr[:, 1:2], in0=self.lnmv[:, 0:1],
                                        scalar=-1.0, in1=self.lnr[:, 0:1],
                                        op0=ALU.mult, op1=ALU.mult))
        sVE.wait(ve)
        sVE.inc(ve.tensor_scalar(out=out_bf, in0=xbuf,
                                 scalar1=self.lnr[:, 0:1], scalar2=self.lnr[:, 1:2],
                                 op0=ALU.mult, op1=ALU.add))

    def phase_a(self):
        ctx = self.ctx
        nc = self.nc
        te, ve, se, ge, sy = self.te, self.ve, self.se, self.ge, self.sy
        sLD, sST, sPE, sVE, sAC = self.sLD, self.sST, self.sPE, self.sVE, self.sAC

        with ExitStack() as actx:
            self.bo_b = self.sb(actx, "bo_b", [1, D], BF16)
            kvctx = ExitStack()
            negm_mat = self.sb(kvctx, "negm_mat", [128, 128], F32)
            K_one = self.sb(kvctx, "K_one", [128, NP * T], BF16)
            V_aug = self.sb(kvctx, "V_aug", [128, NT * NH * 65], BF16)
            a1outer = ExitStack()
            self.wqkb = self.sb(a1outer, "wqkb", [128, 2 * D], BF16)
            self.bqkv_b = self.sb(a1outer, "bqkv_b", [1, 3 * D], BF16)
            rctx = ExitStack()
            r_fm = self.sb(rctx, "r_fm", [32, T], BF16)
            with ExitStack() as wctx:
                stg1 = self.sb(wctx, "stg1a", [1, 3 * D], F32)
                ps_pre = self.psum(wctx, "ps_prea", [128, 512], F32)
                sLD.inc(sy.dma_start(out=stg1[0:1, 0:2 * D], in_=self.WQKROW))
                sLD.wait(te)
                self.sVE.wait(te)
                for nb in range(4):
                    if nb:
                        self.sVE.wait(te)
                    sPE.inc(te.matmul(ps_pre[:], self.ones_f[0:1, :],
                                      stg1[0:1, nb * 512:(nb + 1) * 512],
                                      start=True, stop=True))
                    sPE.wait(ve)
                    sVE.inc(ve.tensor_copy(out=self.wqkb[:, nb * 512:(nb + 1) * 512],
                                           in_=ps_pre[:]))
                sVE.wait(sy)
                sLD.inc(sy.dma_start(out=stg1[0:1, :], in_=self.BQKV))
                sLD.wait(ve)
                sVE.inc(ve.tensor_copy(out=self.bqkv_b[:], in_=stg1[0:1, :]))
                sVE.wait(sy)
                sLD.inc(sy.dma_start(out=stg1[0:1, 0:D], in_=self.BO))
                sLD.wait(ve)
                sVE.inc(ve.tensor_copy(out=self.bo_b[:], in_=stg1[0:1, 0:D]))
                sVE.wait(sy)
                sVE.wait(te)

            with ExitStack() as a1:
                xls = self.sb(a1, "xls", [128, 2, D], F32)
                xlnb = self.sb(a1, "xlnb", [128, D], BF16)
                xlf_t = self.sb(a1, "xlf_t", [128, D], BF16)
                wst = self.sb(a1, "wst", [128, 2048], F32)
                q2sb = wst
                s2h = self.sb(a1, "s2h", [128, 32], F32)
                rstdh = self.sb(a1, "rstdh", [128, 32], F32)
                qn1 = self.sb(a1, "qn1", [128, 512], F32)
                qkn = self.sb(a1, "qkn", [128, 2048], BF16)
                rtm = self.sb(a1, "rtm", [128, 32], F32)
                qstg = self.sb(a1, "qstg", [128, 2, D], BF16)
                ps_t = self.psum(a1, "ps_t", [128, 1024], BF16)
                ps_qk = self.psum(a1, "ps_qk", [128, 2048], F32)
                ps_qt = self.psum(a1, "ps_qt", [128, 1024], BF16)
                ps_kt = self.psum(a1, "ps_kt", [128, 1024], BF16)
                ps_r = self.psum(a1, "ps_r", [32, 128], F32)

                with ExitStack() as a1a:
                    wqk_b = self.sb(a1a, "wqk_b", [128, NCH * 2048], BF16)
                    wqk_bv = wqk_b.ap().rearrange("p (c n) -> p c n", c=NCH)
                    cast_done = []
                    for c in range(NCH):
                        if c >= 1:
                            sy.wait_ge(sVE.h, cast_done[c - 1])
                        sLD.inc(sy.dma_start(out=wst[:],
                                             in_=self.WQKVT[c * 128:(c + 1) * 128, 0:2048]))
                        sLD.wait(ve)
                        sVE.inc(ve.tensor_copy(out=wqk_bv[:, c, :], in_=wst[:]))
                        cast_done.append(sVE.n)

                    xld = []
                    sLD.inc(sy.dma_start(out=xls[:, 0, :], in_=self.X[0:128, :]))
                    xld.append(sLD.n)
                    sLD.inc(sy.dma_start(out=xls[:, 1, :], in_=self.X[128:256, :]))
                    xld.append(sLD.n)

                    sn_x = [0] * NT
                    sn_t = [0] * NT
                    sn_qk_ve = [0] * NT
                    sn_qk_ac = [0] * NT
                    sn_r = [0] * NT
                    sn_qt = [0] * NT
                    sn_qst = [0] * NT
                    sn_xtr_pe = [0] * NT
                    sn_qtr_pe = [0] * NT

                    for tt in range(NT):
                        xbuf = xls[:, tt % 2, :]
                        sLD.wait(ve, at=xld[tt])
                        if tt >= 1:
                            ve.wait_ge(sPE.h, sn_xtr_pe[tt - 1])
                        self.emit_ln(xbuf, xlnb[:], self.c_eps_ln[:, 0:1])
                        sn_x[tt] = self.sVE.n
                        if tt + 2 < NT:
                            sy.wait_ge(sVE.h, sn_x[tt])
                            sLD.inc(sy.dma_start(out=xls[:, tt % 2, :],
                                                 in_=self.X[(tt + 2) * 128:(tt + 3) * 128, :]))
                            xld.append(sLD.n)
                        if tt >= 1:
                            te.wait_ge(sVE.h, sn_t[tt - 1])
                        sVE.wait(te)
                        for c in range(NCH):
                            ins = te.transpose(ps_t[:, c * 128:(c + 1) * 128],
                                               xlnb[:, c * 128:(c + 1) * 128],
                                               self.ident_b[:])
                        sPE.inc(ins)
                        sn_xtr_pe[tt] = sPE.n
                        sPE.wait(ve)
                        sVE.inc(ve.tensor_copy(out=xlf_t[:], in_=ps_t[:]))
                        sn_t[tt] = sVE.n
                        xlf_ready = sVE.n

                        for nb in range(4):
                            te.wait_ge(sVE.h, xlf_ready)
                            if tt >= 1:
                                te.wait_ge(sVE.h, sn_qk_ve[tt - 1])
                                te.wait_ge(sAC.h, sn_qk_ac[tt - 1])
                            if nb == 0 and tt >= 1:
                                ve.wait_ge(sPE.h, sn_qtr_pe[tt - 1])
                            bank = ps_qk[:, nb * 512:(nb + 1) * 512]
                            for c in range(NCH):
                                te.matmul(bank, xlf_t[:, c * 128:(c + 1) * 128],
                                          wqk_bv[:, c, nb * 512:(nb + 1) * 512],
                                          start=(c == 0), stop=False)
                            sPE.inc(te.matmul(bank, self.ones_b[0:1, 0:128],
                                              self.bqkv_b[0:1, nb * 512:(nb + 1) * 512],
                                              start=False, stop=True))
                            sPE.wait(se)
                            sAC.inc(se.activation(out=q2sb[:, nb * 512:(nb + 1) * 512],
                                                  in_=bank, func=AF.Square))
                            sAC.wait(ve)
                            sVE.inc(ve.tensor_reduce(
                                out=s2h[:, nb * 8:(nb + 1) * 8],
                                in_=q2sb.ap().rearrange("p (h d) -> p h d", d=DH)[:, nb * 8:(nb + 1) * 8, :],
                                op=ALU.add, axis=AX.X))
                            sVE.wait(se)
                            sAC.inc(se.activation(out=rstdh[:, nb * 8:(nb + 1) * 8],
                                          in_=s2h[:, nb * 8:(nb + 1) * 8], func=AF.Ln,
                                          bias=self.c_eps_rms[:, 0:1], scale=1.0 / DH))
                            sAC.wait(se)
                            sAC.inc(se.activation(out=rstdh[:, nb * 8:(nb + 1) * 8],
                                                  in_=rstdh[:, nb * 8:(nb + 1) * 8],
                                                  func=AF.Exp, scale=-0.5))
                            sAC.wait(ve)
                            ve.tensor_tensor(
                                out=qn1.ap().rearrange("p (h d) -> p h d", d=DH),
                                in0=ps_qk.ap().rearrange("p (h d) -> p h d", d=DH)[:, nb * 8:(nb + 1) * 8, :],
                                in1=rstdh[:, nb * 8:(nb + 1) * 8].unsqueeze(2).to_broadcast([128, 8, DH]),
                                op=ALU.mult)
                            sVE.inc(ve.tensor_tensor(
                                out=qkn[:, nb * 512:(nb + 1) * 512], in0=qn1[:],
                                in1=self.wqkb[:, nb * 512:(nb + 1) * 512], op=ALU.mult))
                        sn_qk_ve[tt] = sVE.n
                        sn_qk_ac[tt] = sAC.n

                        sVE.wait(se)
                        sAC.inc(se.activation(out=q2sb[:], in_=qkn[:], func=AF.Square))
                        sAC.wait(ve)
                        sVE.inc(ve.tensor_reduce(
                            out=rtm[:],
                            in_=q2sb.ap().rearrange("p (h d) -> p h d", d=DH),
                            op=ALU.add, axis=AX.X))
                        sVE.wait(se)
                        sAC.inc(se.activation(out=rtm[:], in_=rtm[:], func=AF.Ln,
                                      bias=self.c_eps_rms[:, 0:1], scale=1.0))
                        sAC.wait(se)
                        sAC.inc(se.activation(out=rtm[:], in_=rtm[:], func=AF.Exp,
                                              scale=0.5))
                        if tt >= 1:
                            te.wait_ge(sVE.h, sn_r[tt - 1])
                        sAC.wait(te)
                        sPE.inc(te.transpose(ps_r[:], rtm[:, 0:32], self.ident_f[:]))
                        sPE.wait(ve)
                        sVE.inc(ve.tensor_copy(out=r_fm[:, tt * 128:(tt + 1) * 128],
                                               in_=ps_r[:]))
                        sn_r[tt] = sVE.n

                        if tt >= 1:
                            te.wait_ge(sVE.h, sn_qt[tt - 1])
                        for p in range(NP):
                            te.transpose(ps_qt[:, p * 128:(p + 1) * 128],
                                         qkn[:, p * 128:(p + 1) * 128], self.ident_b[:])
                            ins = te.transpose(ps_kt[:, p * 128:(p + 1) * 128],
                                               qkn[:, 1024 + p * 128:1024 + (p + 1) * 128],
                                               self.ident_b[:])
                        sPE.inc(ins)
                        sn_qtr_pe[tt] = sPE.n
                        sPE.wait(ve)
                        if tt >= 2:
                            ve.wait_ge(sST.h, sn_qst[tt - 2])
                        ve.tensor_copy(out=qstg[:, tt % 2, :], in_=ps_qt[:])
                        sVE.inc(ve.tensor_copy(
                            out=K_one.ap().rearrange("p (c t) -> p c t", c=NP)[:, :, tt * 128:(tt + 1) * 128],
                            in_=ps_kt.ap().rearrange("p (c t) -> p c t", c=NP)))
                        sn_qt[tt] = sVE.n
                        sVE.wait(sy)
                        sST.inc(sy.dma_start(out=self.QS[tt, :, :],
                                             in_=qstg.ap()[:, tt % 2, :]))
                        sn_qst[tt] = sST.n
                    sVE.wait(sy)
                    sVE.wait(te)
                    sAC.wait(te)

                with ExitStack() as a1b:
                    wv_b = self.sb(a1b, "wv_b", [128, NCH * 1024], BF16)
                    wv_bv = wv_b.ap().rearrange("p (c n) -> p c n", c=NCH)
                    ps_v = ps_qk
                    cast_done = []
                    for c in range(NCH):
                        if c >= 1:
                            sy.wait_ge(sVE.h, cast_done[c - 1])
                        sLD.inc(sy.dma_start(out=wst[:, 0:1024],
                                             in_=self.WQKVT[c * 128:(c + 1) * 128, 2048:3072]))
                        sLD.wait(ve)
                        sVE.inc(ve.tensor_copy(out=wv_bv[:, c, :], in_=wst[:, 0:1024]))
                        cast_done.append(sVE.n)
                    sVE.inc(ve.memset(V_aug[:], 1.0))
                    vones_ready = sVE.n

                    xld = []
                    sLD.inc(sy.dma_start(out=xls[:, 0, :], in_=self.X[0:128, :]))
                    xld.append(sLD.n)
                    sLD.inc(sy.dma_start(out=xls[:, 1, :], in_=self.X[128:256, :]))
                    xld.append(sLD.n)
                    sn_x = [0] * NT
                    sn_t = [0] * NT
                    sn_v_ac = [0] * NT
                    for tt in range(NT):
                        xbuf = xls[:, tt % 2, :]
                        sLD.wait(ve, at=xld[tt])
                        self.emit_ln(xbuf, xlnb[:], self.c_eps_ln[:, 0:1])
                        sn_x[tt] = sVE.n
                        if tt + 2 < NT:
                            sy.wait_ge(sVE.h, sn_x[tt])
                            sLD.inc(sy.dma_start(out=xls[:, tt % 2, :],
                                                 in_=self.X[(tt + 2) * 128:(tt + 3) * 128, :]))
                            xld.append(sLD.n)
                        if tt >= 1:
                            te.wait_ge(sVE.h, sn_t[tt - 1])
                        sVE.wait(te)
                        for c in range(NCH):
                            ins = te.transpose(ps_t[:, c * 128:(c + 1) * 128],
                                               xlnb[:, c * 128:(c + 1) * 128],
                                               self.ident_b[:])
                        sPE.inc(ins)
                        sPE.wait(ve)
                        sVE.inc(ve.tensor_copy(out=xlf_t[:], in_=ps_t[:]))
                        sn_t[tt] = sVE.n
                        xlf_ready = sVE.n
                        for nb in range(2):
                            te.wait_ge(sVE.h, xlf_ready)
                            if tt == 0:
                                te.wait_ge(sVE.h, vones_ready)
                            if tt >= 1:
                                te.wait_ge(sVE.h, sn_v_ac[tt - 1])
                            bank = ps_v[:, nb * 512:(nb + 1) * 512]
                            for c in range(NCH):
                                te.matmul(bank, xlf_t[:, c * 128:(c + 1) * 128],
                                          wv_bv[:, c, nb * 512:(nb + 1) * 512],
                                          start=(c == 0), stop=False)
                            sPE.inc(te.matmul(bank, self.ones_b[0:1, 0:128],
                                              self.bqkv_b[0:1, 2048 + nb * 512:2048 + (nb + 1) * 512],
                                              start=False, stop=True))
                            sPE.wait(ve)
                            sVE.inc(ve.tensor_copy(
                                out=V_aug.ap().rearrange("p (t h e) -> p t h e", t=NT, h=NH)[:, tt, nb * 8:(nb + 1) * 8, 0:64],
                                in_=bank.rearrange("p (h e) -> p h e", e=DH)))
                        sn_v_ac[tt] = sVE.n
                    sVE.wait(sy)
                    sVE.wait(te)
                    sAC.wait(te)

            if self.taps:
                self.tap("t_kone", K_one.ap())
                self.tap("t_vaug", V_aug.ap())
                self.tap("t_rfm", r_fm.ap())
                self.tap("t_qs", self.QS)

            with ExitStack() as nmctx:
                rk_col = self.sb(nmctx, "rk_col", [32, 1], F32)
                rqb32 = self.sb(nmctx, "rqb32", [32, 8], F32)
                rqbT = self.sb(nmctx, "rqbT", [8, 32], F32)
                rkT = self.sb(nmctx, "rkT", [1, 32], F32)
                rq_row = self.sb(nmctx, "rq_row", [1, 256], F32)
                negm_row = self.sb(nmctx, "negm_row", [1, 128], F32)
                ps_nm = self.psum(nmctx, "ps_nm", [128, 512], F32)
                sVE.inc(ve.tensor_reduce(out=rk_col[:], in_=r_fm.ap(),
                                         op=ALU.max, axis=AX.X))
                sVE.inc(ve.tensor_reduce(
                    out=rqb32[:],
                    in_=r_fm.ap().rearrange("h (q t) -> h q t", q=8),
                    op=ALU.max, axis=AX.X))
                sVE.wait(te)
                te.transpose(ps_nm[0:8, 0:32], rqb32[:], self.ident_f[0:32, 0:32])
                sPE.inc(te.transpose(ps_nm[0:1, 32:64], rk_col[:],
                                     self.ident_f[0:32, 0:32]))
                sPE.wait(ve)
                ve.tensor_copy(out=rqbT[:], in_=ps_nm[0:8, 0:32])
                sVE.inc(ve.tensor_copy(out=rkT[:], in_=ps_nm[0:1, 32:64]))
                sVE.wait(sy)
                sST.inc(sy.dma_start(out=self.MSC, in_=rqbT[:]))
                sST.wait(sy)
                sLD.inc(sy.dma_start(out=rq_row[:],
                                     in_=self.MSC.rearrange("a b -> (a b)").unsqueeze(0)))
                sLD.wait(ve)
                sVE.inc(ve.scalar_tensor_tensor(
                    out=negm_row.ap().rearrange("a (q h) -> a q h", h=16),
                    in0=rq_row.ap().rearrange("a (q h) -> a q h", h=32)[:, :, 0:16],
                    scalar=-0.5,
                    in1=rkT.ap()[0:1, 16:32].unsqueeze(1).to_broadcast([1, 8, 16]),
                    op0=ALU.mult, op1=ALU.mult))
                sVE.wait(te)
                sPE.inc(te.matmul(ps_nm[:, 0:128], self.ones_f[0:1, :],
                                  negm_row[:], start=True, stop=True))
                sPE.wait(ve)
                sVE.inc(ve.tensor_copy(out=negm_mat[:], in_=ps_nm[:, 0:128]))
                self.tap("t_negm", negm_mat.ap())
                sVE.wait(te)
                sVE.wait(sy)
            rctx.close()
            a1outer.close()

            with ExitStack() as a2:
                P = self.sb(a2, "P", [128, 2 * NT * 512], BF16)
                Qb = self.sb(a2, "Qb", [128, 2, 512], BF16)
                rl = self.sb(a2, "rl", [1, 512], F32)
                bc_sb = self.sb(a2, "bc_sb", [64, 512], F32)
                zstg = self.sb(a2, "zstg", [128, 2, 512], BF16)
                ps_sA = self.psum(a2, "ps_sA", [128, 1536], F32)
                ps_sB = self.psum(a2, "ps_sB", [128, 1536], F32)
                ps_z = self.psum(a2, "ps_z", [128, 512], F32)
                ps_m = self.psum(a2, "ps_m", [128, 512], F32)

                Vv = V_aug.ap().rearrange("p (t h e) -> p t h e", t=NT, h=NH)
                Kv = K_one.ap().rearrange("p (c t) -> p c t", c=NP)

                groups = [(g * 3, min(3, NT - g * 3)) for g in range((NT + 2) // 3)]
                sn_grp = {0: 0, 1: 0}
                sn_z_ve = 0
                sn_bc_ac = 0
                sn_exp_done = {}
                sn_S_pe = {}
                sn_q = {}
                sn_zst = {}
                qbs = range(NTB) if A2_QBS is None else A2_QBS
                heads = [(qb, pc, hh) for qb in qbs for pc in range(NP)
                         for hh in range(2)]
                Pv = P.ap().rearrange("p (b n) -> p b n", b=2)

                def emit_S(i):
                    nonlocal sn_z_ve
                    qb, pc, hh = heads[i]
                    pair = i // 2
                    base = 64 * hh
                    if hh == 0:
                        if pair >= 2:
                            sy.wait_ge(sPE.h, sn_S_pe.get(pair - 2, 0))
                        sLD.inc(sy.dma_start(
                            out=Qb.ap()[:, pc % 2, :].rearrange("p (t d) -> p t d", t=4),
                            in_=self.QS.rearrange("t p d -> p t d")[:, qb * 4:(qb + 1) * 4, pc * 128:(pc + 1) * 128]))
                        sn_q[pair] = sLD.n
                    bias_ap = negm_mat[:, qb * 16 + 2 * pc + hh:qb * 16 + 2 * pc + hh + 1]
                    sLD.wait(te, at=sn_q[pair])
                    for gi, (k0, glen) in enumerate(groups):
                        pss = ps_sA if gi % 2 == 0 else ps_sB
                        te.wait_ge(sAC.h, sn_grp[gi % 2])
                        for kk in range(glen):
                            kt = k0 + kk
                            ins = te.matmul(
                                pss[:, kk * 512:(kk + 1) * 512],
                                Kv[base:base + 64, pc, kt * 128:(kt + 1) * 128],
                                Qb[base:base + 64, pc % 2, :],
                                start=True, stop=True)
                        sPE.inc(ins)
                        sPE.wait(se)
                        sAC.inc(se.activation(
                            out=Pv[:, i % 2, k0 * 512:(k0 + glen) * 512],
                            in_=pss[:, 0:glen * 512],
                            func=AF.Exp, bias=bias_ap, scale=1.0))
                        sn_grp[gi % 2] = sAC.n
                    sn_exp_done[i] = sAC.n
                    sn_S_pe[pair] = sPE.n

                def emit_AV(i):
                    nonlocal sn_z_ve, sn_bc_ac
                    qb, pc, hh = heads[i]
                    h = 2 * pc + hh
                    base = 64 * hh
                    te.wait_ge(sVE.h, sn_z_ve)
                    te.wait_ge(sAC.h, sn_exp_done[i])
                    for kt in range(NT):
                        ins = te.matmul(ps_z[0:65, :],
                                        Vv[:, kt, h, :],
                                        Pv[:, i % 2, kt * 512:(kt + 1) * 512],
                                        start=(kt == 0), stop=(kt == NT - 1))
                    sPE.inc(ins)
                    sPE.wait(ve)
                    sVE.inc(ve.tensor_scalar(out=rl[:], in0=ps_z[64:65, :],
                                             scalar1=1e-30, scalar2=None, op0=ALU.add))
                    sVE.wait(ve)
                    sVE.inc(ve.reciprocal(out=rl[:], in_=rl[:]))
                    sn_rl = sVE.n
                    te.wait_ge(sVE.h, sn_rl)
                    te.wait_ge(sAC.h, sn_bc_ac)
                    sPE.inc(te.matmul(ps_m[0:64, :], self.ones_f[0:1, 0:64],
                                      rl[:], start=True, stop=True))
                    sPE.wait(se)
                    se.wait_ge(sVE.h, sn_z_ve)
                    sAC.inc(se.activation(out=bc_sb[:], in_=ps_m[0:64, :],
                                          func=AF.Identity))
                    sn_bc_ac = sAC.n
                    sAC.wait(ve)
                    pair = i // 2
                    if pair >= 2:
                        ve.wait_ge(sST.h, sn_zst.get(pair - 2, 0))
                    sVE.inc(ve.tensor_tensor(out=zstg[base:base + 64, pc % 2, :],
                                             in0=ps_z[0:64, :], in1=bc_sb[:],
                                             op=ALU.mult))
                    sn_z_ve = sVE.n
                    if hh == 1:
                        sVE.wait(sy)
                        sST.inc(sy.dma_start(
                            out=self.ZS[pc, :, qb * 512:(qb + 1) * 512],
                            in_=zstg.ap()[:, pc % 2, :]))
                        sn_zst[pair] = sST.n

                for i in range(len(heads)):
                    emit_S(i)
                    if i > 0:
                        emit_AV(i - 1)
                emit_AV(len(heads) - 1)
                sVE.wait(te)
                sAC.wait(te)
                sVE.wait(sy)
                self.tap("t_p", P.ap())
                self.tap("t_qb", Qb.ap())

            if self.taps:
                self.tap("t_zs", self.ZS)
            kvctx.close()
            with ExitStack() as a3:
                wo_b = self.sb(a3, "wo_b", [128, NCH * 1024], BF16)
                wo_bv = wo_b.ap().rearrange("p (c n) -> p c n", c=NCH)
                wstA3 = self.sb(a3, "wstA3", [128, 1024], F32)
                zbuf = self.sb(a3, "zbuf", [128, 2, NP, 512], BF16)
                xrt = self.sb(a3, "xrt", [128, 4, D], F32)
                ofm = self.sb(a3, "ofm", [128, 4096], BF16)
                xstg = self.sb(a3, "xstg", [128, 2, D], F32)
                ps_o = self.psum(a3, "ps_o", [128, 2048], F32)
                ps_T = self.psum(a3, "ps_T", [128, 1024], BF16)

                cast_done = []
                for c in range(NCH):
                    if c >= 1:
                        sy.wait_ge(sVE.h, cast_done[c - 1])
                    sLD.inc(sy.dma_start(out=wstA3[:],
                                         in_=self.WOT[c * 128:(c + 1) * 128, :]))
                    sLD.wait(ve)
                    sVE.inc(ve.tensor_copy(out=wo_bv[:, c, :], in_=wstA3[:]))
                    cast_done.append(sVE.n)

                zld = []
                for pc in range(NP):
                    sLD.inc(sy.dma_start(out=zbuf[:, 0, pc, :],
                                         in_=self.ZS[pc, :, 0:512]))
                zld.append(sLD.n)
                sn_o_ac = [0] * NTB
                sn_T_ve = [0] * NTB
                xst_list = []
                for tb in range(NTB):
                    if tb + 1 < NTB:
                        if tb >= 1:
                            sy.wait_ge(sPE.h, self.sPE.n)
                        for pc in range(NP):
                            sLD.inc(sy.dma_start(
                                out=zbuf[:, (tb + 1) % 2, pc, :],
                                in_=self.ZS[pc, :, (tb + 1) * 512:(tb + 2) * 512]))
                        zld.append(sLD.n)
                    if tb >= 1:
                        sy.wait_ge(sVE.h, sn_T_ve[tb - 1])
                    for ts4 in range(4):
                        sLD.inc(sy.dma_start(
                            out=xrt[:, ts4, :],
                            in_=self.X[tb * 512 + ts4 * 128:tb * 512 + (ts4 + 1) * 128, :]))
                    xrld = sLD.n
                    for half in range(2):
                        sLD.wait(te, at=zld[tb])
                        if tb >= 1 or half >= 1:
                            te.wait_ge(sAC.h, sn_o_ac[tb - 1] if half == 0 else self.sAC.n)
                        for oc4 in range(4):
                            oc = half * 4 + oc4
                            bank = ps_o[:, oc4 * 512:(oc4 + 1) * 512]
                            for zc in range(NP):
                                te.matmul(bank, wo_bv[:, zc, oc * 128:(oc + 1) * 128],
                                          zbuf[:, tb % 2, zc, :],
                                          start=(zc == 0), stop=False)
                            sPE.inc(te.matmul(bank, self.bo_b[0:1, oc * 128:(oc + 1) * 128],
                                              self.ones_b[0:1, :],
                                              start=False, stop=True))
                            sPE.wait(se)
                            sAC.inc(se.activation(out=ofm[:, oc * 512:(oc + 1) * 512],
                                                  in_=bank, func=AF.Identity))
                        sn_o_ac[tb] = sAC.n
                    for ts4 in range(4):
                        if tb >= 1 and ts4 == 0:
                            te.wait_ge(sVE.h, sn_T_ve[tb - 1])
                        if ts4 >= 1:
                            te.wait_ge(sVE.h, self.sVE.n)
                        sAC.wait(te)
                        for oc in range(NCH):
                            ins = te.transpose(ps_T[:, oc * 128:(oc + 1) * 128],
                                               ofm[:, oc * 512 + ts4 * 128:oc * 512 + (ts4 + 1) * 128],
                                               self.ident_b[:])
                        sPE.inc(ins)
                        sPE.wait(ve)
                        sLD.wait(ve, at=xrld)
                        if len(xst_list) >= 2:
                            ve.wait_ge(sST.h, xst_list[-2])
                        sVE.inc(ve.tensor_tensor(out=xstg[:, ts4 % 2, :], in0=ps_T[:],
                                                 in1=xrt[:, ts4, :], op=ALU.add))
                        sn_T_ve[tb] = sVE.n
                        sVE.wait(sy)
                        r0 = tb * 512 + ts4 * 128
                        sST.inc(sy.dma_start(out=self.XR[r0:r0 + 128, :],
                                             in_=xstg.ap()[:, ts4 % 2, :]))
                        xst_list.append(sST.n)
                sVE.wait(te)
                sAC.wait(te)
                sVE.wait(sy)
                sVE.wait(ge)

    def emit_ln_pass(self, src_dram, xlf, fstage, xlnb, ps_t):
        te, ve, se, sy = self.te, self.ve, self.se, self.sy
        sLD, sPE, sVE, sAC = self.sLD, self.sPE, self.sVE, self.sAC
        xlf_v = xlf.ap().rearrange("p (c t) -> p c t", c=NCH)
        xld = []
        sVE.wait(sy)
        sLD.inc(sy.dma_start(out=fstage[:, 0, :], in_=src_dram[0:128, :]))
        xld.append(sLD.n)
        sLD.inc(sy.dma_start(out=fstage[:, 1, :], in_=src_dram[128:256, :]))
        xld.append(sLD.n)
        sn_x = [0] * NT
        sn_t = [0] * NT
        sn_pe = [0] * NT
        for tt in range(NT):
            xbuf = fstage[:, tt % 2, :]
            sLD.wait(ve, at=xld[tt])
            if tt >= 1:
                ve.wait_ge(sPE.h, sn_pe[tt - 1])
            self.emit_ln(xbuf, xlnb[:], self.c_eps_ln[:, 0:1])
            sn_x[tt] = sVE.n
            if tt + 2 < NT:
                sy.wait_ge(sVE.h, sn_x[tt])
                sLD.inc(sy.dma_start(out=fstage[:, tt % 2, :],
                                     in_=src_dram[(tt + 2) * 128:(tt + 3) * 128, :]))
                xld.append(sLD.n)
            if tt >= 1:
                te.wait_ge(sVE.h, sn_t[tt - 1])
            sVE.wait(te)
            for c in range(NCH):
                ins = te.transpose(ps_t[:, c * 128:(c + 1) * 128],
                                   xlnb[:, c * 128:(c + 1) * 128], self.ident_b[:])
            sPE.inc(ins)
            sn_pe[tt] = sPE.n
            sPE.wait(ve)
            sVE.inc(ve.tensor_copy(
                out=xlf_v[:, :, tt * 128:(tt + 1) * 128],
                in_=ps_t.rearrange("p (c t) -> p c t", c=NCH)))
            sn_t[tt] = sVE.n

    def phase_ffn(self, src_dram, dst_dram, first_call):
        sfx = "a" if first_call else "b"
        te, ve, se, ge, sy = self.te, self.ve, self.se, self.ge, self.sy
        sLD, sST, sPE, sVE, sAC = self.sLD, self.sST, self.sPE, self.sVE, self.sAC
        NCK = NE * 4
        W1Bv = self.W1B.rearrange("e c p i -> e p c i")
        W3Bv = self.W3B.rearrange("e c p i -> e p c i")

        with ExitStack() as f:
            xlf = self.sb(f, "xlf" + sfx, [128, NCH * T], BF16)
            act_all = self.sb(f, "act_all" + sfx, [128, NCK * 512], BF16)
            w1b = self.sb(f, "w1b" + sfx, [128, 2, NCH * 512], BF16)
            w3b = self.sb(f, "w3b" + sfx, [128, 2, NCH * 512], BF16)
            w2b = self.sb(f, "w2b" + sfx, [128, 2, D], BF16)
            sbc = self.sb(f, "sbc" + sfx, [128, 512], F32)
            s_t = self.sb(f, "s_t" + sfx, [16, 128], BF16)
            s_row = self.sb(f, "s_row" + sfx, [1, 2, 512], BF16)
            fstage = self.sb(f, "fstage" + sfx, [128, 4, D], F32)
            xlnb = self.sb(f, "xlnb2" + sfx, [128, D], BF16)
            gw_b = self.sb(f, "gw_b" + sfx, [128, NCH * 16], BF16)
            rsc = self.sb(f, "rsc" + sfx, [128, 64], F32)
            ofm = self.sb(f, "ofm2" + sfx, [128, 4096], BF16)
            ps_g = self.psum(f, "ps_g" + sfx, [128, 2048], F32)
            ps_u = self.psum(f, "ps_u" + sfx, [128, 2048], F32)
            ps_t = ps_g.ap()[:, 0:512].bitcast(BF16)
            ps_o_banks = ([ps_g.ap()[:, i * 512:(i + 1) * 512] for i in range(4)]
                          + [ps_u.ap()[:, i * 512:(i + 1) * 512] for i in range(4)])

            if first_call:
                stg = act_all.ap().bitcast(F32)
                wsb = ofm.ap()
                for e in range(NE):
                    for (WSRC, WDST) in (
                        (self.W1.rearrange("e c p i -> e p c i"), W1Bv),
                        (self.W3.rearrange("e c p i -> e p c i"), W3Bv),
                    ):
                        sVE.wait(sy)
                        self.sST.wait(sy)
                        sLD.inc(sy.dma_start(
                            out=stg[:, 0:4096].rearrange("p (c i) -> p c i", c=NCH),
                            in_=WSRC[e]))
                        sLD.wait(ve)
                        sVE.inc(ve.tensor_copy(out=wsb[:, 0:4096], in_=stg[:, 0:4096]))
                        sVE.wait(sy)
                        sST.inc(sy.dma_start(
                            out=WDST[e],
                            in_=wsb[:, 0:4096].rearrange("p (c i) -> p c i", c=NCH)))
                for ck in range(NCK):
                    sVE.wait(sy)
                    self.sST.wait(sy)
                    sLD.inc(sy.dma_start(out=stg[:, 0:1024], in_=self.W2[ck]))
                    sLD.wait(ve)
                    sVE.inc(ve.tensor_copy(out=wsb[:, 0:1024], in_=stg[:, 0:1024]))
                    sVE.wait(sy)
                    sST.inc(sy.dma_start(out=self.W2B[ck], in_=wsb[:, 0:1024]))
                sST.wait(sy)
                sST.wait(ve)

            self.emit_ln_pass(src_dram, xlf, fstage, xlnb,
                              ps_t=ps_u.ap()[:, 0:512].bitcast(BF16))
            xlf_v = xlf.ap().rearrange("p (c t) -> p c t", c=NCH)

            with ExitStack() as g:
                gstg = self.sb(g, "gstg" + sfx, [128, NCH * 16], F32)
                sVE.wait(sy)
                sLD.inc(sy.dma_start(
                    out=gstg.ap().rearrange("p (c n) -> p c n", c=NCH),
                    in_=self.GWT.rearrange("(c p) n -> p c n", p=128)))
                sLD.wait(ve)
                sVE.inc(ve.tensor_copy(out=gw_b[:], in_=gstg[:]))
            gw_v = gw_b.ap().rearrange("p (c n) -> p c n", c=NCH)

            ps_rt = ps_g.ap()[:, 0:512]
            lg = rsc[:, 0:16]
            t8 = rsc[:, 16:24]
            neg = rsc[:, 24:25]
            thr = rsc[:, 25:26]
            s4 = rsc[:, 26:27]
            rcp = rsc[:, 27:28]
            es = rsc[:, 32:48]
            sct = rsc[:, 48:64]
            sn_rt = 0
            sn_st1 = sn_st2 = 0
            for tt in range(NT):
                te.wait_ge(sVE.h, self.sVE.n)
                for c in range(NCH):
                    ins = te.matmul(ps_rt[:, 0:16], xlf_v[:, c, tt * 128:(tt + 1) * 128],
                                    gw_v[:, c, :], start=(c == 0), stop=(c == NCH - 1))
                sPE.inc(ins)
                sPE.wait(ve)
                sVE.inc(ve.tensor_copy(out=lg, in_=ps_rt[:, 0:16]))
                sVE.wait(ve)
                sVE.inc(ve.max(out=t8, in_=lg))
                sVE.wait(ve)
                sVE.inc(ve.tensor_scalar_mul(neg, t8[:, 0:1], -1.0))
                sVE.wait(se)
                sAC.inc(se.activation(out=es, in_=lg, func=AF.Exp, bias=neg[:, 0:1],
                                      scale=1.0))
                sAC.wait(se)
                sAC.inc(se.activation(out=thr, in_=t8[:, 3:4], func=AF.Exp,
                                      bias=neg[:, 0:1], scale=1.0))
                sAC.wait(ve)
                sVE.inc(ve.tensor_scalar(out=sct, in0=es, scalar1=thr[:, 0:1],
                                         scalar2=None, op0=ALU.is_ge))
                sVE.wait(ve)
                sVE.inc(ve.tensor_tensor(out=es, in0=es, in1=sct, op=ALU.mult))
                sVE.wait(ve)
                sVE.inc(ve.tensor_reduce(out=s4, in_=es, op=ALU.add, axis=AX.X))
                sVE.wait(ve)
                sVE.inc(ve.reciprocal(out=rcp, in_=s4))
                sVE.wait(ve)
                sVE.inc(ve.tensor_scalar(out=sct, in0=es, scalar1=rcp[:, 0:1],
                                         scalar2=None, op0=ALU.mult))
                te.wait_ge(sVE.h, self.sVE.n)
                if tt >= 1:
                    te.wait_ge(sVE.h, sn_rt)
                if tt >= 2:
                    ve.wait_ge(sST.h, sn_st2)
                sPE.inc(te.transpose(ps_rt[0:16, 16:144], sct, self.ident_f[:]))
                sPE.wait(ve)
                sVE.inc(ve.tensor_copy(out=s_t[:], in_=ps_rt[0:16, 16:144]))
                sn_rt = sVE.n
                sVE.wait(sy)
                sST.inc(sy.dma_start(out=self.SFD[:, tt * 128:(tt + 1) * 128],
                                     in_=s_t[:]))
                sn_st2 = sn_st1
                sn_st1 = sST.n

            w1v = w1b.ap().rearrange("p b (c n) -> p b c n", c=NCH)
            w3v = w3b.ap().rearrange("p b (c n) -> p b c n", c=NCH)
            sn_w13 = {}
            sn_bc_pe = {}
            sn_gu_pe = {}
            sn_act_ve = {}
            sn_silu_ac = {}
            sn_w2 = {}
            sn_w2_pe = {}
            sn_o_ac = 0
            sn_T_ve = 0
            sn_xst = []
            for gi0 in (0, 1):
                sLD.inc(sy.dma_start(out=w1v[:, gi0, :, :], in_=W1Bv[gi0 % NE]))
                sLD.inc(sy.dma_start(out=w3v[:, gi0, :, :], in_=W3Bv[gi0 % NE]))
                sn_w13[gi0] = sLD.n

            for tb in range(NTB):
                sn_sbc_prev = 0
                for e in range(NE):
                    gi = tb * NE + e
                    if gi >= 2:
                        sy.wait_ge(sPE.h, sn_bc_pe.get(gi - 2, 0))
                    sLD.inc(sy.dma_start(out=s_row.ap()[:, gi % 2, :],
                                         in_=self.SFD[e:e + 1, tb * 512:(tb + 1) * 512]))
                    sn_srow = sLD.n
                    te.wait_ge(sAC.h, max(sn_silu_ac.get(gi - 1, 0), sn_o_ac))
                    te.wait_ge(sVE.h, sn_sbc_prev)
                    sLD.wait(te, at=sn_srow)
                    sPE.inc(te.matmul(ps_g.ap()[:, 0:512], self.ones_b[0:1, 0:128],
                                      s_row.ap()[:, gi % 2, :],
                                      start=True, stop=True))
                    sn_bc_pe[gi] = sPE.n
                    sPE.wait(ve)
                    sVE.inc(ve.tensor_copy(out=sbc[:], in_=ps_g.ap()[:, 0:512]))
                    sn_sbc = sn_sbc_prev = sVE.n

                    sLD.wait(te, at=sn_w13[gi])
                    te.wait_ge(sVE.h, sn_sbc)
                    te.wait_ge(sVE.h, sn_act_ve.get(gi - 1, 0))
                    te.wait_ge(sAC.h, max(sn_silu_ac.get(gi - 1, 0), sn_o_ac))
                    for ic in range(4):
                        gbank = ps_g.ap()[:, ic * 512:(ic + 1) * 512]
                        ubank = ps_u.ap()[:, ic * 512:(ic + 1) * 512]
                        for c in range(NCH):
                            te.matmul(gbank, w1v[:, gi % 2, c, ic * 128:(ic + 1) * 128],
                                      xlf_v[:, c, tb * 512:(tb + 1) * 512],
                                      start=(c == 0), stop=(c == NCH - 1))
                        for c in range(NCH):
                            ins = te.matmul(ubank, w3v[:, gi % 2, c, ic * 128:(ic + 1) * 128],
                                            xlf_v[:, c, tb * 512:(tb + 1) * 512],
                                            start=(c == 0), stop=(c == NCH - 1))
                    sPE.inc(ins)
                    sn_gu_pe[gi] = sPE.n
                    ni = gi + 2
                    if ni < NTB * NE:
                        sy.wait_ge(sPE.h, sn_gu_pe[gi])
                        sLD.inc(sy.dma_start(out=w1v[:, ni % 2, :, :], in_=W1Bv[ni % NE]))
                        sLD.inc(sy.dma_start(out=w3v[:, ni % 2, :, :], in_=W3Bv[ni % NE]))
                        sn_w13[ni] = sLD.n
                    sPE.wait(se)
                    sAC.inc(se.activation(out=ofm[:, 0:2048], in_=ps_g.ap()[:, 0:2048],
                                          func=AF.Silu))
                    sn_silu_ac[gi] = sAC.n
                    sPE.wait(ve)
                    sVE.inc(ve.tensor_tensor(
                        out=ofm.ap()[:, 2048:4096].rearrange("p (i n) -> p i n", i=4),
                        in0=ps_u.ap().rearrange("p (i n) -> p i n", i=4),
                        in1=sbc.ap().unsqueeze(1).to_broadcast([128, 4, 512]),
                        op=ALU.mult))
                    sAC.wait(ve)
                    sVE.inc(ve.tensor_tensor(
                        out=act_all[:, e * 2048:(e + 1) * 2048],
                        in0=ofm[:, 0:2048], in1=ofm[:, 2048:4096], op=ALU.mult))
                    sn_act_ve[gi] = sVE.n

                if tb >= 1:
                    sy.wait_ge(sPE.h, self.sPE.n)
                sLD.inc(sy.dma_start(out=w2b[:, 0, :], in_=self.W2B[0]))
                sn_w2[0] = sLD.n
                for ck in range(NCK):
                    if ck + 1 < NCK:
                        if ck >= 1:
                            sy.wait_ge(sPE.h, sn_w2_pe.get(ck - 1, 0))
                        sLD.inc(sy.dma_start(out=w2b[:, (ck + 1) % 2, :],
                                             in_=self.W2B[ck + 1]))
                        sn_w2[ck + 1] = sLD.n
                    sLD.wait(te, at=sn_w2[ck])
                    if ck == 0:
                        te.wait_ge(sVE.h, sn_act_ve[tb * NE + NE - 1])
                        te.wait_ge(sAC.h, sn_silu_ac[tb * NE + NE - 1])
                        te.wait_ge(sVE.h, sn_sbc_prev)
                    for oc in range(NCH):
                        ins = te.matmul(ps_o_banks[oc],
                                        w2b[:, ck % 2, oc * 128:(oc + 1) * 128],
                                        act_all[:, ck * 512:(ck + 1) * 512],
                                        start=(ck == 0), stop=(ck == NCK - 1))
                    sPE.inc(ins)
                    sn_w2_pe[ck] = sPE.n
                sPE.wait(se)
                for oc in range(NCH):
                    ins = se.activation(out=ofm[:, oc * 512:(oc + 1) * 512],
                                        in_=ps_o_banks[oc], func=AF.Identity)
                sAC.inc(ins)
                sn_o_ac = self.sAC.n
                for ts4 in range(4):
                    r0 = tb * 512 + ts4 * 128
                    sy.wait_ge(sVE.h, sn_T_ve)
                    sLD.inc(sy.dma_start(out=fstage[:, ts4 % 2, :],
                                         in_=src_dram[r0:r0 + 128, :]))
                    xrld = sLD.n
                    sAC.wait(te)
                    te.wait_ge(sVE.h, sn_T_ve)
                    for oc in range(NCH):
                        ins = te.transpose(ps_t[:, oc * 128:(oc + 1) * 128],
                                           ofm[:, oc * 512 + ts4 * 128:oc * 512 + (ts4 + 1) * 128],
                                           self.ident_b[:])
                    sPE.inc(ins)
                    sPE.wait(ve)
                    sLD.wait(ve, at=xrld)
                    if len(sn_xst) >= 2:
                        ve.wait_ge(sST.h, sn_xst[-2])
                    sVE.inc(ve.tensor_tensor(out=fstage[:, 2 + ts4 % 2, :],
                                             in0=ps_t, in1=fstage[:, ts4 % 2, :],
                                             op=ALU.add))
                    sn_T_ve = sVE.n
                    sVE.wait(sy)
                    sST.inc(sy.dma_start(out=dst_dram[r0:r0 + 128, :],
                                         in_=fstage.ap()[:, 2 + ts4 % 2, :]))
                    sn_xst.append(sST.n)
            sVE.wait(te)
            sAC.wait(te)
            sVE.wait(sy)
            sVE.wait(ge)
            self.tap("t_sfd", self.SFD)
            self.tap("t_act", act_all.ap())
            self.tap("t_xlf", xlf.ap())

    def phase_b(self, src_dram, dst_dram):
        te, ve, se, ge, sy = self.te, self.ve, self.se, self.ge, self.sy
        sLD, sST, sPE, sVE, sAC = self.sLD, self.sST, self.sPE, self.sVE, self.sAC

        with ExitStack() as b:
            xlf = self.sb(b, "xlf3", [128, NCH * T], BF16)
            B1 = self.sb(b, "B1", [128, T], F32)
            B2 = self.sb(b, "B2", [128, T], F32)
            B3 = self.sb(b, "B3", [128, T], F32)
            B4 = self.sb(b, "B4", [128, T], F32)
            fstage = self.sb(b, "fstage3", [128, 4, D], F32)
            xlnb = self.sb(b, "xlnb3", [128, D], BF16)
            wg_b = self.sb(b, "wg_b", [128, 2, 128], BF16)
            wgstg = self.sb(b, "wgstg", [128, 128], F32)
            bcol = self.sb(b, "bcol", [128, 2], F32)
            hstage = self.sb(b, "hstage", [128, 2, 512], F32)
            xq = self.sb(b, "xq", [128, 2, 512], F32)
            ps_gx = self.psum(b, "ps_gx", [128, 2048], F32)
            ps_ht = self.psum(b, "ps_ht", [128, 512], F32)

            self.emit_ln_pass(src_dram, xlf, fstage, xlnb,
                              ps_t=ps_gx.ap()[:, 0:512].bitcast(BF16))
            xlf_v = xlf.ap().rearrange("p (c t) -> p c t", c=NCH)

            for c in range(NCH):
                for i, (WSRC, BSRC) in enumerate(((self.IGW, self.IGB),
                                                  (self.AGW, self.AGB))):
                    sVE.wait(sy)
                    sLD.inc(sy.dma_start(out=wgstg[:], in_=WSRC[c]))
                    sLD.inc(sy.dma_start(out=bcol[:, i:i + 1],
                                         in_=BSRC[c:c + 1, :].rearrange("a b -> b a")))
                    sLD.wait(ve)
                    sVE.inc(ve.tensor_copy(out=wg_b[:, i, :], in_=wgstg[:]))
                sn_wg = sVE.n
                for i, dstbuf in ((0, B1), (1, B2)):
                    for tb in range(NTB):
                        bank = ps_gx.ap()[:, (tb % 2) * 512:(tb % 2 + 1) * 512]
                        te.wait_ge(sVE.h, sn_wg)
                        te.wait_ge(sAC.h, self.sAC.n)
                        sPE.inc(te.matmul(bank, wg_b[:, i, :],
                                          xlf_v[:, c, tb * 512:(tb + 1) * 512],
                                          start=True, stop=True))
                        sPE.wait(se)
                        sAC.inc(se.activation(out=dstbuf[:, tb * 512:(tb + 1) * 512],
                                              in_=bank, func=AF.Sigmoid,
                                              bias=bcol[:, i:i + 1], scale=1.0))
                sAC.wait(ve)
                sVE.inc(ve.tensor_scalar(out=B2[:], in0=B2[:],
                                         scalar1=self.sp8[:, c:c + 1], scalar2=None,
                                         op0=ALU.mult))
                sVE.inc(ve.tensor_tensor_scan(out=B3[:], data0=B2[:], data1=B2[:],
                                              initial=0.0, op0=ALU.add, op1=ALU.bypass))
                sVE.inc(ve.tensor_scalar(out=B3[:], in0=B3[:], scalar1=LCLIP,
                                         scalar2=None, op0=ALU.max))
                ve.memset(B4[:, 0:1], 0.0)
                ve.tensor_copy(out=B4[:, 1:2], in_=B3[:, 0:1])
                sVE.inc(ve.tensor_tensor(out=B4[:, 2:T], in0=B3[:, 1:T - 1],
                                         in1=B3[:, 0:T - 2], op=ALU.subtract))
                sVE.wait(se)
                sAC.inc(se.activation(out=B4[:], in_=B4[:], func=AF.Exp))
                sAC.inc(se.activation(out=B2[:], in_=B2[:], func=AF.Exp, scale=2.0))
                sAC.wait(ve)
                ve.tensor_scalar(out=B2[:], in0=B2[:], scalar1=-1.0, scalar2=1.0,
                                 op0=ALU.mult, op1=ALU.add)
                sVE.inc(ve.tensor_scalar(out=B2[:], in0=B2[:], scalar1=1e-7,
                                         scalar2=None, op0=ALU.max))
                sVE.wait(se)
                se.activation(out=B2[:], in_=B2[:], func=AF.Ln)
                sAC.inc(se.activation(out=B2[:], in_=B2[:], func=AF.Exp, scale=0.5))
                sAC.wait(ve)
                ve.tensor_tensor(out=B1[:], in0=xlf_v[:, c, :], in1=B1[:], op=ALU.mult)
                sVE.inc(ve.tensor_tensor(out=B1[:], in0=B1[:], in1=B2[:], op=ALU.mult))
                sVE.inc(ve.tensor_tensor_scan(out=B3[:], data0=B4[:], data1=B1[:],
                                              initial=0.0, op0=ALU.mult, op1=ALU.add))
                sn_hst = []
                sn_hT = 0
                xqld = {}
                for q4 in range(NCH):
                    r0 = q4 * 512
                    dcol = dst_dram[r0:r0 + 512, c * 128:(c + 1) * 128] \
                        .rearrange("(q p) d -> p q d", p=128)
                    if q4 >= 2:
                        sy.wait_ge(sVE.h, sn_hT)
                    sLD.inc(sy.dma_start(out=xq.ap()[:, q4 % 2, :]
                                         .rearrange("p (q d) -> p q d", q=4),
                                         in_=dcol))
                    xqld[q4] = sLD.n
                    if q4 >= 1:
                        te.wait_ge(sVE.h, sn_hT)
                    sVE.wait(te)
                    for j in range(4):
                        tt = q4 * 4 + j
                        ins = te.transpose(ps_ht[:, j * 128:(j + 1) * 128],
                                           B3[:, tt * 128:(tt + 1) * 128],
                                           self.ident_f[:])
                    sPE.inc(ins)
                    sPE.wait(ve)
                    sLD.wait(ve, at=xqld[q4])
                    if len(sn_hst) >= 2:
                        ve.wait_ge(sST.h, sn_hst[-2])
                    sVE.inc(ve.tensor_tensor(out=hstage.ap()[:, q4 % 2, :],
                                             in0=ps_ht.ap(),
                                             in1=xq.ap()[:, q4 % 2, :], op=ALU.add))
                    sn_hT = sVE.n
                    sVE.wait(sy)
                    sST.inc(sy.dma_start(
                        out=dcol,
                        in_=hstage.ap()[:, q4 % 2, :].rearrange("p (q d) -> p q d", q=4)))
                    sn_hst.append(sST.n)
                sST.wait(sy)
            sVE.wait(te)
            sAC.wait(te)
            sVE.wait(sy)
            self.tap("t_B1", B1.ap())
            self.tap("t_B2", B2.ap())
            self.tap("t_B3", B3.ap())
            self.tap("t_B4", B4.ap())
            self.tap("t_xlf3", xlf.ap())

    def finish(self, copy_xr_to_out=False):
        ge, sy, ve = self.ge, self.sy, self.ve
        if copy_xr_to_out:
            self.sST.wait(sy)
            self.sST.inc(sy.dma_start(out=self.OUT, in_=self.XR))
        self.sST.wait(sy)
        self.sLD.wait(sy)
        self.sST.wait(ge)

    def build(self):
        self.prelude()
        if "A" in PHASES:
            self.phase_a()
        first = True
        if "F1" in PHASES:
            self.phase_ffn(self.XR, self.XR, first_call=first)
            first = False
        if "B" in PHASES:
            self.phase_b(self.XR, self.XR)
        if "F2" in PHASES:
            last = ("A" in PHASES and "F1" in PHASES and "B" in PHASES)
            self.phase_ffn(self.XR, self.OUT if last else self.XR,
                           first_call=first)
            first = False
            done_full = last
        else:
            done_full = False
        self.finish(copy_xr_to_out=not done_full)
        self.ctx.close()
        return self.nc


def _shard_inputs(inputs):
    f = np.ascontiguousarray
    x = inputs["L0_attn_a_x"]
    wq = np.asarray(inputs["L0_attn_a_wq"])
    wk = np.asarray(inputs["L0_attn_a_wk"])
    common = {
        "wqkvT": f(np.asarray(inputs["L0_attn_a_Wqkv"]).T),
        "bqkv": f(np.asarray(inputs["L0_attn_a_bqkv"]).reshape(1, 3 * D)),
        "woT": f(np.asarray(inputs["L0_attn_a_Wo"]).T),
        "bo": f(np.asarray(inputs["L0_attn_a_bo"]).reshape(1, D)),
        "wqkrow": f(np.concatenate([np.tile(wq, NH), np.tile(wk, NH)]).reshape(1, 2 * D)),
        "igw": f(np.asarray(inputs["L0_attn_b_ig_w"])),
        "igb": f(np.asarray(inputs["L0_attn_b_ig_b"])),
        "agw": f(np.asarray(inputs["L0_attn_b_ag_w"])),
        "agb": f(np.asarray(inputs["L0_attn_b_ag_b"])),
        "apar": f(np.asarray(inputs["L0_attn_b_a_param"]).reshape(NCH, 128)),
        "gwT": f(np.asarray(inputs["L0_ffn_gate_w"]).T),
        "w1": f(np.asarray(inputs["L0_ffn_w1"]).reshape(NE, NCH, 128, DI)),
        "w3": f(np.asarray(inputs["L0_ffn_w3"]).reshape(NE, NCH, 128, DI)),
        "w2": f(np.asarray(inputs["L0_ffn_w2"]).reshape(NE * 4, 128, D)),
    }
    in_maps = []
    for i in range(8):
        m = dict(common)
        m["x"] = f(np.asarray(x[i]))
        in_maps.append(m)
    return in_maps


def kernel(**inputs):
    in_maps = _shard_inputs(inputs)
    nc = Builder().build()
    res = run_bass_kernel_spmd(nc, in_maps, core_ids=list(range(8)))
    out = np.stack([np.asarray(r["out"]) for r in res.results], axis=0)
    return out.astype(np.float32)


# version 04
# speedup vs baseline: 1.0854x; 1.0537x over previous
import numpy as np
import concourse.bass as bass
import concourse.mybir as mybir
from concourse.bass_utils import run_bass_kernel_spmd
from contextlib import ExitStack

F32 = mybir.dt.float32
BF16 = mybir.dt.bfloat16
AF = mybir.ActivationFunctionType
ALU = mybir.AluOpType
AX = mybir.AxisListType

T = 4096
D = 1024
NT = 32
NH = 16
DH = 64
NP = 8
NE = 16
DI = 512
NTB = 8
NCH = 8
LCLIP = float(np.log(np.float32(1e-7)))

PHASES = ("A", "F1", "B", "F2")
A2_QBS = None
DEV_TAPS = {}


class Sem:
    def __init__(self, ctx, nc, name, step=1):
        self.h = ctx.enter_context(nc.semaphore(name))
        self.n = 0
        self.step = step

    def inc(self, ins, k=1):
        ins.then_inc(self.h, k * self.step)
        self.n += k * self.step
        return self.n

    def wait(self, eng, at=None):
        eng.wait_ge(self.h, self.n if at is None else at)


class Builder:
    def __init__(self):
        nc = bass.Bass("TRN2", target_bir_lowering=False, dynamic_dma_scratch_size=8192)
        self.nc = nc
        self.te, self.ve, self.se, self.ge, self.sy = (
            nc.tensor, nc.vector, nc.scalar, nc.gpsimd, nc.sync)
        self.ctx = ExitStack()
        self.sLD = Sem(self.ctx, nc, "sLD", 16)
        self.sLG = Sem(self.ctx, nc, "sLG", 16)
        self.sST = Sem(self.ctx, nc, "sST", 16)
        self.sPE = Sem(self.ctx, nc, "sPE")
        self.sVE = Sem(self.ctx, nc, "sVE")
        self.sAC = Sem(self.ctx, nc, "sAC")
        self.sGE = Sem(self.ctx, nc, "sGE")
        self._dram()
        self.taps = {}
        for name, (shape, dt) in DEV_TAPS.items():
            self.taps[name] = nc.dram_tensor(name, list(shape), dt,
                                             kind="ExternalOutput").ap()

    def sb(self, ctx, name, shape, dtype):
        return ctx.enter_context(self.nc.sbuf_tensor(name, shape, dtype))

    def psum(self, ctx, name, shape, dtype):
        return ctx.enter_context(self.nc.psum_tensor(name, shape, dtype))

    def _dram(self):
        nc = self.nc
        self.X = nc.dram_tensor("x", [T, D], F32, kind="ExternalInput").ap()
        self.WQKVT = nc.dram_tensor("wqkvT", [D, 3 * D], F32, kind="ExternalInput").ap()
        self.BQKV = nc.dram_tensor("bqkv", [1, 3 * D], F32, kind="ExternalInput").ap()
        self.WOT = nc.dram_tensor("woT", [D, D], F32, kind="ExternalInput").ap()
        self.BO = nc.dram_tensor("bo", [1, D], F32, kind="ExternalInput").ap()
        self.WQKROW = nc.dram_tensor("wqkrow", [1, 2 * D], F32, kind="ExternalInput").ap()
        self.IGW = nc.dram_tensor("igw", [NCH, 128, 128], F32, kind="ExternalInput").ap()
        self.IGB = nc.dram_tensor("igb", [NCH, 128], F32, kind="ExternalInput").ap()
        self.AGW = nc.dram_tensor("agw", [NCH, 128, 128], F32, kind="ExternalInput").ap()
        self.AGB = nc.dram_tensor("agb", [NCH, 128], F32, kind="ExternalInput").ap()
        self.APAR = nc.dram_tensor("apar", [NCH, 128], F32, kind="ExternalInput").ap()
        self.GWT = nc.dram_tensor("gwT", [D, NE], F32, kind="ExternalInput").ap()
        self.W1 = nc.dram_tensor("w1", [NE, NCH, 128, DI], F32, kind="ExternalInput").ap()
        self.W3 = nc.dram_tensor("w3", [NE, NCH, 128, DI], F32, kind="ExternalInput").ap()
        self.W2 = nc.dram_tensor("w2", [NE * 4, 128, D], F32, kind="ExternalInput").ap()
        self.OUT = nc.dram_tensor("out", [T, D], F32, kind="ExternalOutput").ap()
        self.XR = nc.dram_tensor("xres", [T, D], F32, kind="Internal").ap()
        self.QS = nc.dram_tensor("qsc", [NT, 128, D], BF16, kind="Internal").ap()
        self.ZS = nc.dram_tensor("zsc", [NP, 128, T], BF16, kind="Internal").ap()
        self.W13B = nc.dram_tensor("w13bd", [NE, 128, 2 * NCH * DI], BF16, kind="Internal").ap()
        self.W2B = nc.dram_tensor("w2bd", [NE * 4, 128, D], BF16, kind="Internal").ap()
        self.MSC = nc.dram_tensor("msc", [8, 32], F32, kind="Internal").ap()
        self.SFD = nc.dram_tensor("sfd", [NE, T], BF16, kind="Internal").ap()

    def tap(self, name, src_ap):
        if name in self.taps:
            self.sVE.wait(self.sy)
            self.sAC.wait(self.sy)
            self.sPE.wait(self.sy)
            self.sST.inc(self.sy.dma_start(out=self.taps[name], in_=src_ap))

    def prelude(self):
        ctx, nc = self.ctx, self.nc
        te, ve, se, ge, sy = self.te, self.ve, self.se, self.ge, self.sy
        sLD, sVE, sAC, sPE, sGE = self.sLD, self.sVE, self.sAC, self.sPE, self.sGE

        self.ident_b = self.sb(ctx, "ident_b", [128, 128], BF16)
        self.ident_f = self.sb(ctx, "ident_f", [128, 128], F32)
        self.ones_b = self.sb(ctx, "ones_b", [1, 512], BF16)
        self.ones_f = self.sb(ctx, "ones_f", [1, 128], F32)
        self.c_eps_ln = self.sb(ctx, "c_eps_ln", [128, 1], F32)
        self.c_eps_rms = self.sb(ctx, "c_eps_rms", [128, 1], F32)
        self.c_one = self.sb(ctx, "c_one", [128, 1], F32)
        self.sp8 = self.sb(ctx, "sp8", [128, NCH], F32)
        self.lnst = self.sb(ctx, "lnst", [128, 12], F32)
        self.lnmv = self.sb(ctx, "lnmv", [128, 2], F32)
        self.lnr = self.sb(ctx, "lnr", [128, 2], F32)

        with ExitStack() as pctx:
            iscr = self.sb(pctx, "iscr", [128, 128], F32)
            stg1 = self.sb(pctx, "stg1", [1, 3 * D], F32)
            stg2 = self.sb(pctx, "stg2", [128, NCH], F32)
            sp_t = self.sb(pctx, "sp_t", [128, NCH], F32)
            ps_pre = self.psum(pctx, "ps_pre", [128, 512], F32)

            sGE.inc(ge.iota(out=iscr[:], pattern=[[-1, 128]], base=0,
                            channel_multiplier=1,
                            allow_small_or_imprecise_dtypes=True))
            ve.memset(self.ones_b[:], 1.0)
            ve.memset(self.ones_f[:], 1.0)
            ve.memset(self.c_eps_ln[:], 1e-5)
            ve.memset(self.c_eps_rms[:], 1e-6)
            sVE.inc(ve.memset(self.c_one[:], 1.0))
            sGE.wait(ve)
            ve.tensor_scalar(out=self.ident_b[:], in0=iscr[:], scalar1=0.0,
                             scalar2=None, op0=ALU.is_equal)
            sVE.inc(ve.tensor_scalar(out=self.ident_f[:], in0=iscr[:],
                                     scalar1=0.0, scalar2=None, op0=ALU.is_equal))

            for c in range(NCH):
                sLD.inc(sy.dma_start(
                    out=stg2[:, c:c + 1],
                    in_=self.APAR[c:c + 1, :].rearrange("a b -> b a")))
            sLD.wait(se)
            sAC.inc(se.activation(out=sp_t[:], in_=stg2[:], func=AF.Abs))
            sAC.wait(se)
            sAC.inc(se.activation(out=sp_t[:], in_=sp_t[:], func=AF.Exp, scale=-1.0))
            sAC.wait(se)
            sVE.wait(se)
            sAC.inc(se.activation(out=sp_t[:], in_=sp_t[:], func=AF.Ln,
                                  bias=self.c_one[:, 0:1], scale=1.0))
            sLD.wait(ve)
            sAC.wait(ve)
            sVE.inc(ve.tensor_scalar(out=self.sp8[:], in0=stg2[:], scalar1=0.0,
                             scalar2=None, op0=ALU.max))
            sVE.wait(ve)
            sVE.inc(ve.tensor_tensor(out=self.sp8[:], in0=self.sp8[:], in1=sp_t[:], op=ALU.add))
            sVE.wait(ve)
            sVE.inc(ve.tensor_scalar(out=self.sp8[:], in0=self.sp8[:],
                                     scalar1=-8.0, scalar2=None, op0=ALU.mult))
            sVE.wait(te)
            sVE.wait(ge)
            sVE.wait(sy)

    def emit_ln(self, xbuf, out_bf, eps_ap):
        ve, se = self.ve, self.se
        sVE, sAC = self.sVE, self.sAC
        ve.bn_stats(out=self.lnst[:, 0:6], in_=xbuf[:, 0:512])
        sVE.inc(ve.bn_stats(out=self.lnst[:, 6:12], in_=xbuf[:, 512:1024]))
        sVE.wait(ve)
        sVE.inc(ve.bn_aggr(out=self.lnmv[:], in_=self.lnst[:]))
        sVE.wait(ve)
        sVE.wait(se)
        sAC.inc(se.activation(out=self.lnr[:, 0:1], in_=self.lnmv[:, 1:2], func=AF.Ln,
                      bias=eps_ap, scale=1.0))
        sAC.wait(se)
        sAC.inc(se.activation(out=self.lnr[:, 0:1], in_=self.lnr[:, 0:1],
                              func=AF.Exp, scale=-0.5))
        sAC.wait(ve)
        sVE.inc(ve.scalar_tensor_tensor(out=self.lnr[:, 1:2], in0=self.lnmv[:, 0:1],
                                        scalar=-1.0, in1=self.lnr[:, 0:1],
                                        op0=ALU.mult, op1=ALU.mult))
        sVE.wait(ve)
        sVE.inc(ve.tensor_scalar(out=out_bf, in0=xbuf,
                                 scalar1=self.lnr[:, 0:1], scalar2=self.lnr[:, 1:2],
                                 op0=ALU.mult, op1=ALU.add))

    def phase_a(self):
        ctx = self.ctx
        nc = self.nc
        te, ve, se, ge, sy = self.te, self.ve, self.se, self.ge, self.sy
        sLD, sST, sPE, sVE, sAC = self.sLD, self.sST, self.sPE, self.sVE, self.sAC

        with ExitStack() as actx:
            self.bo_b = self.sb(actx, "bo_b", [1, D], BF16)
            kvctx = ExitStack()
            negm_mat = self.sb(kvctx, "negm_mat", [128, 128], F32)
            K_one = self.sb(kvctx, "K_one", [128, NP * T], BF16)
            V_aug = self.sb(kvctx, "V_aug", [128, NT * NH * 65], BF16)
            a1outer = ExitStack()
            self.wqkb = self.sb(a1outer, "wqkb", [128, 2 * D], BF16)
            self.bqkv_b = self.sb(a1outer, "bqkv_b", [1, 3 * D], BF16)
            rctx = ExitStack()
            r_fm = self.sb(rctx, "r_fm", [32, T], BF16)
            with ExitStack() as wctx:
                stg1 = self.sb(wctx, "stg1a", [1, 3 * D], F32)
                ps_pre = self.psum(wctx, "ps_prea", [128, 512], F32)
                sLD.inc(sy.dma_start(out=stg1[0:1, 0:2 * D], in_=self.WQKROW))
                sLD.wait(te)
                self.sVE.wait(te)
                for nb in range(4):
                    if nb:
                        self.sVE.wait(te)
                    sPE.inc(te.matmul(ps_pre[:], self.ones_f[0:1, :],
                                      stg1[0:1, nb * 512:(nb + 1) * 512],
                                      start=True, stop=True))
                    sPE.wait(ve)
                    sVE.inc(ve.tensor_copy(out=self.wqkb[:, nb * 512:(nb + 1) * 512],
                                           in_=ps_pre[:]))
                sVE.wait(sy)
                sLD.inc(sy.dma_start(out=stg1[0:1, :], in_=self.BQKV))
                sLD.wait(ve)
                sVE.inc(ve.tensor_copy(out=self.bqkv_b[:], in_=stg1[0:1, :]))
                sVE.wait(sy)
                sLD.inc(sy.dma_start(out=stg1[0:1, 0:D], in_=self.BO))
                sLD.wait(ve)
                sVE.inc(ve.tensor_copy(out=self.bo_b[:], in_=stg1[0:1, 0:D]))
                sVE.wait(sy)
                sVE.wait(te)

            with ExitStack() as a1:
                xls = self.sb(a1, "xls", [128, 2, D], F32)
                xlnb = self.sb(a1, "xlnb", [128, D], BF16)
                xlf_t = self.sb(a1, "xlf_t", [128, D], BF16)
                wst = self.sb(a1, "wst", [128, 2048], F32)
                q2sb = wst
                s2h = self.sb(a1, "s2h", [128, 32], F32)
                rstdh = self.sb(a1, "rstdh", [128, 32], F32)
                qn1 = self.sb(a1, "qn1", [128, 512], F32)
                qkn = self.sb(a1, "qkn", [128, 2048], BF16)
                rtm = self.sb(a1, "rtm", [128, 32], F32)
                qstg = self.sb(a1, "qstg", [128, 2, D], BF16)
                ps_t = self.psum(a1, "ps_t", [128, 1024], BF16)
                ps_qk = self.psum(a1, "ps_qk", [128, 2048], F32)
                ps_qt = self.psum(a1, "ps_qt", [128, 1024], BF16)
                ps_kt = self.psum(a1, "ps_kt", [128, 1024], BF16)
                ps_r = self.psum(a1, "ps_r", [32, 128], F32)

                with ExitStack() as a1a:
                    wqk_b = self.sb(a1a, "wqk_b", [128, NCH * 2048], BF16)
                    wqk_bv = wqk_b.ap().rearrange("p (c n) -> p c n", c=NCH)
                    cast_done = []
                    for c in range(NCH):
                        if c >= 1:
                            sy.wait_ge(sVE.h, cast_done[c - 1])
                        sLD.inc(sy.dma_start(out=wst[:],
                                             in_=self.WQKVT[c * 128:(c + 1) * 128, 0:2048]))
                        sLD.wait(ve)
                        sVE.inc(ve.tensor_copy(out=wqk_bv[:, c, :], in_=wst[:]))
                        cast_done.append(sVE.n)

                    xld = []
                    sLD.inc(sy.dma_start(out=xls[:, 0, :], in_=self.X[0:128, :]))
                    xld.append(sLD.n)
                    sLD.inc(sy.dma_start(out=xls[:, 1, :], in_=self.X[128:256, :]))
                    xld.append(sLD.n)

                    sn_x = [0] * NT
                    sn_t = [0] * NT
                    sn_qk_ve = [0] * NT
                    sn_qk_ac = [0] * NT
                    sn_r = [0] * NT
                    sn_qt = [0] * NT
                    sn_qst = [0] * NT
                    sn_xtr_pe = [0] * NT
                    sn_qtr_pe = [0] * NT

                    for tt in range(NT):
                        xbuf = xls[:, tt % 2, :]
                        sLD.wait(ve, at=xld[tt])
                        if tt >= 1:
                            ve.wait_ge(sPE.h, sn_xtr_pe[tt - 1])
                        self.emit_ln(xbuf, xlnb[:], self.c_eps_ln[:, 0:1])
                        sn_x[tt] = self.sVE.n
                        if tt + 2 < NT:
                            sy.wait_ge(sVE.h, sn_x[tt])
                            sLD.inc(sy.dma_start(out=xls[:, tt % 2, :],
                                                 in_=self.X[(tt + 2) * 128:(tt + 3) * 128, :]))
                            xld.append(sLD.n)
                        if tt >= 1:
                            te.wait_ge(sVE.h, sn_t[tt - 1])
                        sVE.wait(te)
                        for c in range(NCH):
                            ins = te.transpose(ps_t[:, c * 128:(c + 1) * 128],
                                               xlnb[:, c * 128:(c + 1) * 128],
                                               self.ident_b[:])
                        sPE.inc(ins)
                        sn_xtr_pe[tt] = sPE.n
                        sPE.wait(ve)
                        sVE.inc(ve.tensor_copy(out=xlf_t[:], in_=ps_t[:]))
                        sn_t[tt] = sVE.n
                        xlf_ready = sVE.n

                        for nb in range(4):
                            te.wait_ge(sVE.h, xlf_ready)
                            if tt >= 1:
                                te.wait_ge(sVE.h, sn_qk_ve[tt - 1])
                                te.wait_ge(sAC.h, sn_qk_ac[tt - 1])
                            if nb == 0 and tt >= 1:
                                ve.wait_ge(sPE.h, sn_qtr_pe[tt - 1])
                            bank = ps_qk[:, nb * 512:(nb + 1) * 512]
                            for c in range(NCH):
                                te.matmul(bank, xlf_t[:, c * 128:(c + 1) * 128],
                                          wqk_bv[:, c, nb * 512:(nb + 1) * 512],
                                          start=(c == 0), stop=False)
                            sPE.inc(te.matmul(bank, self.ones_b[0:1, 0:128],
                                              self.bqkv_b[0:1, nb * 512:(nb + 1) * 512],
                                              start=False, stop=True))
                            sPE.wait(se)
                            sAC.inc(se.activation(out=q2sb[:, nb * 512:(nb + 1) * 512],
                                                  in_=bank, func=AF.Square))
                            sAC.wait(ve)
                            sVE.inc(ve.tensor_reduce(
                                out=s2h[:, nb * 8:(nb + 1) * 8],
                                in_=q2sb.ap().rearrange("p (h d) -> p h d", d=DH)[:, nb * 8:(nb + 1) * 8, :],
                                op=ALU.add, axis=AX.X))
                            sVE.wait(se)
                            sAC.inc(se.activation(out=rstdh[:, nb * 8:(nb + 1) * 8],
                                          in_=s2h[:, nb * 8:(nb + 1) * 8], func=AF.Ln,
                                          bias=self.c_eps_rms[:, 0:1], scale=1.0 / DH))
                            sAC.wait(se)
                            sAC.inc(se.activation(out=rstdh[:, nb * 8:(nb + 1) * 8],
                                                  in_=rstdh[:, nb * 8:(nb + 1) * 8],
                                                  func=AF.Exp, scale=-0.5))
                            sAC.wait(ve)
                            ve.tensor_tensor(
                                out=qn1.ap().rearrange("p (h d) -> p h d", d=DH),
                                in0=ps_qk.ap().rearrange("p (h d) -> p h d", d=DH)[:, nb * 8:(nb + 1) * 8, :],
                                in1=rstdh[:, nb * 8:(nb + 1) * 8].unsqueeze(2).to_broadcast([128, 8, DH]),
                                op=ALU.mult)
                            sVE.inc(ve.tensor_tensor(
                                out=qkn[:, nb * 512:(nb + 1) * 512], in0=qn1[:],
                                in1=self.wqkb[:, nb * 512:(nb + 1) * 512], op=ALU.mult))
                        sn_qk_ve[tt] = sVE.n
                        sn_qk_ac[tt] = sAC.n

                        sVE.wait(se)
                        sAC.inc(se.activation(out=q2sb[:], in_=qkn[:], func=AF.Square))
                        sAC.wait(ve)
                        sVE.inc(ve.tensor_reduce(
                            out=rtm[:],
                            in_=q2sb.ap().rearrange("p (h d) -> p h d", d=DH),
                            op=ALU.add, axis=AX.X))
                        sVE.wait(se)
                        sAC.inc(se.activation(out=rtm[:], in_=rtm[:], func=AF.Ln,
                                      bias=self.c_eps_rms[:, 0:1], scale=1.0))
                        sAC.wait(se)
                        sAC.inc(se.activation(out=rtm[:], in_=rtm[:], func=AF.Exp,
                                              scale=0.5))
                        if tt >= 1:
                            te.wait_ge(sVE.h, sn_r[tt - 1])
                        sAC.wait(te)
                        sPE.inc(te.transpose(ps_r[:], rtm[:, 0:32], self.ident_f[:]))
                        sPE.wait(ve)
                        sVE.inc(ve.tensor_copy(out=r_fm[:, tt * 128:(tt + 1) * 128],
                                               in_=ps_r[:]))
                        sn_r[tt] = sVE.n

                        if tt >= 1:
                            te.wait_ge(sVE.h, sn_qt[tt - 1])
                        for p in range(NP):
                            te.transpose(ps_qt[:, p * 128:(p + 1) * 128],
                                         qkn[:, p * 128:(p + 1) * 128], self.ident_b[:])
                            ins = te.transpose(ps_kt[:, p * 128:(p + 1) * 128],
                                               qkn[:, 1024 + p * 128:1024 + (p + 1) * 128],
                                               self.ident_b[:])
                        sPE.inc(ins)
                        sn_qtr_pe[tt] = sPE.n
                        sPE.wait(ve)
                        if tt >= 2:
                            ve.wait_ge(sST.h, sn_qst[tt - 2])
                        ve.tensor_copy(out=qstg[:, tt % 2, :], in_=ps_qt[:])
                        sVE.inc(ve.tensor_copy(
                            out=K_one.ap().rearrange("p (c t) -> p c t", c=NP)[:, :, tt * 128:(tt + 1) * 128],
                            in_=ps_kt.ap().rearrange("p (c t) -> p c t", c=NP)))
                        sn_qt[tt] = sVE.n
                        sVE.wait(sy)
                        sST.inc(sy.dma_start(out=self.QS[tt, :, :],
                                             in_=qstg.ap()[:, tt % 2, :]))
                        sn_qst[tt] = sST.n
                    sVE.wait(sy)
                    sVE.wait(te)
                    sAC.wait(te)

                with ExitStack() as a1b:
                    wv_b = self.sb(a1b, "wv_b", [128, NCH * 1024], BF16)
                    wv_bv = wv_b.ap().rearrange("p (c n) -> p c n", c=NCH)
                    ps_v = ps_qk
                    cast_done = []
                    for c in range(NCH):
                        if c >= 1:
                            sy.wait_ge(sVE.h, cast_done[c - 1])
                        sLD.inc(sy.dma_start(out=wst[:, 0:1024],
                                             in_=self.WQKVT[c * 128:(c + 1) * 128, 2048:3072]))
                        sLD.wait(ve)
                        sVE.inc(ve.tensor_copy(out=wv_bv[:, c, :], in_=wst[:, 0:1024]))
                        cast_done.append(sVE.n)
                    sVE.inc(ve.memset(V_aug[:], 1.0))
                    vones_ready = sVE.n

                    xld = []
                    sLD.inc(sy.dma_start(out=xls[:, 0, :], in_=self.X[0:128, :]))
                    xld.append(sLD.n)
                    sLD.inc(sy.dma_start(out=xls[:, 1, :], in_=self.X[128:256, :]))
                    xld.append(sLD.n)
                    sn_x = [0] * NT
                    sn_t = [0] * NT
                    sn_v_ac = [0] * NT
                    for tt in range(NT):
                        xbuf = xls[:, tt % 2, :]
                        sLD.wait(ve, at=xld[tt])
                        self.emit_ln(xbuf, xlnb[:], self.c_eps_ln[:, 0:1])
                        sn_x[tt] = sVE.n
                        if tt + 2 < NT:
                            sy.wait_ge(sVE.h, sn_x[tt])
                            sLD.inc(sy.dma_start(out=xls[:, tt % 2, :],
                                                 in_=self.X[(tt + 2) * 128:(tt + 3) * 128, :]))
                            xld.append(sLD.n)
                        if tt >= 1:
                            te.wait_ge(sVE.h, sn_t[tt - 1])
                        sVE.wait(te)
                        for c in range(NCH):
                            ins = te.transpose(ps_t[:, c * 128:(c + 1) * 128],
                                               xlnb[:, c * 128:(c + 1) * 128],
                                               self.ident_b[:])
                        sPE.inc(ins)
                        sPE.wait(ve)
                        sVE.inc(ve.tensor_copy(out=xlf_t[:], in_=ps_t[:]))
                        sn_t[tt] = sVE.n
                        xlf_ready = sVE.n
                        for nb in range(2):
                            te.wait_ge(sVE.h, xlf_ready)
                            if tt == 0:
                                te.wait_ge(sVE.h, vones_ready)
                            if tt >= 1:
                                te.wait_ge(sVE.h, sn_v_ac[tt - 1])
                            bank = ps_v[:, nb * 512:(nb + 1) * 512]
                            for c in range(NCH):
                                te.matmul(bank, xlf_t[:, c * 128:(c + 1) * 128],
                                          wv_bv[:, c, nb * 512:(nb + 1) * 512],
                                          start=(c == 0), stop=False)
                            sPE.inc(te.matmul(bank, self.ones_b[0:1, 0:128],
                                              self.bqkv_b[0:1, 2048 + nb * 512:2048 + (nb + 1) * 512],
                                              start=False, stop=True))
                            sPE.wait(ve)
                            sVE.inc(ve.tensor_copy(
                                out=V_aug.ap().rearrange("p (t h e) -> p t h e", t=NT, h=NH)[:, tt, nb * 8:(nb + 1) * 8, 0:64],
                                in_=bank.rearrange("p (h e) -> p h e", e=DH)))
                        sn_v_ac[tt] = sVE.n
                    sVE.wait(sy)
                    sVE.wait(te)
                    sAC.wait(te)

            if self.taps:
                self.tap("t_kone", K_one.ap())
                self.tap("t_vaug", V_aug.ap())
                self.tap("t_rfm", r_fm.ap())
                self.tap("t_qs", self.QS)

            with ExitStack() as nmctx:
                rk_col = self.sb(nmctx, "rk_col", [32, 1], F32)
                rqb32 = self.sb(nmctx, "rqb32", [32, 8], F32)
                rqbT = self.sb(nmctx, "rqbT", [8, 32], F32)
                rkT = self.sb(nmctx, "rkT", [1, 32], F32)
                rq_row = self.sb(nmctx, "rq_row", [1, 256], F32)
                negm_row = self.sb(nmctx, "negm_row", [1, 128], F32)
                ps_nm = self.psum(nmctx, "ps_nm", [128, 512], F32)
                sVE.inc(ve.tensor_reduce(out=rk_col[:], in_=r_fm.ap(),
                                         op=ALU.max, axis=AX.X))
                sVE.inc(ve.tensor_reduce(
                    out=rqb32[:],
                    in_=r_fm.ap().rearrange("h (q t) -> h q t", q=8),
                    op=ALU.max, axis=AX.X))
                sVE.wait(te)
                te.transpose(ps_nm[0:8, 0:32], rqb32[:], self.ident_f[0:32, 0:32])
                sPE.inc(te.transpose(ps_nm[0:1, 32:64], rk_col[:],
                                     self.ident_f[0:32, 0:32]))
                sPE.wait(ve)
                ve.tensor_copy(out=rqbT[:], in_=ps_nm[0:8, 0:32])
                sVE.inc(ve.tensor_copy(out=rkT[:], in_=ps_nm[0:1, 32:64]))
                sVE.wait(sy)
                sST.inc(sy.dma_start(out=self.MSC, in_=rqbT[:]))
                sST.wait(sy)
                sLD.inc(sy.dma_start(out=rq_row[:],
                                     in_=self.MSC.rearrange("a b -> (a b)").unsqueeze(0)))
                sLD.wait(ve)
                sVE.inc(ve.scalar_tensor_tensor(
                    out=negm_row.ap().rearrange("a (q h) -> a q h", h=16),
                    in0=rq_row.ap().rearrange("a (q h) -> a q h", h=32)[:, :, 0:16],
                    scalar=-0.5,
                    in1=rkT.ap()[0:1, 16:32].unsqueeze(1).to_broadcast([1, 8, 16]),
                    op0=ALU.mult, op1=ALU.mult))
                sVE.wait(te)
                sPE.inc(te.matmul(ps_nm[:, 0:128], self.ones_f[0:1, :],
                                  negm_row[:], start=True, stop=True))
                sPE.wait(ve)
                sVE.inc(ve.tensor_copy(out=negm_mat[:], in_=ps_nm[:, 0:128]))
                self.tap("t_negm", negm_mat.ap())
                sVE.wait(te)
                sVE.wait(sy)
            rctx.close()
            a1outer.close()

            with ExitStack() as a2:
                P = self.sb(a2, "P", [128, 2 * NT * 512], BF16)
                Qb = self.sb(a2, "Qb", [128, 2, 512], BF16)
                rl = self.sb(a2, "rl", [1, 512], F32)
                bc_sb = self.sb(a2, "bc_sb", [64, 512], F32)
                zstg = self.sb(a2, "zstg", [128, 2, 512], BF16)
                ps_sA = self.psum(a2, "ps_sA", [128, 1536], F32)
                ps_sB = self.psum(a2, "ps_sB", [128, 1536], F32)
                ps_z = self.psum(a2, "ps_z", [128, 512], F32)
                ps_m = self.psum(a2, "ps_m", [128, 512], F32)

                Vv = V_aug.ap().rearrange("p (t h e) -> p t h e", t=NT, h=NH)
                Kv = K_one.ap().rearrange("p (c t) -> p c t", c=NP)

                groups = [(g * 3, min(3, NT - g * 3)) for g in range((NT + 2) // 3)]
                sn_grp = {0: 0, 1: 0}
                sn_z_ve = 0
                sn_bc_ac = 0
                sn_exp_done = {}
                sn_S_pe = {}
                sn_q = {}
                sn_zst = {}
                qbs = range(NTB) if A2_QBS is None else A2_QBS
                heads = [(qb, pc, hh) for qb in qbs for pc in range(NP)
                         for hh in range(2)]
                Pv = P.ap().rearrange("p (b n) -> p b n", b=2)

                def emit_S(i):
                    nonlocal sn_z_ve
                    qb, pc, hh = heads[i]
                    pair = i // 2
                    base = 64 * hh
                    if hh == 0:
                        if pair >= 2:
                            sy.wait_ge(sPE.h, sn_S_pe.get(pair - 2, 0))
                        sLD.inc(sy.dma_start(
                            out=Qb.ap()[:, pc % 2, :].rearrange("p (t d) -> p t d", t=4),
                            in_=self.QS.rearrange("t p d -> p t d")[:, qb * 4:(qb + 1) * 4, pc * 128:(pc + 1) * 128]))
                        sn_q[pair] = sLD.n
                    bias_ap = negm_mat[:, qb * 16 + 2 * pc + hh:qb * 16 + 2 * pc + hh + 1]
                    sLD.wait(te, at=sn_q[pair])
                    for gi, (k0, glen) in enumerate(groups):
                        pss = ps_sA if gi % 2 == 0 else ps_sB
                        te.wait_ge(sAC.h, sn_grp[gi % 2])
                        for kk in range(glen):
                            kt = k0 + kk
                            ins = te.matmul(
                                pss[:, kk * 512:(kk + 1) * 512],
                                Kv[base:base + 64, pc, kt * 128:(kt + 1) * 128],
                                Qb[base:base + 64, pc % 2, :],
                                start=True, stop=True)
                        sPE.inc(ins)
                        sPE.wait(se)
                        sAC.inc(se.activation(
                            out=Pv[:, i % 2, k0 * 512:(k0 + glen) * 512],
                            in_=pss[:, 0:glen * 512],
                            func=AF.Exp, bias=bias_ap, scale=1.0))
                        sn_grp[gi % 2] = sAC.n
                    sn_exp_done[i] = sAC.n
                    sn_S_pe[pair] = sPE.n

                def emit_AV(i):
                    nonlocal sn_z_ve, sn_bc_ac
                    qb, pc, hh = heads[i]
                    h = 2 * pc + hh
                    base = 64 * hh
                    te.wait_ge(sVE.h, sn_z_ve)
                    te.wait_ge(sAC.h, sn_exp_done[i])
                    for kt in range(NT):
                        ins = te.matmul(ps_z[0:65, :],
                                        Vv[:, kt, h, :],
                                        Pv[:, i % 2, kt * 512:(kt + 1) * 512],
                                        start=(kt == 0), stop=(kt == NT - 1))
                    sPE.inc(ins)
                    sPE.wait(ve)
                    sVE.inc(ve.tensor_scalar(out=rl[:], in0=ps_z[64:65, :],
                                             scalar1=1e-30, scalar2=None, op0=ALU.add))
                    sVE.wait(ve)
                    sVE.inc(ve.reciprocal(out=rl[:], in_=rl[:]))
                    sn_rl = sVE.n
                    te.wait_ge(sVE.h, sn_rl)
                    te.wait_ge(sAC.h, sn_bc_ac)
                    sPE.inc(te.matmul(ps_m[0:64, :], self.ones_f[0:1, 0:64],
                                      rl[:], start=True, stop=True))
                    sPE.wait(se)
                    se.wait_ge(sVE.h, sn_z_ve)
                    sAC.inc(se.activation(out=bc_sb[:], in_=ps_m[0:64, :],
                                          func=AF.Identity))
                    sn_bc_ac = sAC.n
                    sAC.wait(ve)
                    pair = i // 2
                    if pair >= 2:
                        ve.wait_ge(sST.h, sn_zst.get(pair - 2, 0))
                    sVE.inc(ve.tensor_tensor(out=zstg[base:base + 64, pc % 2, :],
                                             in0=ps_z[0:64, :], in1=bc_sb[:],
                                             op=ALU.mult))
                    sn_z_ve = sVE.n
                    if hh == 1:
                        sVE.wait(sy)
                        sST.inc(sy.dma_start(
                            out=self.ZS[pc, :, qb * 512:(qb + 1) * 512],
                            in_=zstg.ap()[:, pc % 2, :]))
                        sn_zst[pair] = sST.n

                for i in range(len(heads)):
                    emit_S(i)
                    if i > 0:
                        emit_AV(i - 1)
                emit_AV(len(heads) - 1)
                sVE.wait(te)
                sAC.wait(te)
                sVE.wait(sy)
                self.tap("t_p", P.ap())
                self.tap("t_qb", Qb.ap())

            if self.taps:
                self.tap("t_zs", self.ZS)
            kvctx.close()
            with ExitStack() as a3:
                wo_b = self.sb(a3, "wo_b", [128, NCH * 1024], BF16)
                wo_bv = wo_b.ap().rearrange("p (c n) -> p c n", c=NCH)
                wstA3 = self.sb(a3, "wstA3", [128, 1024], F32)
                zbuf = self.sb(a3, "zbuf", [128, 2, NP, 512], BF16)
                xrt = self.sb(a3, "xrt", [128, 4, D], F32)
                ofm = self.sb(a3, "ofm", [128, 4096], BF16)
                xstg = self.sb(a3, "xstg", [128, 2, D], F32)
                ps_o = self.psum(a3, "ps_o", [128, 2048], F32)
                ps_T = self.psum(a3, "ps_T", [128, 1024], BF16)

                cast_done = []
                for c in range(NCH):
                    if c >= 1:
                        sy.wait_ge(sVE.h, cast_done[c - 1])
                    sLD.inc(sy.dma_start(out=wstA3[:],
                                         in_=self.WOT[c * 128:(c + 1) * 128, :]))
                    sLD.wait(ve)
                    sVE.inc(ve.tensor_copy(out=wo_bv[:, c, :], in_=wstA3[:]))
                    cast_done.append(sVE.n)

                zld = []
                for pc in range(NP):
                    sLD.inc(sy.dma_start(out=zbuf[:, 0, pc, :],
                                         in_=self.ZS[pc, :, 0:512]))
                zld.append(sLD.n)
                sn_o_ac = [0] * NTB
                sn_T_ve = [0] * NTB
                xst_list = []
                for tb in range(NTB):
                    if tb + 1 < NTB:
                        if tb >= 1:
                            sy.wait_ge(sPE.h, self.sPE.n)
                        for pc in range(NP):
                            sLD.inc(sy.dma_start(
                                out=zbuf[:, (tb + 1) % 2, pc, :],
                                in_=self.ZS[pc, :, (tb + 1) * 512:(tb + 2) * 512]))
                        zld.append(sLD.n)
                    if tb >= 1:
                        sy.wait_ge(sVE.h, sn_T_ve[tb - 1])
                    for ts4 in range(4):
                        sLD.inc(sy.dma_start(
                            out=xrt[:, ts4, :],
                            in_=self.X[tb * 512 + ts4 * 128:tb * 512 + (ts4 + 1) * 128, :]))
                    xrld = sLD.n
                    for half in range(2):
                        sLD.wait(te, at=zld[tb])
                        if tb >= 1 or half >= 1:
                            te.wait_ge(sAC.h, sn_o_ac[tb - 1] if half == 0 else self.sAC.n)
                        for oc4 in range(4):
                            oc = half * 4 + oc4
                            bank = ps_o[:, oc4 * 512:(oc4 + 1) * 512]
                            for zc in range(NP):
                                te.matmul(bank, wo_bv[:, zc, oc * 128:(oc + 1) * 128],
                                          zbuf[:, tb % 2, zc, :],
                                          start=(zc == 0), stop=False)
                            sPE.inc(te.matmul(bank, self.bo_b[0:1, oc * 128:(oc + 1) * 128],
                                              self.ones_b[0:1, :],
                                              start=False, stop=True))
                            sPE.wait(se)
                            sAC.inc(se.activation(out=ofm[:, oc * 512:(oc + 1) * 512],
                                                  in_=bank, func=AF.Identity))
                        sn_o_ac[tb] = sAC.n
                    for ts4 in range(4):
                        if tb >= 1 and ts4 == 0:
                            te.wait_ge(sVE.h, sn_T_ve[tb - 1])
                        if ts4 >= 1:
                            te.wait_ge(sVE.h, self.sVE.n)
                        sAC.wait(te)
                        for oc in range(NCH):
                            ins = te.transpose(ps_T[:, oc * 128:(oc + 1) * 128],
                                               ofm[:, oc * 512 + ts4 * 128:oc * 512 + (ts4 + 1) * 128],
                                               self.ident_b[:])
                        sPE.inc(ins)
                        sPE.wait(ve)
                        sLD.wait(ve, at=xrld)
                        if len(xst_list) >= 2:
                            ve.wait_ge(sST.h, xst_list[-2])
                        sVE.inc(ve.tensor_tensor(out=xstg[:, ts4 % 2, :], in0=ps_T[:],
                                                 in1=xrt[:, ts4, :], op=ALU.add))
                        sn_T_ve[tb] = sVE.n
                        sVE.wait(sy)
                        r0 = tb * 512 + ts4 * 128
                        sST.inc(sy.dma_start(out=self.XR[r0:r0 + 128, :],
                                             in_=xstg.ap()[:, ts4 % 2, :]))
                        xst_list.append(sST.n)
                sVE.wait(te)
                sAC.wait(te)
                sVE.wait(sy)
                sVE.wait(ge)

    def emit_ln_pass(self, src_dram, xlf, fstage, xlnb, ps_t):
        te, ve, se, sy = self.te, self.ve, self.se, self.sy
        sLD, sPE, sVE, sAC = self.sLD, self.sPE, self.sVE, self.sAC
        xlf_v = xlf.ap().rearrange("p (c t) -> p c t", c=NCH)
        xld = []
        sVE.wait(sy)
        self.sST.wait(sy)
        sLD.inc(sy.dma_start(out=fstage[:, 0, :], in_=src_dram[0:128, :]))
        xld.append(sLD.n)
        sLD.inc(sy.dma_start(out=fstage[:, 1, :], in_=src_dram[128:256, :]))
        xld.append(sLD.n)
        sn_x = [0] * NT
        sn_t = [0] * NT
        sn_pe = [0] * NT
        for tt in range(NT):
            xbuf = fstage[:, tt % 2, :]
            sLD.wait(ve, at=xld[tt])
            if tt >= 1:
                ve.wait_ge(sPE.h, sn_pe[tt - 1])
            self.emit_ln(xbuf, xlnb[:], self.c_eps_ln[:, 0:1])
            sn_x[tt] = sVE.n
            if tt + 2 < NT:
                sy.wait_ge(sVE.h, sn_x[tt])
                sLD.inc(sy.dma_start(out=fstage[:, tt % 2, :],
                                     in_=src_dram[(tt + 2) * 128:(tt + 3) * 128, :]))
                xld.append(sLD.n)
            if tt >= 1:
                te.wait_ge(sVE.h, sn_t[tt - 1])
            sVE.wait(te)
            for c in range(NCH):
                ins = te.transpose(ps_t[:, c * 128:(c + 1) * 128],
                                   xlnb[:, c * 128:(c + 1) * 128], self.ident_b[:])
            sPE.inc(ins)
            sn_pe[tt] = sPE.n
            sPE.wait(ve)
            sVE.inc(ve.tensor_copy(
                out=xlf_v[:, :, tt * 128:(tt + 1) * 128],
                in_=ps_t.rearrange("p (c t) -> p c t", c=NCH)))
            sn_t[tt] = sVE.n

    def phase_ffn(self, src_dram, dst_dram, first_call):
        sfx = "a" if first_call else "b"
        te, ve, se, ge, sy = self.te, self.ve, self.se, self.ge, self.sy
        sLD, sST, sPE, sVE, sAC = self.sLD, self.sST, self.sPE, self.sVE, self.sAC
        NCK = NE * 4

        with ExitStack() as f:
            xlf = self.sb(f, "xlf" + sfx, [128, NCH * T], BF16)
            act_all = self.sb(f, "act_all" + sfx, [128, NCK * 512], BF16)
            w13b = self.sb(f, "w13b" + sfx, [128, 2, 2 * NCH * 512], BF16)
            w2b = self.sb(f, "w2b" + sfx, [128, 6, D], BF16)
            sbc = self.sb(f, "sbc" + sfx, [128, 512], F32)
            s_t = self.sb(f, "s_t" + sfx, [16, 128], BF16)
            s_row = self.sb(f, "s_row" + sfx, [1, 2, 512], BF16)
            fstage = self.sb(f, "fstage" + sfx, [128, 4, D], F32)
            xlnb = self.sb(f, "xlnb2" + sfx, [128, D], BF16)
            gw_b = self.sb(f, "gw_b" + sfx, [128, NCH * 16], BF16)
            rsc = self.sb(f, "rsc" + sfx, [128, 64], F32)
            ofm = self.sb(f, "ofm2" + sfx, [128, 4096], BF16)
            ps_g = self.psum(f, "ps_g" + sfx, [128, 2048], F32)
            ps_u = self.psum(f, "ps_u" + sfx, [128, 2048], F32)
            ps_t = ps_g.ap()[:, 0:512].bitcast(BF16)
            ps_o_banks = ([ps_g.ap()[:, i * 512:(i + 1) * 512] for i in range(4)]
                          + [ps_u.ap()[:, i * 512:(i + 1) * 512] for i in range(4)])

            if first_call:
                stgv = act_all.ap().bitcast(F32)
                stg_s = [stgv[:, 0:4096], stgv[:, 4096:8192]]
                wsb_s = [act_all.ap()[:, 16384:20480], act_all.ap()[:, 20480:24576]]
                w1s = self.W1.rearrange("e c p i -> e p c i")
                w3s = self.W3.rearrange("e c p i -> e p c i")
                jobs = []
                for e in range(NE):
                    jobs.append((w1s[e], self.W13B[e][:, 0:4096], 4096, True))
                    jobs.append((w3s[e], self.W13B[e][:, 4096:8192], 4096, True))
                for ck in range(NCK):
                    jobs.append((self.W2[ck], self.W2B[ck], 1024, False))
                sn_cast = {}
                sn_store = {}
                for j, (src, dst, wdt, chunked) in enumerate(jobs):
                    sl = j % 2
                    out_ap = stg_s[sl][:, 0:wdt]
                    if chunked:
                        out_ap = out_ap.rearrange("p (c i) -> p c i", c=NCH)
                    if j >= 2:
                        sy.wait_ge(sVE.h, sn_cast[j - 2])
                    sLD.inc(sy.dma_start(out=out_ap, in_=src))
                    ld = sLD.n
                    sLD.wait(ve, at=ld)
                    if j >= 2:
                        ve.wait_ge(sST.h, sn_store[j - 2])
                    sVE.inc(ve.tensor_copy(out=wsb_s[sl][:, 0:wdt],
                                           in_=stg_s[sl][:, 0:wdt]))
                    sn_cast[j] = sVE.n
                    sy.wait_ge(sVE.h, sn_cast[j])
                    sST.inc(sy.dma_start(out=dst, in_=wsb_s[sl][:, 0:wdt]))
                    sn_store[j] = sST.n
                sST.wait(sy)
                sST.wait(ve)
                sLD.wait(ve)

            self.emit_ln_pass(src_dram, xlf, fstage, xlnb,
                              ps_t=ps_u.ap()[:, 0:512].bitcast(BF16))
            xlf_v = xlf.ap().rearrange("p (c t) -> p c t", c=NCH)

            with ExitStack() as g:
                gstg = self.sb(g, "gstg" + sfx, [128, NCH * 16], F32)
                sVE.wait(sy)
                sLD.inc(sy.dma_start(
                    out=gstg.ap().rearrange("p (c n) -> p c n", c=NCH),
                    in_=self.GWT.rearrange("(c p) n -> p c n", p=128)))
                sLD.wait(ve)
                sVE.inc(ve.tensor_copy(out=gw_b[:], in_=gstg[:]))
            gw_v = gw_b.ap().rearrange("p (c n) -> p c n", c=NCH)

            ps_rt = ps_g.ap()[:, 0:512]
            lg = rsc[:, 0:16]
            t8 = rsc[:, 16:24]
            neg = rsc[:, 24:25]
            thr = rsc[:, 25:26]
            s4 = rsc[:, 26:27]
            rcp = rsc[:, 27:28]
            es = rsc[:, 32:48]
            sct = rsc[:, 48:64]
            sn_rt = 0
            sn_st1 = sn_st2 = 0
            for tt in range(NT):
                te.wait_ge(sVE.h, self.sVE.n)
                for c in range(NCH):
                    ins = te.matmul(ps_rt[:, 0:16], xlf_v[:, c, tt * 128:(tt + 1) * 128],
                                    gw_v[:, c, :], start=(c == 0), stop=(c == NCH - 1))
                sPE.inc(ins)
                sPE.wait(ve)
                sVE.inc(ve.tensor_copy(out=lg, in_=ps_rt[:, 0:16]))
                sVE.wait(ve)
                sVE.inc(ve.max(out=t8, in_=lg))
                sVE.wait(ve)
                sVE.inc(ve.tensor_scalar_mul(neg, t8[:, 0:1], -1.0))
                sVE.wait(se)
                sAC.inc(se.activation(out=es, in_=lg, func=AF.Exp, bias=neg[:, 0:1],
                                      scale=1.0))
                sAC.wait(se)
                sAC.inc(se.activation(out=thr, in_=t8[:, 3:4], func=AF.Exp,
                                      bias=neg[:, 0:1], scale=1.0))
                sAC.wait(ve)
                sVE.inc(ve.tensor_scalar(out=sct, in0=es, scalar1=thr[:, 0:1],
                                         scalar2=None, op0=ALU.is_ge))
                sVE.wait(ve)
                sVE.inc(ve.tensor_tensor(out=es, in0=es, in1=sct, op=ALU.mult))
                sVE.wait(ve)
                sVE.inc(ve.tensor_reduce(out=s4, in_=es, op=ALU.add, axis=AX.X))
                sVE.wait(ve)
                sVE.inc(ve.reciprocal(out=rcp, in_=s4))
                sVE.wait(ve)
                sVE.inc(ve.tensor_scalar(out=sct, in0=es, scalar1=rcp[:, 0:1],
                                         scalar2=None, op0=ALU.mult))
                te.wait_ge(sVE.h, self.sVE.n)
                if tt >= 1:
                    te.wait_ge(sVE.h, sn_rt)
                if tt >= 2:
                    ve.wait_ge(sST.h, sn_st2)
                sPE.inc(te.transpose(ps_rt[0:16, 16:144], sct, self.ident_f[:]))
                sPE.wait(ve)
                sVE.inc(ve.tensor_copy(out=s_t[:], in_=ps_rt[0:16, 16:144]))
                sn_rt = sVE.n
                sVE.wait(sy)
                sST.inc(sy.dma_start(out=self.SFD[:, tt * 128:(tt + 1) * 128],
                                     in_=s_t[:]))
                sn_st2 = sn_st1
                sn_st1 = sST.n

            w13v = w13b.ap().rearrange("p b (h c n) -> p b h c n", h=2, c=NCH)
            w1v = w13v[:, :, 0]
            w3v = w13v[:, :, 1]
            sn_w13 = {}
            sn_bc_pe = {}
            sn_gu_pe = {}
            sn_act_ve = {}
            sn_silu_ac = {}
            sn_w2 = {}
            sn_w2_pe = {}
            sn_o_ac = 0
            sn_T_ve = 0
            sn_xst = []
            for gi0 in (0, 1):
                sLD.inc(sy.dma_start(out=w13b.ap()[:, gi0, :],
                                     in_=self.W13B[gi0 % NE]))
                sn_w13[gi0] = sLD.n

            for tb in range(NTB):
                sn_sbc_prev = 0
                for e in range(NE):
                    gi = tb * NE + e
                    if gi >= 2:
                        sy.wait_ge(sPE.h, sn_bc_pe.get(gi - 2, 0))
                    sLD.inc(sy.dma_start(out=s_row.ap()[:, gi % 2, :],
                                         in_=self.SFD[e:e + 1, tb * 512:(tb + 1) * 512]))
                    sn_srow = sLD.n
                    te.wait_ge(sAC.h, max(sn_silu_ac.get(gi - 1, 0), sn_o_ac))
                    te.wait_ge(sVE.h, sn_sbc_prev)
                    sLD.wait(te, at=sn_srow)
                    sPE.inc(te.matmul(ps_g.ap()[:, 0:512], self.ones_b[0:1, 0:128],
                                      s_row.ap()[:, gi % 2, :],
                                      start=True, stop=True))
                    sn_bc_pe[gi] = sPE.n
                    sPE.wait(ve)
                    sVE.inc(ve.tensor_copy(out=sbc[:], in_=ps_g.ap()[:, 0:512]))
                    sn_sbc = sn_sbc_prev = sVE.n

                    te.wait_ge(sLD.h, sn_w13[gi])
                    te.wait_ge(sVE.h, sn_sbc)
                    te.wait_ge(sVE.h, sn_act_ve.get(gi - 1, 0))
                    te.wait_ge(sAC.h, max(sn_silu_ac.get(gi - 1, 0), sn_o_ac))
                    for ic in range(4):
                        gbank = ps_g.ap()[:, ic * 512:(ic + 1) * 512]
                        ubank = ps_u.ap()[:, ic * 512:(ic + 1) * 512]
                        for c in range(NCH):
                            te.matmul(gbank, w1v[:, gi % 2, c, ic * 128:(ic + 1) * 128],
                                      xlf_v[:, c, tb * 512:(tb + 1) * 512],
                                      start=(c == 0), stop=(c == NCH - 1))
                        for c in range(NCH):
                            ins = te.matmul(ubank, w3v[:, gi % 2, c, ic * 128:(ic + 1) * 128],
                                            xlf_v[:, c, tb * 512:(tb + 1) * 512],
                                            start=(c == 0), stop=(c == NCH - 1))
                    sPE.inc(ins)
                    sn_gu_pe[gi] = sPE.n
                    ni = gi + 2
                    if ni < NTB * NE:
                        sy.wait_ge(sPE.h, sn_gu_pe[gi])
                        sLD.inc(sy.dma_start(out=w13b.ap()[:, ni % 2, :],
                                             in_=self.W13B[ni % NE]))
                        sn_w13[ni] = sLD.n
                    sPE.wait(se)
                    sAC.inc(se.activation(out=ofm[:, 0:2048], in_=ps_g.ap()[:, 0:2048],
                                          func=AF.Silu))
                    sn_silu_ac[gi] = sAC.n
                    sPE.wait(ve)
                    sVE.inc(ve.tensor_tensor(
                        out=ofm.ap()[:, 2048:4096].rearrange("p (i n) -> p i n", i=4),
                        in0=ps_u.ap().rearrange("p (i n) -> p i n", i=4),
                        in1=sbc.ap().unsqueeze(1).to_broadcast([128, 4, 512]),
                        op=ALU.mult))
                    sAC.wait(ve)
                    sVE.inc(ve.tensor_tensor(
                        out=act_all[:, e * 2048:(e + 1) * 2048],
                        in0=ofm[:, 0:2048], in1=ofm[:, 2048:4096], op=ALU.mult))
                    sn_act_ve[gi] = sVE.n

                if tb >= 1:
                    sy.wait_ge(sPE.h, self.sPE.n)
                for pp in range(3):
                    sLD.inc(sy.dma_start(
                        out=w2b[:, 2 * pp:2 * pp + 2, :],
                        in_=self.W2B[2 * pp:2 * pp + 2].rearrange("k p d -> p k d")))
                    sn_w2[pp] = sLD.n
                for ck in range(NCK):
                    te.wait_ge(sLD.h, sn_w2[ck // 2])
                    if ck == 0:
                        te.wait_ge(sVE.h, sn_act_ve[tb * NE + NE - 1])
                        te.wait_ge(sAC.h, sn_silu_ac[tb * NE + NE - 1])
                        te.wait_ge(sVE.h, sn_sbc_prev)
                    for oc in range(NCH):
                        ins = te.matmul(ps_o_banks[oc],
                                        w2b[:, ck % 6, oc * 128:(oc + 1) * 128],
                                        act_all[:, ck * 512:(ck + 1) * 512],
                                        start=(ck == 0), stop=(ck == NCK - 1))
                    sPE.inc(ins)
                    sn_w2_pe[ck] = sPE.n
                    if ck % 2 == 1:
                        npair = ck // 2 + 3
                        if 2 * npair < NCK:
                            s0 = (2 * npair) % 6
                            sy.wait_ge(sPE.h, sn_w2_pe[ck])
                            sLD.inc(sy.dma_start(
                                out=w2b[:, s0:s0 + 2, :],
                                in_=self.W2B[2 * npair:2 * npair + 2]
                                .rearrange("k p d -> p k d")))
                            sn_w2[npair] = sLD.n
                sPE.wait(se)
                for oc in range(NCH):
                    ins = se.activation(out=ofm[:, oc * 512:(oc + 1) * 512],
                                        in_=ps_o_banks[oc], func=AF.Identity)
                sAC.inc(ins)
                sn_o_ac = self.sAC.n
                for ts4 in range(4):
                    r0 = tb * 512 + ts4 * 128
                    sy.wait_ge(sVE.h, sn_T_ve)
                    sLD.inc(sy.dma_start(out=fstage[:, ts4 % 2, :],
                                         in_=src_dram[r0:r0 + 128, :]))
                    xrld = sLD.n
                    sAC.wait(te)
                    te.wait_ge(sVE.h, sn_T_ve)
                    for oc in range(NCH):
                        ins = te.transpose(ps_t[:, oc * 128:(oc + 1) * 128],
                                           ofm[:, oc * 512 + ts4 * 128:oc * 512 + (ts4 + 1) * 128],
                                           self.ident_b[:])
                    sPE.inc(ins)
                    sPE.wait(ve)
                    sLD.wait(ve, at=xrld)
                    if len(sn_xst) >= 2:
                        ve.wait_ge(sST.h, sn_xst[-2])
                    sVE.inc(ve.tensor_tensor(out=fstage[:, 2 + ts4 % 2, :],
                                             in0=ps_t, in1=fstage[:, ts4 % 2, :],
                                             op=ALU.add))
                    sn_T_ve = sVE.n
                    sVE.wait(sy)
                    sST.inc(sy.dma_start(out=dst_dram[r0:r0 + 128, :],
                                         in_=fstage.ap()[:, 2 + ts4 % 2, :]))
                    sn_xst.append(sST.n)
            sVE.wait(te)
            sAC.wait(te)
            sVE.wait(sy)
            sVE.wait(ge)
            self.tap("t_sfd", self.SFD)
            self.tap("t_act", act_all.ap())
            self.tap("t_xlf", xlf.ap())
            self.tap("t_w13b", self.W13B)
            self.tap("t_w2b", self.W2B)

    def phase_b(self, src_dram, dst_dram):
        te, ve, se, ge, sy = self.te, self.ve, self.se, self.ge, self.sy
        sLD, sST, sPE, sVE, sAC = self.sLD, self.sST, self.sPE, self.sVE, self.sAC

        with ExitStack() as b:
            xlf = self.sb(b, "xlf3", [128, NCH * T], BF16)
            B1 = self.sb(b, "B1", [128, T], F32)
            B2 = self.sb(b, "B2", [128, T], F32)
            B3 = self.sb(b, "B3", [128, T], F32)
            B4 = self.sb(b, "B4", [128, T], F32)
            fstage = self.sb(b, "fstage3", [128, 4, D], F32)
            xlnb = self.sb(b, "xlnb3", [128, D], BF16)
            wg_b = self.sb(b, "wg_b", [128, 2, 128], BF16)
            wgstg = self.sb(b, "wgstg", [128, 128], F32)
            bcol = self.sb(b, "bcol", [128, 2], F32)
            hstage = self.sb(b, "hstage", [128, 2, 512], F32)
            xq = self.sb(b, "xq", [128, 2, 512], F32)
            ps_gx = self.psum(b, "ps_gx", [128, 2048], F32)
            ps_ht = self.psum(b, "ps_ht", [128, 512], F32)

            self.emit_ln_pass(src_dram, xlf, fstage, xlnb,
                              ps_t=ps_gx.ap()[:, 0:512].bitcast(BF16))
            xlf_v = xlf.ap().rearrange("p (c t) -> p c t", c=NCH)

            for c in range(NCH):
                for i, (WSRC, BSRC) in enumerate(((self.IGW, self.IGB),
                                                  (self.AGW, self.AGB))):
                    sVE.wait(sy)
                    sLD.inc(sy.dma_start(out=wgstg[:], in_=WSRC[c]))
                    sLD.inc(sy.dma_start(out=bcol[:, i:i + 1],
                                         in_=BSRC[c:c + 1, :].rearrange("a b -> b a")))
                    sLD.wait(ve)
                    sVE.inc(ve.tensor_copy(out=wg_b[:, i, :], in_=wgstg[:]))
                sn_wg = sVE.n
                for i, dstbuf in ((0, B1), (1, B2)):
                    for tb in range(NTB):
                        bank = ps_gx.ap()[:, (tb % 2) * 512:(tb % 2 + 1) * 512]
                        te.wait_ge(sVE.h, sn_wg)
                        te.wait_ge(sAC.h, self.sAC.n)
                        sPE.inc(te.matmul(bank, wg_b[:, i, :],
                                          xlf_v[:, c, tb * 512:(tb + 1) * 512],
                                          start=True, stop=True))
                        sPE.wait(se)
                        sAC.inc(se.activation(out=dstbuf[:, tb * 512:(tb + 1) * 512],
                                              in_=bank, func=AF.Sigmoid,
                                              bias=bcol[:, i:i + 1], scale=1.0))
                sAC.wait(ve)
                sVE.inc(ve.tensor_scalar(out=B2[:], in0=B2[:],
                                         scalar1=self.sp8[:, c:c + 1], scalar2=None,
                                         op0=ALU.mult))
                sVE.inc(ve.tensor_tensor_scan(out=B3[:], data0=B2[:], data1=B2[:],
                                              initial=0.0, op0=ALU.add, op1=ALU.bypass))
                sVE.inc(ve.tensor_scalar(out=B3[:], in0=B3[:], scalar1=LCLIP,
                                         scalar2=None, op0=ALU.max))
                ve.memset(B4[:, 0:1], 0.0)
                ve.tensor_copy(out=B4[:, 1:2], in_=B3[:, 0:1])
                sVE.inc(ve.tensor_tensor(out=B4[:, 2:T], in0=B3[:, 1:T - 1],
                                         in1=B3[:, 0:T - 2], op=ALU.subtract))
                sVE.wait(se)
                sAC.inc(se.activation(out=B4[:], in_=B4[:], func=AF.Exp))
                sAC.inc(se.activation(out=B2[:], in_=B2[:], func=AF.Exp, scale=2.0))
                sAC.wait(ve)
                ve.tensor_scalar(out=B2[:], in0=B2[:], scalar1=-1.0, scalar2=1.0,
                                 op0=ALU.mult, op1=ALU.add)
                sVE.inc(ve.tensor_scalar(out=B2[:], in0=B2[:], scalar1=1e-7,
                                         scalar2=None, op0=ALU.max))
                sVE.wait(se)
                se.activation(out=B2[:], in_=B2[:], func=AF.Ln)
                sAC.inc(se.activation(out=B2[:], in_=B2[:], func=AF.Exp, scale=0.5))
                sAC.wait(ve)
                ve.tensor_tensor(out=B1[:], in0=xlf_v[:, c, :], in1=B1[:], op=ALU.mult)
                sVE.inc(ve.tensor_tensor(out=B1[:], in0=B1[:], in1=B2[:], op=ALU.mult))
                sVE.inc(ve.tensor_tensor_scan(out=B3[:], data0=B4[:], data1=B1[:],
                                              initial=0.0, op0=ALU.mult, op1=ALU.add))
                sn_hst = []
                sn_hT = 0
                xqld = {}
                for q4 in range(NCH):
                    r0 = q4 * 512
                    dcol = dst_dram[r0:r0 + 512, c * 128:(c + 1) * 128] \
                        .rearrange("(q p) d -> p q d", p=128)
                    if q4 >= 2:
                        sy.wait_ge(sVE.h, sn_hT)
                    sLD.inc(sy.dma_start(out=xq.ap()[:, q4 % 2, :]
                                         .rearrange("p (q d) -> p q d", q=4),
                                         in_=dcol))
                    xqld[q4] = sLD.n
                    if q4 >= 1:
                        te.wait_ge(sVE.h, sn_hT)
                    sVE.wait(te)
                    for j in range(4):
                        tt = q4 * 4 + j
                        ins = te.transpose(ps_ht[:, j * 128:(j + 1) * 128],
                                           B3[:, tt * 128:(tt + 1) * 128],
                                           self.ident_f[:])
                    sPE.inc(ins)
                    sPE.wait(ve)
                    sLD.wait(ve, at=xqld[q4])
                    if len(sn_hst) >= 2:
                        ve.wait_ge(sST.h, sn_hst[-2])
                    sVE.inc(ve.tensor_tensor(out=hstage.ap()[:, q4 % 2, :],
                                             in0=ps_ht.ap(),
                                             in1=xq.ap()[:, q4 % 2, :], op=ALU.add))
                    sn_hT = sVE.n
                    sVE.wait(sy)
                    sST.inc(sy.dma_start(
                        out=dcol,
                        in_=hstage.ap()[:, q4 % 2, :].rearrange("p (q d) -> p q d", q=4)))
                    sn_hst.append(sST.n)
                sST.wait(sy)
            sVE.wait(te)
            sAC.wait(te)
            sVE.wait(sy)
            self.tap("t_B1", B1.ap())
            self.tap("t_B2", B2.ap())
            self.tap("t_B3", B3.ap())
            self.tap("t_B4", B4.ap())
            self.tap("t_xlf3", xlf.ap())

    def finish(self, copy_xr_to_out=False):
        ge, sy, ve = self.ge, self.sy, self.ve
        if copy_xr_to_out:
            self.sST.wait(sy)
            self.sST.inc(sy.dma_start(out=self.OUT, in_=self.XR))
        self.sST.wait(sy)
        self.sLD.wait(sy)
        self.sLG.wait(sy)
        self.sST.wait(ge)

    def build(self):
        self.prelude()
        if "A" in PHASES:
            self.phase_a()
        first = True
        if "F1" in PHASES:
            self.phase_ffn(self.XR, self.XR, first_call=first)
            first = False
        if "B" in PHASES:
            self.phase_b(self.XR, self.XR)
        if "F2" in PHASES:
            last = ("A" in PHASES and "F1" in PHASES and "B" in PHASES)
            self.phase_ffn(self.XR, self.OUT if last else self.XR,
                           first_call=first)
            first = False
            done_full = last
        else:
            done_full = False
        self.finish(copy_xr_to_out=not done_full)
        self.ctx.close()
        return self.nc


def _shard_inputs(inputs):
    f = np.ascontiguousarray
    x = inputs["L0_attn_a_x"]
    wq = np.asarray(inputs["L0_attn_a_wq"])
    wk = np.asarray(inputs["L0_attn_a_wk"])
    common = {
        "wqkvT": f(np.asarray(inputs["L0_attn_a_Wqkv"]).T),
        "bqkv": f(np.asarray(inputs["L0_attn_a_bqkv"]).reshape(1, 3 * D)),
        "woT": f(np.asarray(inputs["L0_attn_a_Wo"]).T),
        "bo": f(np.asarray(inputs["L0_attn_a_bo"]).reshape(1, D)),
        "wqkrow": f(np.concatenate([np.tile(wq, NH), np.tile(wk, NH)]).reshape(1, 2 * D)),
        "igw": f(np.asarray(inputs["L0_attn_b_ig_w"])),
        "igb": f(np.asarray(inputs["L0_attn_b_ig_b"])),
        "agw": f(np.asarray(inputs["L0_attn_b_ag_w"])),
        "agb": f(np.asarray(inputs["L0_attn_b_ag_b"])),
        "apar": f(np.asarray(inputs["L0_attn_b_a_param"]).reshape(NCH, 128)),
        "gwT": f(np.asarray(inputs["L0_ffn_gate_w"]).T),
        "w1": f(np.asarray(inputs["L0_ffn_w1"]).reshape(NE, NCH, 128, DI)),
        "w3": f(np.asarray(inputs["L0_ffn_w3"]).reshape(NE, NCH, 128, DI)),
        "w2": f(np.asarray(inputs["L0_ffn_w2"]).reshape(NE * 4, 128, D)),
    }
    in_maps = []
    for i in range(8):
        m = dict(common)
        m["x"] = f(np.asarray(x[i]))
        in_maps.append(m)
    return in_maps


def kernel(**inputs):
    in_maps = _shard_inputs(inputs)
    nc = Builder().build()
    res = run_bass_kernel_spmd(nc, in_maps, core_ids=list(range(8)))
    out = np.stack([np.asarray(r["out"]) for r in res.results], axis=0)
    return out.astype(np.float32)


# version 07
# speedup vs baseline: 1.0947x; 1.0086x over previous
import numpy as np
import concourse.bass as bass
import concourse.mybir as mybir
from concourse.bass_utils import run_bass_kernel_spmd
from contextlib import ExitStack

F32 = mybir.dt.float32
BF16 = mybir.dt.bfloat16
AF = mybir.ActivationFunctionType
ALU = mybir.AluOpType
AX = mybir.AxisListType

T = 4096
D = 1024
NT = 32
NH = 16
DH = 64
NP = 8
NE = 16
DI = 512
NTB = 8
NCH = 8
LCLIP = float(np.log(np.float32(1e-7)))

PHASES = ("A", "F1", "B", "F2")
A2_QBS = None
DEV_TAPS = {}


class Sem:
    def __init__(self, ctx, nc, name, step=1):
        self.h = ctx.enter_context(nc.semaphore(name))
        self.n = 0
        self.step = step

    def inc(self, ins, k=1):
        ins.then_inc(self.h, k * self.step)
        self.n += k * self.step
        return self.n

    def wait(self, eng, at=None):
        eng.wait_ge(self.h, self.n if at is None else at)


class Builder:
    def __init__(self):
        nc = bass.Bass("TRN2", target_bir_lowering=False, dynamic_dma_scratch_size=8192)
        self.nc = nc
        self.te, self.ve, self.se, self.ge, self.sy = (
            nc.tensor, nc.vector, nc.scalar, nc.gpsimd, nc.sync)
        self.ctx = ExitStack()
        self.sLD = Sem(self.ctx, nc, "sLD", 16)
        self.sLG = Sem(self.ctx, nc, "sLG", 16)
        self.sST = Sem(self.ctx, nc, "sST", 16)
        self.sPE = Sem(self.ctx, nc, "sPE")
        self.sVE = Sem(self.ctx, nc, "sVE")
        self.sAC = Sem(self.ctx, nc, "sAC")
        self.sGE = Sem(self.ctx, nc, "sGE")
        self._dram()
        self.taps = {}
        for name, (shape, dt) in DEV_TAPS.items():
            self.taps[name] = nc.dram_tensor(name, list(shape), dt,
                                             kind="ExternalOutput").ap()

    def sb(self, ctx, name, shape, dtype):
        return ctx.enter_context(self.nc.sbuf_tensor(name, shape, dtype))

    def psum(self, ctx, name, shape, dtype):
        return ctx.enter_context(self.nc.psum_tensor(name, shape, dtype))

    def _dram(self):
        nc = self.nc
        self.X = nc.dram_tensor("x", [T, D], F32, kind="ExternalInput").ap()
        self.WQKVT = nc.dram_tensor("wqkvT", [D, 3 * D], F32, kind="ExternalInput").ap()
        self.BQKV = nc.dram_tensor("bqkv", [1, 3 * D], F32, kind="ExternalInput").ap()
        self.WOT = nc.dram_tensor("woT", [D, D], F32, kind="ExternalInput").ap()
        self.BO = nc.dram_tensor("bo", [1, D], F32, kind="ExternalInput").ap()
        self.WQKROW = nc.dram_tensor("wqkrow", [1, 2 * D], F32, kind="ExternalInput").ap()
        self.IGW = nc.dram_tensor("igw", [NCH, 128, 128], F32, kind="ExternalInput").ap()
        self.IGB = nc.dram_tensor("igb", [NCH, 128], F32, kind="ExternalInput").ap()
        self.AGW = nc.dram_tensor("agw", [NCH, 128, 128], F32, kind="ExternalInput").ap()
        self.AGB = nc.dram_tensor("agb", [NCH, 128], F32, kind="ExternalInput").ap()
        self.APAR = nc.dram_tensor("apar", [NCH, 128], F32, kind="ExternalInput").ap()
        self.GWT = nc.dram_tensor("gwT", [D, NE], F32, kind="ExternalInput").ap()
        self.W13 = nc.dram_tensor("w13", [NE, 128, 2 * NCH * DI], F32, kind="ExternalInput").ap()
        self.W2 = nc.dram_tensor("w2", [NE * 4, 128, D], F32, kind="ExternalInput").ap()
        self.OUT = nc.dram_tensor("out", [T, D], F32, kind="ExternalOutput").ap()
        self.XR = nc.dram_tensor("xres", [T, D], F32, kind="Internal").ap()
        self.QS = nc.dram_tensor("qsc", [NT, 128, D], BF16, kind="Internal").ap()
        self.ZS = nc.dram_tensor("zsc", [NP, 128, T], BF16, kind="Internal").ap()
        self.W13B = nc.dram_tensor("w13bd", [NE, 128, 2 * NCH * DI], BF16, kind="Internal").ap()
        self.W2B = nc.dram_tensor("w2bd", [NE * 4, 128, D], BF16, kind="Internal").ap()
        self.MSC = nc.dram_tensor("msc", [8, 32], F32, kind="Internal").ap()
        self.SFD = nc.dram_tensor("sfd", [NE, T], BF16, kind="Internal").ap()

    def tap(self, name, src_ap):
        if name in self.taps:
            self.sVE.wait(self.sy)
            self.sAC.wait(self.sy)
            self.sPE.wait(self.sy)
            self.sST.inc(self.sy.dma_start(out=self.taps[name], in_=src_ap))

    def prelude(self):
        ctx, nc = self.ctx, self.nc
        te, ve, se, ge, sy = self.te, self.ve, self.se, self.ge, self.sy
        sLD, sVE, sAC, sPE, sGE = self.sLD, self.sVE, self.sAC, self.sPE, self.sGE

        self.ident_b = self.sb(ctx, "ident_b", [128, 128], BF16)
        self.ident_f = self.sb(ctx, "ident_f", [128, 128], F32)
        self.ones_b = self.sb(ctx, "ones_b", [1, 512], BF16)
        self.ones_f = self.sb(ctx, "ones_f", [1, 128], F32)
        self.c_eps_ln = self.sb(ctx, "c_eps_ln", [128, 1], F32)
        self.c_eps_rms = self.sb(ctx, "c_eps_rms", [128, 1], F32)
        self.c_one = self.sb(ctx, "c_one", [128, 1], F32)
        self.sp8 = self.sb(ctx, "sp8", [128, NCH], F32)
        self.lnst = self.sb(ctx, "lnst", [128, 12], F32)
        self.lnmv = self.sb(ctx, "lnmv", [128, 2], F32)
        self.lnr = self.sb(ctx, "lnr", [128, 2], F32)

        with ExitStack() as pctx:
            iscr = self.sb(pctx, "iscr", [128, 128], F32)
            stg1 = self.sb(pctx, "stg1", [1, 3 * D], F32)
            stg2 = self.sb(pctx, "stg2", [128, NCH], F32)
            sp_t = self.sb(pctx, "sp_t", [128, NCH], F32)
            ps_pre = self.psum(pctx, "ps_pre", [128, 512], F32)

            sGE.inc(ge.iota(out=iscr[:], pattern=[[-1, 128]], base=0,
                            channel_multiplier=1,
                            allow_small_or_imprecise_dtypes=True))
            ve.memset(self.ones_b[:], 1.0)
            ve.memset(self.ones_f[:], 1.0)
            ve.memset(self.c_eps_ln[:], 1e-5)
            ve.memset(self.c_eps_rms[:], 1e-6)
            sVE.inc(ve.memset(self.c_one[:], 1.0))
            sGE.wait(ve)
            ve.tensor_scalar(out=self.ident_b[:], in0=iscr[:], scalar1=0.0,
                             scalar2=None, op0=ALU.is_equal)
            sVE.inc(ve.tensor_scalar(out=self.ident_f[:], in0=iscr[:],
                                     scalar1=0.0, scalar2=None, op0=ALU.is_equal))

            for c in range(NCH):
                sLD.inc(sy.dma_start(
                    out=stg2[:, c:c + 1],
                    in_=self.APAR[c:c + 1, :].rearrange("a b -> b a")))
            sLD.wait(se)
            sAC.inc(se.activation(out=sp_t[:], in_=stg2[:], func=AF.Abs))
            sAC.wait(se)
            sAC.inc(se.activation(out=sp_t[:], in_=sp_t[:], func=AF.Exp, scale=-1.0))
            sAC.wait(se)
            sVE.wait(se)
            sAC.inc(se.activation(out=sp_t[:], in_=sp_t[:], func=AF.Ln,
                                  bias=self.c_one[:, 0:1], scale=1.0))
            sLD.wait(ve)
            sAC.wait(ve)
            sVE.inc(ve.tensor_scalar(out=self.sp8[:], in0=stg2[:], scalar1=0.0,
                             scalar2=None, op0=ALU.max))
            sVE.wait(ve)
            sVE.inc(ve.tensor_tensor(out=self.sp8[:], in0=self.sp8[:], in1=sp_t[:], op=ALU.add))
            sVE.wait(ve)
            sVE.inc(ve.tensor_scalar(out=self.sp8[:], in0=self.sp8[:],
                                     scalar1=-8.0, scalar2=None, op0=ALU.mult))
            sVE.wait(te)
            sVE.wait(ge)
            sVE.wait(sy)

    def emit_ln(self, xbuf, out_bf, eps_ap):
        ve, se = self.ve, self.se
        sVE, sAC = self.sVE, self.sAC
        ve.bn_stats(out=self.lnst[:, 0:6], in_=xbuf[:, 0:512])
        sVE.inc(ve.bn_stats(out=self.lnst[:, 6:12], in_=xbuf[:, 512:1024]))
        sVE.wait(ve)
        sVE.inc(ve.bn_aggr(out=self.lnmv[:], in_=self.lnst[:]))
        sVE.wait(ve)
        sVE.wait(se)
        sAC.inc(se.activation(out=self.lnr[:, 0:1], in_=self.lnmv[:, 1:2], func=AF.Ln,
                      bias=eps_ap, scale=1.0))
        sAC.wait(se)
        sAC.inc(se.activation(out=self.lnr[:, 0:1], in_=self.lnr[:, 0:1],
                              func=AF.Exp, scale=-0.5))
        sAC.wait(ve)
        sVE.inc(ve.scalar_tensor_tensor(out=self.lnr[:, 1:2], in0=self.lnmv[:, 0:1],
                                        scalar=-1.0, in1=self.lnr[:, 0:1],
                                        op0=ALU.mult, op1=ALU.mult))
        sVE.wait(ve)
        sVE.inc(ve.tensor_scalar(out=out_bf, in0=xbuf,
                                 scalar1=self.lnr[:, 0:1], scalar2=self.lnr[:, 1:2],
                                 op0=ALU.mult, op1=ALU.add))

    def phase_a(self):
        ctx = self.ctx
        nc = self.nc
        te, ve, se, ge, sy = self.te, self.ve, self.se, self.ge, self.sy
        sLD, sST, sPE, sVE, sAC = self.sLD, self.sST, self.sPE, self.sVE, self.sAC

        with ExitStack() as actx:
            self.bo_b = self.sb(actx, "bo_b", [1, D], BF16)
            kvctx = ExitStack()
            negm_mat = self.sb(kvctx, "negm_mat", [128, 128], F32)
            K_one = self.sb(kvctx, "K_one", [128, NP * T], BF16)
            V_aug = self.sb(kvctx, "V_aug", [128, NT * NH * 65], BF16)
            a1outer = ExitStack()
            self.wqkb = self.sb(a1outer, "wqkb", [128, 2 * D], BF16)
            self.bqkv_b = self.sb(a1outer, "bqkv_b", [1, 3 * D], BF16)
            rctx = ExitStack()
            r_fm = self.sb(rctx, "r_fm", [32, T], BF16)
            with ExitStack() as wctx:
                stg1 = self.sb(wctx, "stg1a", [1, 3 * D], F32)
                ps_pre = self.psum(wctx, "ps_prea", [128, 512], F32)
                sLD.inc(sy.dma_start(out=stg1[0:1, 0:2 * D], in_=self.WQKROW))
                sLD.wait(te)
                self.sVE.wait(te)
                for nb in range(4):
                    if nb:
                        self.sVE.wait(te)
                    sPE.inc(te.matmul(ps_pre[:], self.ones_f[0:1, :],
                                      stg1[0:1, nb * 512:(nb + 1) * 512],
                                      start=True, stop=True))
                    sPE.wait(ve)
                    sVE.inc(ve.tensor_copy(out=self.wqkb[:, nb * 512:(nb + 1) * 512],
                                           in_=ps_pre[:]))
                sVE.wait(sy)
                sLD.inc(sy.dma_start(out=stg1[0:1, :], in_=self.BQKV))
                sLD.wait(ve)
                sVE.inc(ve.tensor_copy(out=self.bqkv_b[:], in_=stg1[0:1, :]))
                sVE.wait(sy)
                sLD.inc(sy.dma_start(out=stg1[0:1, 0:D], in_=self.BO))
                sLD.wait(ve)
                sVE.inc(ve.tensor_copy(out=self.bo_b[:], in_=stg1[0:1, 0:D]))
                sVE.wait(sy)
                sVE.wait(te)

            with ExitStack() as a1:
                xls = self.sb(a1, "xls", [128, 2, D], F32)
                xlnb = self.sb(a1, "xlnb", [128, D], BF16)
                xlf_t = self.sb(a1, "xlf_t", [128, D], BF16)
                wst = self.sb(a1, "wst", [128, 2048], F32)
                q2sb = wst
                s2h = self.sb(a1, "s2h", [128, 32], F32)
                rstdh = self.sb(a1, "rstdh", [128, 32], F32)
                qn1 = self.sb(a1, "qn1", [128, 512], F32)
                qkn = self.sb(a1, "qkn", [128, 2048], BF16)
                rtm = self.sb(a1, "rtm", [128, 32], F32)
                qstg = self.sb(a1, "qstg", [128, 2, D], BF16)
                ps_t = self.psum(a1, "ps_t", [128, 1024], BF16)
                ps_qk = self.psum(a1, "ps_qk", [128, 2048], F32)
                ps_qt = self.psum(a1, "ps_qt", [128, 1024], BF16)
                ps_kt = self.psum(a1, "ps_kt", [128, 1024], BF16)
                ps_r = self.psum(a1, "ps_r", [32, 128], F32)

                with ExitStack() as a1a:
                    wqk_b = self.sb(a1a, "wqk_b", [128, NCH * 2048], BF16)
                    wqk_bv = wqk_b.ap().rearrange("p (c n) -> p c n", c=NCH)
                    cast_done = []
                    for c in range(NCH):
                        if c >= 1:
                            sy.wait_ge(sVE.h, cast_done[c - 1])
                        sLD.inc(sy.dma_start(out=wst[:],
                                             in_=self.WQKVT[c * 128:(c + 1) * 128, 0:2048]))
                        sLD.wait(ve)
                        sVE.inc(ve.tensor_copy(out=wqk_bv[:, c, :], in_=wst[:]))
                        cast_done.append(sVE.n)

                    xld = []
                    sLD.inc(sy.dma_start(out=xls[:, 0, :], in_=self.X[0:128, :]))
                    xld.append(sLD.n)
                    sLD.inc(sy.dma_start(out=xls[:, 1, :], in_=self.X[128:256, :]))
                    xld.append(sLD.n)

                    sn_x = [0] * NT
                    sn_t = [0] * NT
                    sn_qk_ve = [0] * NT
                    sn_qk_ac = [0] * NT
                    sn_r = [0] * NT
                    sn_qt = [0] * NT
                    sn_qst = [0] * NT
                    sn_xtr_pe = [0] * NT
                    sn_qtr_pe = [0] * NT

                    for tt in range(NT):
                        xbuf = xls[:, tt % 2, :]
                        sLD.wait(ve, at=xld[tt])
                        if tt >= 1:
                            ve.wait_ge(sPE.h, sn_xtr_pe[tt - 1])
                        self.emit_ln(xbuf, xlnb[:], self.c_eps_ln[:, 0:1])
                        sn_x[tt] = self.sVE.n
                        if tt + 2 < NT:
                            sy.wait_ge(sVE.h, sn_x[tt])
                            sLD.inc(sy.dma_start(out=xls[:, tt % 2, :],
                                                 in_=self.X[(tt + 2) * 128:(tt + 3) * 128, :]))
                            xld.append(sLD.n)
                        if tt >= 1:
                            te.wait_ge(sVE.h, sn_t[tt - 1])
                        sVE.wait(te)
                        for c in range(NCH):
                            ins = te.transpose(ps_t[:, c * 128:(c + 1) * 128],
                                               xlnb[:, c * 128:(c + 1) * 128],
                                               self.ident_b[:])
                        sPE.inc(ins)
                        sn_xtr_pe[tt] = sPE.n
                        sPE.wait(ve)
                        sVE.inc(ve.tensor_copy(out=xlf_t[:], in_=ps_t[:]))
                        sn_t[tt] = sVE.n
                        xlf_ready = sVE.n

                        for nb in range(4):
                            te.wait_ge(sVE.h, xlf_ready)
                            if tt >= 1:
                                te.wait_ge(sVE.h, sn_qk_ve[tt - 1])
                                te.wait_ge(sAC.h, sn_qk_ac[tt - 1])
                            if nb == 0 and tt >= 1:
                                ve.wait_ge(sPE.h, sn_qtr_pe[tt - 1])
                            bank = ps_qk[:, nb * 512:(nb + 1) * 512]
                            for c in range(NCH):
                                te.matmul(bank, xlf_t[:, c * 128:(c + 1) * 128],
                                          wqk_bv[:, c, nb * 512:(nb + 1) * 512],
                                          start=(c == 0), stop=False)
                            sPE.inc(te.matmul(bank, self.ones_b[0:1, 0:128],
                                              self.bqkv_b[0:1, nb * 512:(nb + 1) * 512],
                                              start=False, stop=True))
                            sPE.wait(se)
                            sAC.inc(se.activation(out=q2sb[:, nb * 512:(nb + 1) * 512],
                                                  in_=bank, func=AF.Square))
                            sAC.wait(ve)
                            sVE.inc(ve.tensor_reduce(
                                out=s2h[:, nb * 8:(nb + 1) * 8],
                                in_=q2sb.ap().rearrange("p (h d) -> p h d", d=DH)[:, nb * 8:(nb + 1) * 8, :],
                                op=ALU.add, axis=AX.X))
                            sVE.wait(se)
                            sAC.inc(se.activation(out=rstdh[:, nb * 8:(nb + 1) * 8],
                                          in_=s2h[:, nb * 8:(nb + 1) * 8], func=AF.Ln,
                                          bias=self.c_eps_rms[:, 0:1], scale=1.0 / DH))
                            sAC.wait(se)
                            sAC.inc(se.activation(out=rstdh[:, nb * 8:(nb + 1) * 8],
                                                  in_=rstdh[:, nb * 8:(nb + 1) * 8],
                                                  func=AF.Exp, scale=-0.5))
                            sAC.wait(ve)
                            ve.tensor_tensor(
                                out=qn1.ap().rearrange("p (h d) -> p h d", d=DH),
                                in0=ps_qk.ap().rearrange("p (h d) -> p h d", d=DH)[:, nb * 8:(nb + 1) * 8, :],
                                in1=rstdh[:, nb * 8:(nb + 1) * 8].unsqueeze(2).to_broadcast([128, 8, DH]),
                                op=ALU.mult)
                            sVE.inc(ve.tensor_tensor(
                                out=qkn[:, nb * 512:(nb + 1) * 512], in0=qn1[:],
                                in1=self.wqkb[:, nb * 512:(nb + 1) * 512], op=ALU.mult))
                        sn_qk_ve[tt] = sVE.n
                        sn_qk_ac[tt] = sAC.n

                        sVE.wait(se)
                        sAC.inc(se.activation(out=q2sb[:], in_=qkn[:], func=AF.Square))
                        sAC.wait(ve)
                        sVE.inc(ve.tensor_reduce(
                            out=rtm[:],
                            in_=q2sb.ap().rearrange("p (h d) -> p h d", d=DH),
                            op=ALU.add, axis=AX.X))
                        sVE.wait(se)
                        sAC.inc(se.activation(out=rtm[:], in_=rtm[:], func=AF.Ln,
                                      bias=self.c_eps_rms[:, 0:1], scale=1.0))
                        sAC.wait(se)
                        sAC.inc(se.activation(out=rtm[:], in_=rtm[:], func=AF.Exp,
                                              scale=0.5))
                        if tt >= 1:
                            te.wait_ge(sVE.h, sn_r[tt - 1])
                        sAC.wait(te)
                        sPE.inc(te.transpose(ps_r[:], rtm[:, 0:32], self.ident_f[:]))
                        sPE.wait(ve)
                        sVE.inc(ve.tensor_copy(out=r_fm[:, tt * 128:(tt + 1) * 128],
                                               in_=ps_r[:]))
                        sn_r[tt] = sVE.n

                        if tt >= 1:
                            te.wait_ge(sVE.h, sn_qt[tt - 1])
                        for p in range(NP):
                            te.transpose(ps_qt[:, p * 128:(p + 1) * 128],
                                         qkn[:, p * 128:(p + 1) * 128], self.ident_b[:])
                            ins = te.transpose(ps_kt[:, p * 128:(p + 1) * 128],
                                               qkn[:, 1024 + p * 128:1024 + (p + 1) * 128],
                                               self.ident_b[:])
                        sPE.inc(ins)
                        sn_qtr_pe[tt] = sPE.n
                        sPE.wait(ve)
                        if tt >= 2:
                            ve.wait_ge(sST.h, sn_qst[tt - 2])
                        ve.tensor_copy(out=qstg[:, tt % 2, :], in_=ps_qt[:])
                        sVE.inc(ve.tensor_copy(
                            out=K_one.ap().rearrange("p (c t) -> p c t", c=NP)[:, :, tt * 128:(tt + 1) * 128],
                            in_=ps_kt.ap().rearrange("p (c t) -> p c t", c=NP)))
                        sn_qt[tt] = sVE.n
                        sVE.wait(sy)
                        sST.inc(sy.dma_start(out=self.QS[tt, :, :],
                                             in_=qstg.ap()[:, tt % 2, :]))
                        sn_qst[tt] = sST.n
                    sVE.wait(sy)
                    sVE.wait(te)
                    sAC.wait(te)

                with ExitStack() as a1b:
                    wv_b = self.sb(a1b, "wv_b", [128, NCH * 1024], BF16)
                    wv_bv = wv_b.ap().rearrange("p (c n) -> p c n", c=NCH)
                    ps_v = ps_qk
                    cast_done = []
                    for c in range(NCH):
                        if c >= 1:
                            sy.wait_ge(sVE.h, cast_done[c - 1])
                        sLD.inc(sy.dma_start(out=wst[:, 0:1024],
                                             in_=self.WQKVT[c * 128:(c + 1) * 128, 2048:3072]))
                        sLD.wait(ve)
                        sVE.inc(ve.tensor_copy(out=wv_bv[:, c, :], in_=wst[:, 0:1024]))
                        cast_done.append(sVE.n)
                    sVE.inc(ve.memset(V_aug[:], 1.0))
                    vones_ready = sVE.n

                    NCK = NE * 4
                    pslot = [wst.ap()[:, 0:512], wst.ap()[:, 512:1024],
                             wst.ap()[:, 1024:1536], wst.ap()[:, 1536:2048]]
                    pwq = qkn.ap()
                    pjobs = []
                    for ee in range(NE):
                        for k8 in range(8):
                            pjobs.append((self.W13[ee][:, k8 * 1024:(k8 + 1) * 1024],
                                          self.W13B[ee][:, k8 * 1024:(k8 + 1) * 1024]))
                    for ck in range(NCK):
                        pjobs.append((self.W2[ck], self.W2B[ck]))
                    pj_cast = {}
                    pj_store = {}
                    a1b_ve0 = sVE.n
                    a1b_ac0 = self.sAC.n

                    def emit_pjob(p):
                        src, dst = pjobs[p]
                        s0 = (2 * p) % 4
                        if p >= 2:
                            sy.wait_ge(sAC.h, pj_cast[p - 2])
                        else:
                            sy.wait_ge(sVE.h, a1b_ve0)
                            sy.wait_ge(sAC.h, a1b_ac0)
                        sLD.inc(sy.dma_start(out=pslot[s0], in_=src[:, 0:512]))
                        sLD.inc(sy.dma_start(out=pslot[s0 + 1], in_=src[:, 512:1024]))
                        ld = sLD.n
                        se.wait_ge(sLD.h, ld)
                        if p >= 2:
                            se.wait_ge(sST.h, pj_store[p - 2])
                        se.activation(out=pwq[:, s0 * 512:(s0 + 1) * 512],
                                      in_=pslot[s0], func=AF.Identity)
                        self.sAC.inc(se.activation(
                            out=pwq[:, (s0 + 1) * 512:(s0 + 2) * 512],
                            in_=pslot[s0 + 1], func=AF.Identity))
                        pj_cast[p] = self.sAC.n
                        sy.wait_ge(sAC.h, pj_cast[p])
                        sST.inc(sy.dma_start(out=dst,
                                             in_=pwq[:, s0 * 512:(s0 + 2) * 512]))
                        pj_store[p] = sST.n

                    xld = []
                    sLD.inc(sy.dma_start(out=xls[:, 0, :], in_=self.X[0:128, :]))
                    xld.append(sLD.n)
                    sLD.inc(sy.dma_start(out=xls[:, 1, :], in_=self.X[128:256, :]))
                    xld.append(sLD.n)
                    sn_x = [0] * NT
                    sn_t = [0] * NT
                    sn_v_ac = [0] * NT
                    for tt in range(NT):
                        xbuf = xls[:, tt % 2, :]
                        sLD.wait(ve, at=xld[tt])
                        self.emit_ln(xbuf, xlnb[:], self.c_eps_ln[:, 0:1])
                        sn_x[tt] = sVE.n
                        if tt + 2 < NT:
                            sy.wait_ge(sVE.h, sn_x[tt])
                            sLD.inc(sy.dma_start(out=xls[:, tt % 2, :],
                                                 in_=self.X[(tt + 2) * 128:(tt + 3) * 128, :]))
                            xld.append(sLD.n)
                        if tt >= 1:
                            te.wait_ge(sVE.h, sn_t[tt - 1])
                        sVE.wait(te)
                        for c in range(NCH):
                            ins = te.transpose(ps_t[:, c * 128:(c + 1) * 128],
                                               xlnb[:, c * 128:(c + 1) * 128],
                                               self.ident_b[:])
                        sPE.inc(ins)
                        sPE.wait(ve)
                        sVE.inc(ve.tensor_copy(out=xlf_t[:], in_=ps_t[:]))
                        sn_t[tt] = sVE.n
                        xlf_ready = sVE.n
                        for nb in range(2):
                            te.wait_ge(sVE.h, xlf_ready)
                            if tt == 0:
                                te.wait_ge(sVE.h, vones_ready)
                            if tt >= 1:
                                te.wait_ge(sVE.h, sn_v_ac[tt - 1])
                            bank = ps_v[:, nb * 512:(nb + 1) * 512]
                            for c in range(NCH):
                                te.matmul(bank, xlf_t[:, c * 128:(c + 1) * 128],
                                          wv_bv[:, c, nb * 512:(nb + 1) * 512],
                                          start=(c == 0), stop=False)
                            sPE.inc(te.matmul(bank, self.ones_b[0:1, 0:128],
                                              self.bqkv_b[0:1, 2048 + nb * 512:2048 + (nb + 1) * 512],
                                              start=False, stop=True))
                            sPE.wait(ve)
                            sVE.inc(ve.tensor_copy(
                                out=V_aug.ap().rearrange("p (t h e) -> p t h e", t=NT, h=NH)[:, tt, nb * 8:(nb + 1) * 8, 0:64],
                                in_=bank.rearrange("p (h e) -> p h e", e=DH)))
                        sn_v_ac[tt] = sVE.n
                        for p in range(6 * tt, 6 * tt + 6):
                            emit_pjob(p)
                    self.wcast_done = True
                    sVE.wait(sy)
                    sVE.wait(te)
                    sAC.wait(te)
                    sAC.wait(sy)
                    sST.wait(sy)
                    sST.wait(se)
                    sST.wait(ve)

            if self.taps:
                self.tap("t_kone", K_one.ap())
                self.tap("t_vaug", V_aug.ap())
                self.tap("t_rfm", r_fm.ap())
                self.tap("t_qs", self.QS)

            with ExitStack() as nmctx:
                rk_col = self.sb(nmctx, "rk_col", [32, 1], F32)
                rqb32 = self.sb(nmctx, "rqb32", [32, 8], F32)
                rqbT = self.sb(nmctx, "rqbT", [8, 32], F32)
                rkT = self.sb(nmctx, "rkT", [1, 32], F32)
                rq_row = self.sb(nmctx, "rq_row", [1, 256], F32)
                negm_row = self.sb(nmctx, "negm_row", [1, 128], F32)
                ps_nm = self.psum(nmctx, "ps_nm", [128, 512], F32)
                sVE.inc(ve.tensor_reduce(out=rk_col[:], in_=r_fm.ap(),
                                         op=ALU.max, axis=AX.X))
                sVE.inc(ve.tensor_reduce(
                    out=rqb32[:],
                    in_=r_fm.ap().rearrange("h (q t) -> h q t", q=8),
                    op=ALU.max, axis=AX.X))
                sVE.wait(te)
                te.transpose(ps_nm[0:8, 0:32], rqb32[:], self.ident_f[0:32, 0:32])
                sPE.inc(te.transpose(ps_nm[0:1, 32:64], rk_col[:],
                                     self.ident_f[0:32, 0:32]))
                sPE.wait(ve)
                ve.tensor_copy(out=rqbT[:], in_=ps_nm[0:8, 0:32])
                sVE.inc(ve.tensor_copy(out=rkT[:], in_=ps_nm[0:1, 32:64]))
                sVE.wait(sy)
                sST.inc(sy.dma_start(out=self.MSC, in_=rqbT[:]))
                sST.wait(sy)
                sLD.inc(sy.dma_start(out=rq_row[:],
                                     in_=self.MSC.rearrange("a b -> (a b)").unsqueeze(0)))
                sLD.wait(ve)
                sVE.inc(ve.scalar_tensor_tensor(
                    out=negm_row.ap().rearrange("a (q h) -> a q h", h=16),
                    in0=rq_row.ap().rearrange("a (q h) -> a q h", h=32)[:, :, 0:16],
                    scalar=-0.5,
                    in1=rkT.ap()[0:1, 16:32].unsqueeze(1).to_broadcast([1, 8, 16]),
                    op0=ALU.mult, op1=ALU.mult))
                sVE.wait(te)
                sPE.inc(te.matmul(ps_nm[:, 0:128], self.ones_f[0:1, :],
                                  negm_row[:], start=True, stop=True))
                sPE.wait(ve)
                sVE.inc(ve.tensor_copy(out=negm_mat[:], in_=ps_nm[:, 0:128]))
                self.tap("t_negm", negm_mat.ap())
                sVE.wait(te)
                sVE.wait(sy)
            rctx.close()
            a1outer.close()

            with ExitStack() as a2:
                P = self.sb(a2, "P", [128, 2 * NT * 512], BF16)
                Qb = self.sb(a2, "Qb", [128, 2, 512], BF16)
                rl = self.sb(a2, "rl", [1, 512], F32)
                bc_sb = self.sb(a2, "bc_sb", [64, 512], F32)
                zstg = self.sb(a2, "zstg", [128, 2, 512], BF16)
                ps_sA = self.psum(a2, "ps_sA", [128, 1536], F32)
                ps_sB = self.psum(a2, "ps_sB", [128, 1536], F32)
                ps_z = self.psum(a2, "ps_z", [128, 512], F32)
                ps_m = self.psum(a2, "ps_m", [128, 512], F32)

                Vv = V_aug.ap().rearrange("p (t h e) -> p t h e", t=NT, h=NH)
                Kv = K_one.ap().rearrange("p (c t) -> p c t", c=NP)

                groups = [(g * 3, min(3, NT - g * 3)) for g in range((NT + 2) // 3)]
                sn_grp = {0: 0, 1: 0}
                sn_z_ve = 0
                sn_bc_ac = 0
                sn_exp_done = {}
                sn_S_pe = {}
                sn_q = {}
                sn_zst = {}
                qbs = range(NTB) if A2_QBS is None else A2_QBS
                heads = [(qb, pc, hh) for qb in qbs for pc in range(NP)
                         for hh in range(2)]
                Pv = P.ap().rearrange("p (b n) -> p b n", b=2)

                def emit_S(i):
                    nonlocal sn_z_ve
                    qb, pc, hh = heads[i]
                    pair = i // 2
                    base = 64 * hh
                    if hh == 0:
                        if pair >= 2:
                            sy.wait_ge(sPE.h, sn_S_pe.get(pair - 2, 0))
                        sLD.inc(sy.dma_start(
                            out=Qb.ap()[:, pc % 2, :].rearrange("p (t d) -> p t d", t=4),
                            in_=self.QS.rearrange("t p d -> p t d")[:, qb * 4:(qb + 1) * 4, pc * 128:(pc + 1) * 128]))
                        sn_q[pair] = sLD.n
                    bias_ap = negm_mat[:, qb * 16 + 2 * pc + hh:qb * 16 + 2 * pc + hh + 1]
                    sLD.wait(te, at=sn_q[pair])
                    for gi, (k0, glen) in enumerate(groups):
                        pss = ps_sA if gi % 2 == 0 else ps_sB
                        te.wait_ge(sAC.h, sn_grp[gi % 2])
                        for kk in range(glen):
                            kt = k0 + kk
                            ins = te.matmul(
                                pss[:, kk * 512:(kk + 1) * 512],
                                Kv[base:base + 64, pc, kt * 128:(kt + 1) * 128],
                                Qb[base:base + 64, pc % 2, :],
                                start=True, stop=True)
                        sPE.inc(ins)
                        sPE.wait(se)
                        sAC.inc(se.activation(
                            out=Pv[:, i % 2, k0 * 512:(k0 + glen) * 512],
                            in_=pss[:, 0:glen * 512],
                            func=AF.Exp, bias=bias_ap, scale=1.0))
                        sn_grp[gi % 2] = sAC.n
                    sn_exp_done[i] = sAC.n
                    sn_S_pe[pair] = sPE.n

                def emit_AV(i):
                    nonlocal sn_z_ve, sn_bc_ac
                    qb, pc, hh = heads[i]
                    h = 2 * pc + hh
                    base = 64 * hh
                    te.wait_ge(sVE.h, sn_z_ve)
                    te.wait_ge(sAC.h, sn_exp_done[i])
                    for kt in range(NT):
                        ins = te.matmul(ps_z[0:65, :],
                                        Vv[:, kt, h, :],
                                        Pv[:, i % 2, kt * 512:(kt + 1) * 512],
                                        start=(kt == 0), stop=(kt == NT - 1))
                    sPE.inc(ins)
                    sPE.wait(ve)
                    sVE.inc(ve.tensor_scalar(out=rl[:], in0=ps_z[64:65, :],
                                             scalar1=1e-30, scalar2=None, op0=ALU.add))
                    sVE.wait(ve)
                    sVE.inc(ve.reciprocal(out=rl[:], in_=rl[:]))
                    sn_rl = sVE.n
                    te.wait_ge(sVE.h, sn_rl)
                    te.wait_ge(sAC.h, sn_bc_ac)
                    sPE.inc(te.matmul(ps_m[0:64, :], self.ones_f[0:1, 0:64],
                                      rl[:], start=True, stop=True))
                    sPE.wait(se)
                    se.wait_ge(sVE.h, sn_z_ve)
                    sAC.inc(se.activation(out=bc_sb[:], in_=ps_m[0:64, :],
                                          func=AF.Identity))
                    sn_bc_ac = sAC.n
                    sAC.wait(ve)
                    pair = i // 2
                    if pair >= 2:
                        ve.wait_ge(sST.h, sn_zst.get(pair - 2, 0))
                    sVE.inc(ve.tensor_tensor(out=zstg[base:base + 64, pc % 2, :],
                                             in0=ps_z[0:64, :], in1=bc_sb[:],
                                             op=ALU.mult))
                    sn_z_ve = sVE.n
                    if hh == 1:
                        sVE.wait(sy)
                        sST.inc(sy.dma_start(
                            out=self.ZS[pc, :, qb * 512:(qb + 1) * 512],
                            in_=zstg.ap()[:, pc % 2, :]))
                        sn_zst[pair] = sST.n

                for i in range(len(heads)):
                    emit_S(i)
                    if i > 0:
                        emit_AV(i - 1)
                emit_AV(len(heads) - 1)
                sVE.wait(te)
                sAC.wait(te)
                sVE.wait(sy)
                self.tap("t_p", P.ap())
                self.tap("t_qb", Qb.ap())

            if self.taps:
                self.tap("t_zs", self.ZS)
            kvctx.close()
            with ExitStack() as a3:
                wo_b = self.sb(a3, "wo_b", [128, NCH * 1024], BF16)
                wo_bv = wo_b.ap().rearrange("p (c n) -> p c n", c=NCH)
                wstA3 = self.sb(a3, "wstA3", [128, 1024], F32)
                zbuf = self.sb(a3, "zbuf", [128, 2, NP, 512], BF16)
                xrt = self.sb(a3, "xrt", [128, 4, D], F32)
                ofm = self.sb(a3, "ofm", [128, 4096], BF16)
                xstg = self.sb(a3, "xstg", [128, 2, D], F32)
                ps_o = self.psum(a3, "ps_o", [128, 2048], F32)
                ps_T = self.psum(a3, "ps_T", [128, 1024], BF16)

                cast_done = []
                for c in range(NCH):
                    if c >= 1:
                        sy.wait_ge(sVE.h, cast_done[c - 1])
                    sLD.inc(sy.dma_start(out=wstA3[:],
                                         in_=self.WOT[c * 128:(c + 1) * 128, :]))
                    sLD.wait(ve)
                    sVE.inc(ve.tensor_copy(out=wo_bv[:, c, :], in_=wstA3[:]))
                    cast_done.append(sVE.n)

                zld = []
                for pc in range(NP):
                    sLD.inc(sy.dma_start(out=zbuf[:, 0, pc, :],
                                         in_=self.ZS[pc, :, 0:512]))
                zld.append(sLD.n)
                sn_o_ac = [0] * NTB
                sn_T_ve = [0] * NTB
                xst_list = []
                for tb in range(NTB):
                    if tb + 1 < NTB:
                        if tb >= 1:
                            sy.wait_ge(sPE.h, self.sPE.n)
                        for pc in range(NP):
                            sLD.inc(sy.dma_start(
                                out=zbuf[:, (tb + 1) % 2, pc, :],
                                in_=self.ZS[pc, :, (tb + 1) * 512:(tb + 2) * 512]))
                        zld.append(sLD.n)
                    if tb >= 1:
                        sy.wait_ge(sVE.h, sn_T_ve[tb - 1])
                    for ts4 in range(4):
                        sLD.inc(sy.dma_start(
                            out=xrt[:, ts4, :],
                            in_=self.X[tb * 512 + ts4 * 128:tb * 512 + (ts4 + 1) * 128, :]))
                    xrld = sLD.n
                    for half in range(2):
                        sLD.wait(te, at=zld[tb])
                        if tb >= 1 or half >= 1:
                            te.wait_ge(sAC.h, sn_o_ac[tb - 1] if half == 0 else self.sAC.n)
                        for oc4 in range(4):
                            oc = half * 4 + oc4
                            bank = ps_o[:, oc4 * 512:(oc4 + 1) * 512]
                            for zc in range(NP):
                                te.matmul(bank, wo_bv[:, zc, oc * 128:(oc + 1) * 128],
                                          zbuf[:, tb % 2, zc, :],
                                          start=(zc == 0), stop=False)
                            sPE.inc(te.matmul(bank, self.bo_b[0:1, oc * 128:(oc + 1) * 128],
                                              self.ones_b[0:1, :],
                                              start=False, stop=True))
                            sPE.wait(se)
                            sAC.inc(se.activation(out=ofm[:, oc * 512:(oc + 1) * 512],
                                                  in_=bank, func=AF.Identity))
                        sn_o_ac[tb] = sAC.n
                    for ts4 in range(4):
                        if tb >= 1 and ts4 == 0:
                            te.wait_ge(sVE.h, sn_T_ve[tb - 1])
                        if ts4 >= 1:
                            te.wait_ge(sVE.h, self.sVE.n)
                        sAC.wait(te)
                        for oc in range(NCH):
                            ins = te.transpose(ps_T[:, oc * 128:(oc + 1) * 128],
                                               ofm[:, oc * 512 + ts4 * 128:oc * 512 + (ts4 + 1) * 128],
                                               self.ident_b[:])
                        sPE.inc(ins)
                        sPE.wait(ve)
                        sLD.wait(ve, at=xrld)
                        if len(xst_list) >= 2:
                            ve.wait_ge(sST.h, xst_list[-2])
                        sVE.inc(ve.tensor_tensor(out=xstg[:, ts4 % 2, :], in0=ps_T[:],
                                                 in1=xrt[:, ts4, :], op=ALU.add))
                        sn_T_ve[tb] = sVE.n
                        sVE.wait(sy)
                        r0 = tb * 512 + ts4 * 128
                        sST.inc(sy.dma_start(out=self.XR[r0:r0 + 128, :],
                                             in_=xstg.ap()[:, ts4 % 2, :]))
                        xst_list.append(sST.n)
                sVE.wait(te)
                sAC.wait(te)
                sVE.wait(sy)
                sVE.wait(ge)

    def emit_ln_pass(self, src_dram, xlf, fstage, xlnb, ps_t):
        te, ve, se, sy = self.te, self.ve, self.se, self.sy
        sLD, sPE, sVE, sAC = self.sLD, self.sPE, self.sVE, self.sAC
        xlf_v = xlf.ap().rearrange("p (c t) -> p c t", c=NCH)
        xld = []
        sVE.wait(sy)
        self.sST.wait(sy)
        sLD.inc(sy.dma_start(out=fstage[:, 0, :], in_=src_dram[0:128, :]))
        xld.append(sLD.n)
        sLD.inc(sy.dma_start(out=fstage[:, 1, :], in_=src_dram[128:256, :]))
        xld.append(sLD.n)
        sn_x = [0] * NT
        sn_t = [0] * NT
        sn_pe = [0] * NT
        for tt in range(NT):
            xbuf = fstage[:, tt % 2, :]
            sLD.wait(ve, at=xld[tt])
            if tt >= 1:
                ve.wait_ge(sPE.h, sn_pe[tt - 1])
            self.emit_ln(xbuf, xlnb[:], self.c_eps_ln[:, 0:1])
            sn_x[tt] = sVE.n
            if tt + 2 < NT:
                sy.wait_ge(sVE.h, sn_x[tt])
                sLD.inc(sy.dma_start(out=fstage[:, tt % 2, :],
                                     in_=src_dram[(tt + 2) * 128:(tt + 3) * 128, :]))
                xld.append(sLD.n)
            if tt >= 1:
                te.wait_ge(sVE.h, sn_t[tt - 1])
            sVE.wait(te)
            for c in range(NCH):
                ins = te.transpose(ps_t[:, c * 128:(c + 1) * 128],
                                   xlnb[:, c * 128:(c + 1) * 128], self.ident_b[:])
            sPE.inc(ins)
            sn_pe[tt] = sPE.n
            sPE.wait(ve)
            sVE.inc(ve.tensor_copy(
                out=xlf_v[:, :, tt * 128:(tt + 1) * 128],
                in_=ps_t.rearrange("p (c t) -> p c t", c=NCH)))
            sn_t[tt] = sVE.n

    def phase_ffn(self, src_dram, dst_dram, first_call):
        sfx = "a" if first_call else "b"
        te, ve, se, ge, sy = self.te, self.ve, self.se, self.ge, self.sy
        sLD, sST, sPE, sVE, sAC = self.sLD, self.sST, self.sPE, self.sVE, self.sAC
        NCK = NE * 4

        with ExitStack() as f:
            xlf = self.sb(f, "xlf" + sfx, [128, NCH * T], BF16)
            act_all = self.sb(f, "act_all" + sfx, [128, NCK * 512], BF16)
            w13b = self.sb(f, "w13b" + sfx, [128, 2, 2 * NCH * 512], BF16)
            w2b = self.sb(f, "w2b" + sfx, [128, 6, D], BF16)
            sbc = self.sb(f, "sbc" + sfx, [128, 512], F32)
            s_t = self.sb(f, "s_t" + sfx, [16, 128], BF16)
            s_row = self.sb(f, "s_row" + sfx, [1, 2, 512], BF16)
            fstage = self.sb(f, "fstage" + sfx, [128, 4, D], F32)
            xlnb = self.sb(f, "xlnb2" + sfx, [128, D], BF16)
            gw_b = self.sb(f, "gw_b" + sfx, [128, NCH * 16], BF16)
            rsc = self.sb(f, "rsc" + sfx, [128, 64], F32)
            ofm = self.sb(f, "ofm2" + sfx, [128, 4096], BF16)
            ps_g = self.psum(f, "ps_g" + sfx, [128, 2048], F32)
            ps_u = self.psum(f, "ps_u" + sfx, [128, 2048], F32)
            ps_t = ps_g.ap()[:, 0:512].bitcast(BF16)
            ps_o_banks = ([ps_g.ap()[:, i * 512:(i + 1) * 512] for i in range(4)]
                          + [ps_u.ap()[:, i * 512:(i + 1) * 512] for i in range(4)])

            if first_call and not getattr(self, "wcast_done", False):
                stgv = act_all.ap().bitcast(F32)
                stg_s = [stgv[:, 0:8192], stgv[:, 8192:16384]]
                wsb_s = [xlf.ap()[:, 0:8192], xlf.ap()[:, 8192:16384]]
                jobs = []
                for e in range(NE):
                    jobs.append((self.W13[e], self.W13B[e], 8192))
                for ck in range(NCK):
                    jobs.append((self.W2[ck], self.W2B[ck], 1024))
                sn_cast = {}
                sn_store = {}
                for j, (src, dst, wdt) in enumerate(jobs):
                    sl = j % 2
                    if j >= 2:
                        sy.wait_ge(sVE.h, sn_cast[j - 2])
                    sLD.inc(sy.dma_start(out=stg_s[sl][:, 0:wdt], in_=src))
                    ld = sLD.n
                    sLD.wait(ve, at=ld)
                    if j >= 2:
                        ve.wait_ge(sST.h, sn_store[j - 2])
                    sVE.inc(ve.tensor_copy(out=wsb_s[sl][:, 0:wdt],
                                           in_=stg_s[sl][:, 0:wdt]))
                    sn_cast[j] = sVE.n
                    sy.wait_ge(sVE.h, sn_cast[j])
                    sST.inc(sy.dma_start(out=dst, in_=wsb_s[sl][:, 0:wdt]))
                    sn_store[j] = sST.n
                sST.wait(sy)
                sST.wait(ve)
                sLD.wait(ve)

            self.emit_ln_pass(src_dram, xlf, fstage, xlnb,
                              ps_t=ps_u.ap()[:, 0:512].bitcast(BF16))
            xlf_v = xlf.ap().rearrange("p (c t) -> p c t", c=NCH)

            with ExitStack() as g:
                gstg = self.sb(g, "gstg" + sfx, [128, NCH * 16], F32)
                sVE.wait(sy)
                sLD.inc(sy.dma_start(
                    out=gstg.ap().rearrange("p (c n) -> p c n", c=NCH),
                    in_=self.GWT.rearrange("(c p) n -> p c n", p=128)))
                sLD.wait(ve)
                sVE.inc(ve.tensor_copy(out=gw_b[:], in_=gstg[:]))
            gw_v = gw_b.ap().rearrange("p (c n) -> p c n", c=NCH)

            ps_rt = ps_g.ap()[:, 0:512]
            lg = rsc[:, 0:16]
            t8 = rsc[:, 16:24]
            neg = rsc[:, 24:25]
            thr = rsc[:, 25:26]
            s4 = rsc[:, 26:27]
            rcp = rsc[:, 27:28]
            es = rsc[:, 32:48]
            sct = rsc[:, 48:64]
            sn_rt = 0
            sn_st1 = sn_st2 = 0
            for tt in range(NT):
                te.wait_ge(sVE.h, self.sVE.n)
                for c in range(NCH):
                    ins = te.matmul(ps_rt[:, 0:16], xlf_v[:, c, tt * 128:(tt + 1) * 128],
                                    gw_v[:, c, :], start=(c == 0), stop=(c == NCH - 1))
                sPE.inc(ins)
                sPE.wait(ve)
                sVE.inc(ve.tensor_copy(out=lg, in_=ps_rt[:, 0:16]))
                sVE.wait(ve)
                sVE.inc(ve.max(out=t8, in_=lg))
                sVE.wait(ve)
                sVE.inc(ve.tensor_scalar_mul(neg, t8[:, 0:1], -1.0))
                sVE.wait(se)
                sAC.inc(se.activation(out=es, in_=lg, func=AF.Exp, bias=neg[:, 0:1],
                                      scale=1.0))
                sAC.wait(se)
                sAC.inc(se.activation(out=thr, in_=t8[:, 3:4], func=AF.Exp,
                                      bias=neg[:, 0:1], scale=1.0))
                sAC.wait(ve)
                sVE.inc(ve.tensor_scalar(out=sct, in0=es, scalar1=thr[:, 0:1],
                                         scalar2=None, op0=ALU.is_ge))
                sVE.wait(ve)
                sVE.inc(ve.tensor_tensor(out=es, in0=es, in1=sct, op=ALU.mult))
                sVE.wait(ve)
                sVE.inc(ve.tensor_reduce(out=s4, in_=es, op=ALU.add, axis=AX.X))
                sVE.wait(ve)
                sVE.inc(ve.reciprocal(out=rcp, in_=s4))
                sVE.wait(ve)
                sVE.inc(ve.tensor_scalar(out=sct, in0=es, scalar1=rcp[:, 0:1],
                                         scalar2=None, op0=ALU.mult))
                te.wait_ge(sVE.h, self.sVE.n)
                if tt >= 1:
                    te.wait_ge(sVE.h, sn_rt)
                if tt >= 2:
                    ve.wait_ge(sST.h, sn_st2)
                sPE.inc(te.transpose(ps_rt[0:16, 16:144], sct, self.ident_f[:]))
                sPE.wait(ve)
                sVE.inc(ve.tensor_copy(out=s_t[:], in_=ps_rt[0:16, 16:144]))
                sn_rt = sVE.n
                sVE.wait(sy)
                sST.inc(sy.dma_start(out=self.SFD[:, tt * 128:(tt + 1) * 128],
                                     in_=s_t[:]))
                sn_st2 = sn_st1
                sn_st1 = sST.n

            w13v = w13b.ap().rearrange("p b (h c n) -> p b h c n", h=2, c=NCH)
            w1v = w13v[:, :, 0]
            w3v = w13v[:, :, 1]
            sn_w13 = {}
            sn_bc_pe = {}
            sn_gu_pe = {}
            sn_act_ve = {}
            sn_silu_ic = {}
            sn_vu_ic = {}
            sn_w2 = {}
            sn_w2_pe = {}
            sn_o_ac = 0
            sn_T_ve = 0
            sn_xst = []
            for gi0 in (0, 1):
                sLD.inc(sy.dma_start(out=w13b.ap()[:, gi0, :],
                                     in_=self.W13B[gi0 % NE]))
                sn_w13[gi0] = sLD.n

            for tb in range(NTB):
                sn_sbc_prev = 0
                for e in range(NE):
                    gi = tb * NE + e
                    if gi >= 2:
                        sy.wait_ge(sPE.h, sn_bc_pe.get(gi - 2, 0))
                    sLD.inc(sy.dma_start(out=s_row.ap()[:, gi % 2, :],
                                         in_=self.SFD[e:e + 1, tb * 512:(tb + 1) * 512]))
                    sn_srow = sLD.n
                    te.wait_ge(sAC.h, max(
                        sn_silu_ic[gi - 1][0] if (gi - 1) in sn_silu_ic else 0,
                        sn_o_ac))
                    te.wait_ge(sVE.h, sn_sbc_prev)
                    sLD.wait(te, at=sn_srow)
                    sPE.inc(te.matmul(ps_g.ap()[:, 0:512], self.ones_b[0:1, 0:128],
                                      s_row.ap()[:, gi % 2, :],
                                      start=True, stop=True))
                    sn_bc_pe[gi] = sPE.n
                    sPE.wait(ve)
                    sVE.inc(ve.tensor_copy(out=sbc[:], in_=ps_g.ap()[:, 0:512]))
                    sn_sbc = sn_sbc_prev = sVE.n

                    te.wait_ge(sLD.h, sn_w13[gi])
                    te.wait_ge(sVE.h, sn_sbc)
                    prev_silu = sn_silu_ic.get(gi - 1)
                    prev_vu = sn_vu_ic.get(gi - 1)
                    cur_silu = [0] * 4
                    cur_vu = [0] * 4
                    for ic in range(4):
                        gbank = ps_g.ap()[:, ic * 512:(ic + 1) * 512]
                        ubank = ps_u.ap()[:, ic * 512:(ic + 1) * 512]
                        need_ac = sn_o_ac if e == 0 else 0
                        if prev_silu is not None:
                            need_ac = max(need_ac, prev_silu[ic])
                            te.wait_ge(sVE.h, prev_vu[ic])
                        if need_ac:
                            te.wait_ge(sAC.h, need_ac)
                        for c in range(NCH):
                            te.matmul(gbank, w1v[:, gi % 2, c, ic * 128:(ic + 1) * 128],
                                      xlf_v[:, c, tb * 512:(tb + 1) * 512],
                                      start=(c == 0), stop=(c == NCH - 1))
                        for c in range(NCH):
                            ins = te.matmul(ubank, w3v[:, gi % 2, c, ic * 128:(ic + 1) * 128],
                                            xlf_v[:, c, tb * 512:(tb + 1) * 512],
                                            start=(c == 0), stop=(c == NCH - 1))
                        sPE.inc(ins)
                        pe_ic = sPE.n
                        se.wait_ge(sPE.h, pe_ic)
                        sAC.inc(se.activation(out=ofm[:, ic * 512:(ic + 1) * 512],
                                              in_=gbank, func=AF.Silu))
                        cur_silu[ic] = sAC.n
                        ve.wait_ge(sPE.h, pe_ic)
                        sVE.inc(ve.tensor_tensor(
                            out=ofm[:, 2048 + ic * 512:2048 + (ic + 1) * 512],
                            in0=ubank, in1=sbc[:], op=ALU.mult))
                        cur_vu[ic] = sVE.n
                    sn_silu_ic[gi] = cur_silu
                    sn_vu_ic[gi] = cur_vu
                    sn_gu_pe[gi] = sPE.n
                    ni = gi + 2
                    if ni < NTB * NE:
                        sy.wait_ge(sPE.h, sn_gu_pe[gi])
                        sLD.inc(sy.dma_start(out=w13b.ap()[:, ni % 2, :],
                                             in_=self.W13B[ni % NE]))
                        sn_w13[ni] = sLD.n
                    ve.wait_ge(sAC.h, cur_silu[3])
                    sVE.inc(ve.tensor_tensor(
                        out=act_all[:, e * 2048:(e + 1) * 2048],
                        in0=ofm[:, 0:2048], in1=ofm[:, 2048:4096], op=ALU.mult))
                    sn_act_ve[gi] = sVE.n

                if tb >= 1:
                    sy.wait_ge(sPE.h, self.sPE.n)
                for pp in range(3):
                    sLD.inc(sy.dma_start(
                        out=w2b[:, 2 * pp:2 * pp + 2, :],
                        in_=self.W2B[2 * pp:2 * pp + 2].rearrange("k p d -> p k d")))
                    sn_w2[pp] = sLD.n
                for ck in range(NCK):
                    te.wait_ge(sLD.h, sn_w2[ck // 2])
                    if ck == 0:
                        te.wait_ge(sVE.h, sn_act_ve[tb * NE + NE - 1])
                        te.wait_ge(sAC.h, sn_silu_ic[tb * NE + NE - 1][3])
                        te.wait_ge(sVE.h, sn_sbc_prev)
                    for oc in range(NCH):
                        ins = te.matmul(ps_o_banks[oc],
                                        w2b[:, ck % 6, oc * 128:(oc + 1) * 128],
                                        act_all[:, ck * 512:(ck + 1) * 512],
                                        start=(ck == 0), stop=(ck == NCK - 1))
                    sPE.inc(ins)
                    sn_w2_pe[ck] = sPE.n
                    if ck % 2 == 1:
                        npair = ck // 2 + 3
                        if 2 * npair < NCK:
                            s0 = (2 * npair) % 6
                            sy.wait_ge(sPE.h, sn_w2_pe[ck])
                            sLD.inc(sy.dma_start(
                                out=w2b[:, s0:s0 + 2, :],
                                in_=self.W2B[2 * npair:2 * npair + 2]
                                .rearrange("k p d -> p k d")))
                            sn_w2[npair] = sLD.n
                sPE.wait(se)
                for oc in range(NCH):
                    ins = se.activation(out=ofm[:, oc * 512:(oc + 1) * 512],
                                        in_=ps_o_banks[oc], func=AF.Identity)
                sAC.inc(ins)
                sn_o_ac = self.sAC.n
                for ts4 in range(4):
                    r0 = tb * 512 + ts4 * 128
                    sy.wait_ge(sVE.h, sn_T_ve)
                    sLD.inc(sy.dma_start(out=fstage[:, ts4 % 2, :],
                                         in_=src_dram[r0:r0 + 128, :]))
                    xrld = sLD.n
                    sAC.wait(te)
                    te.wait_ge(sVE.h, sn_T_ve)
                    for oc in range(NCH):
                        ins = te.transpose(ps_t[:, oc * 128:(oc + 1) * 128],
                                           ofm[:, oc * 512 + ts4 * 128:oc * 512 + (ts4 + 1) * 128],
                                           self.ident_b[:])
                    sPE.inc(ins)
                    sPE.wait(ve)
                    sLD.wait(ve, at=xrld)
                    if len(sn_xst) >= 2:
                        ve.wait_ge(sST.h, sn_xst[-2])
                    sVE.inc(ve.tensor_tensor(out=fstage[:, 2 + ts4 % 2, :],
                                             in0=ps_t, in1=fstage[:, ts4 % 2, :],
                                             op=ALU.add))
                    sn_T_ve = sVE.n
                    sVE.wait(sy)
                    sST.inc(sy.dma_start(out=dst_dram[r0:r0 + 128, :],
                                         in_=fstage.ap()[:, 2 + ts4 % 2, :]))
                    sn_xst.append(sST.n)
            sVE.wait(te)
            sAC.wait(te)
            sVE.wait(sy)
            sVE.wait(ge)
            self.tap("t_sfd", self.SFD)
            self.tap("t_act", act_all.ap())
            self.tap("t_xlf", xlf.ap())
            self.tap("t_w13b", self.W13B)
            self.tap("t_w2b", self.W2B)

    def phase_b(self, src_dram, dst_dram):
        te, ve, se, ge, sy = self.te, self.ve, self.se, self.ge, self.sy
        sLD, sST, sPE, sVE, sAC = self.sLD, self.sST, self.sPE, self.sVE, self.sAC

        with ExitStack() as b:
            xlf = self.sb(b, "xlf3", [128, NCH * T], BF16)
            B1 = self.sb(b, "B1", [128, T], F32)
            B2 = self.sb(b, "B2", [128, T], F32)
            B3 = self.sb(b, "B3", [128, T], F32)
            B4 = self.sb(b, "B4", [128, T], F32)
            fstage = self.sb(b, "fstage3", [128, 4, D], F32)
            xlnb = self.sb(b, "xlnb3", [128, D], BF16)
            wg_b = self.sb(b, "wg_b", [128, 2, 128], BF16)
            wgstg = self.sb(b, "wgstg", [128, 128], F32)
            bcol = self.sb(b, "bcol", [128, 2], F32)
            hstage = self.sb(b, "hstage", [128, 2, 512], F32)
            xq = self.sb(b, "xq", [128, 2, 512], F32)
            ps_gx = self.psum(b, "ps_gx", [128, 2048], F32)
            ps_ht = self.psum(b, "ps_ht", [128, 512], F32)

            self.emit_ln_pass(src_dram, xlf, fstage, xlnb,
                              ps_t=ps_gx.ap()[:, 0:512].bitcast(BF16))
            xlf_v = xlf.ap().rearrange("p (c t) -> p c t", c=NCH)

            for c in range(NCH):
                for i, (WSRC, BSRC) in enumerate(((self.IGW, self.IGB),
                                                  (self.AGW, self.AGB))):
                    sVE.wait(sy)
                    sLD.inc(sy.dma_start(out=wgstg[:], in_=WSRC[c]))
                    sLD.inc(sy.dma_start(out=bcol[:, i:i + 1],
                                         in_=BSRC[c:c + 1, :].rearrange("a b -> b a")))
                    sLD.wait(ve)
                    sVE.inc(ve.tensor_copy(out=wg_b[:, i, :], in_=wgstg[:]))
                sn_wg = sVE.n
                for i, dstbuf in ((0, B1), (1, B2)):
                    for tb in range(NTB):
                        bank = ps_gx.ap()[:, (tb % 2) * 512:(tb % 2 + 1) * 512]
                        te.wait_ge(sVE.h, sn_wg)
                        te.wait_ge(sAC.h, self.sAC.n)
                        sPE.inc(te.matmul(bank, wg_b[:, i, :],
                                          xlf_v[:, c, tb * 512:(tb + 1) * 512],
                                          start=True, stop=True))
                        sPE.wait(se)
                        sAC.inc(se.activation(out=dstbuf[:, tb * 512:(tb + 1) * 512],
                                              in_=bank, func=AF.Sigmoid,
                                              bias=bcol[:, i:i + 1], scale=1.0))
                sAC.wait(ve)
                sVE.inc(ve.tensor_scalar(out=B2[:], in0=B2[:],
                                         scalar1=self.sp8[:, c:c + 1], scalar2=None,
                                         op0=ALU.mult))
                sVE.inc(ve.tensor_tensor_scan(out=B3[:], data0=B2[:], data1=B2[:],
                                              initial=0.0, op0=ALU.add, op1=ALU.bypass))
                sVE.inc(ve.tensor_scalar(out=B3[:], in0=B3[:], scalar1=LCLIP,
                                         scalar2=None, op0=ALU.max))
                ve.memset(B4[:, 0:1], 0.0)
                ve.tensor_copy(out=B4[:, 1:2], in_=B3[:, 0:1])
                sVE.inc(ve.tensor_tensor(out=B4[:, 2:T], in0=B3[:, 1:T - 1],
                                         in1=B3[:, 0:T - 2], op=ALU.subtract))
                sVE.wait(se)
                sAC.inc(se.activation(out=B4[:], in_=B4[:], func=AF.Exp))
                sAC.inc(se.activation(out=B2[:], in_=B2[:], func=AF.Exp, scale=2.0))
                sAC.wait(ve)
                ve.tensor_scalar(out=B2[:], in0=B2[:], scalar1=-1.0, scalar2=1.0,
                                 op0=ALU.mult, op1=ALU.add)
                sVE.inc(ve.tensor_scalar(out=B2[:], in0=B2[:], scalar1=1e-7,
                                         scalar2=None, op0=ALU.max))
                sVE.wait(se)
                se.activation(out=B2[:], in_=B2[:], func=AF.Ln)
                sAC.inc(se.activation(out=B2[:], in_=B2[:], func=AF.Exp, scale=0.5))
                sAC.wait(ve)
                ve.tensor_tensor(out=B1[:], in0=xlf_v[:, c, :], in1=B1[:], op=ALU.mult)
                sVE.inc(ve.tensor_tensor(out=B1[:], in0=B1[:], in1=B2[:], op=ALU.mult))
                sVE.inc(ve.tensor_tensor_scan(out=B3[:], data0=B4[:], data1=B1[:],
                                              initial=0.0, op0=ALU.mult, op1=ALU.add))
                sn_hst = []
                sn_hT = 0
                xqld = {}
                for q4 in range(NCH):
                    r0 = q4 * 512
                    dcol = dst_dram[r0:r0 + 512, c * 128:(c + 1) * 128] \
                        .rearrange("(q p) d -> p q d", p=128)
                    if q4 >= 2:
                        sy.wait_ge(sVE.h, sn_hT)
                    sLD.inc(sy.dma_start(out=xq.ap()[:, q4 % 2, :]
                                         .rearrange("p (q d) -> p q d", q=4),
                                         in_=dcol))
                    xqld[q4] = sLD.n
                    if q4 >= 1:
                        te.wait_ge(sVE.h, sn_hT)
                    sVE.wait(te)
                    for j in range(4):
                        tt = q4 * 4 + j
                        ins = te.transpose(ps_ht[:, j * 128:(j + 1) * 128],
                                           B3[:, tt * 128:(tt + 1) * 128],
                                           self.ident_f[:])
                    sPE.inc(ins)
                    sPE.wait(ve)
                    sLD.wait(ve, at=xqld[q4])
                    if len(sn_hst) >= 2:
                        ve.wait_ge(sST.h, sn_hst[-2])
                    sVE.inc(ve.tensor_tensor(out=hstage.ap()[:, q4 % 2, :],
                                             in0=ps_ht.ap(),
                                             in1=xq.ap()[:, q4 % 2, :], op=ALU.add))
                    sn_hT = sVE.n
                    sVE.wait(sy)
                    sST.inc(sy.dma_start(
                        out=dcol,
                        in_=hstage.ap()[:, q4 % 2, :].rearrange("p (q d) -> p q d", q=4)))
                    sn_hst.append(sST.n)
                sST.wait(sy)
            sVE.wait(te)
            sAC.wait(te)
            sVE.wait(sy)
            self.tap("t_B1", B1.ap())
            self.tap("t_B2", B2.ap())
            self.tap("t_B3", B3.ap())
            self.tap("t_B4", B4.ap())
            self.tap("t_xlf3", xlf.ap())

    def finish(self, copy_xr_to_out=False):
        ge, sy, ve = self.ge, self.sy, self.ve
        if copy_xr_to_out:
            self.sST.wait(sy)
            self.sST.inc(sy.dma_start(out=self.OUT, in_=self.XR))
        self.sST.wait(sy)
        self.sLD.wait(sy)
        self.sLG.wait(sy)
        self.sST.wait(ge)

    def build(self):
        self.prelude()
        if "A" in PHASES:
            self.phase_a()
        first = True
        if "F1" in PHASES:
            self.phase_ffn(self.XR, self.XR, first_call=first)
            first = False
        if "B" in PHASES:
            self.phase_b(self.XR, self.XR)
        if "F2" in PHASES:
            last = ("A" in PHASES and "F1" in PHASES and "B" in PHASES)
            self.phase_ffn(self.XR, self.OUT if last else self.XR,
                           first_call=first)
            first = False
            done_full = last
        else:
            done_full = False
        self.finish(copy_xr_to_out=not done_full)
        self.ctx.close()
        return self.nc


def _shard_inputs(inputs):
    f = np.ascontiguousarray
    x = inputs["L0_attn_a_x"]
    wq = np.asarray(inputs["L0_attn_a_wq"])
    wk = np.asarray(inputs["L0_attn_a_wk"])
    common = {
        "wqkvT": f(np.asarray(inputs["L0_attn_a_Wqkv"]).T),
        "bqkv": f(np.asarray(inputs["L0_attn_a_bqkv"]).reshape(1, 3 * D)),
        "woT": f(np.asarray(inputs["L0_attn_a_Wo"]).T),
        "bo": f(np.asarray(inputs["L0_attn_a_bo"]).reshape(1, D)),
        "wqkrow": f(np.concatenate([np.tile(wq, NH), np.tile(wk, NH)]).reshape(1, 2 * D)),
        "igw": f(np.asarray(inputs["L0_attn_b_ig_w"])),
        "igb": f(np.asarray(inputs["L0_attn_b_ig_b"])),
        "agw": f(np.asarray(inputs["L0_attn_b_ag_w"])),
        "agb": f(np.asarray(inputs["L0_attn_b_ag_b"])),
        "apar": f(np.asarray(inputs["L0_attn_b_a_param"]).reshape(NCH, 128)),
        "gwT": f(np.asarray(inputs["L0_ffn_gate_w"]).T),
        "w13": f(np.concatenate(
            [np.asarray(inputs["L0_ffn_w1"]).reshape(NE, NCH, 128, DI)
             .transpose(0, 2, 1, 3).reshape(NE, 128, NCH * DI),
             np.asarray(inputs["L0_ffn_w3"]).reshape(NE, NCH, 128, DI)
             .transpose(0, 2, 1, 3).reshape(NE, 128, NCH * DI)], axis=2)),
        "w2": f(np.asarray(inputs["L0_ffn_w2"]).reshape(NE * 4, 128, D)),
    }
    in_maps = []
    for i in range(8):
        m = dict(common)
        m["x"] = f(np.asarray(x[i]))
        in_maps.append(m)
    return in_maps


def kernel(**inputs):
    in_maps = _shard_inputs(inputs)
    nc = Builder().build()
    res = run_bass_kernel_spmd(nc, in_maps, core_ids=list(range(8)))
    out = np.stack([np.asarray(r["out"]) for r in res.results], axis=0)
    return out.astype(np.float32)


# version 08 (final)
# speedup vs baseline: 1.1089x; 1.0129x over previous
import numpy as np
import concourse.bass as bass
import concourse.mybir as mybir
from concourse.bass_utils import run_bass_kernel_spmd
from contextlib import ExitStack

F32 = mybir.dt.float32
BF16 = mybir.dt.bfloat16
AF = mybir.ActivationFunctionType
ALU = mybir.AluOpType
AX = mybir.AxisListType

T = 4096
D = 1024
NT = 32
NH = 16
DH = 64
NP = 8
NE = 16
DI = 512
NTB = 8
NCH = 8
LCLIP = float(np.log(np.float32(1e-7)))

PHASES = ("A", "F1", "B", "F2")
A2_QBS = None
DEV_TAPS = {}


class Sem:
    def __init__(self, ctx, nc, name, step=1):
        self.h = ctx.enter_context(nc.semaphore(name))
        self.n = 0
        self.step = step

    def inc(self, ins, k=1):
        ins.then_inc(self.h, k * self.step)
        self.n += k * self.step
        return self.n

    def wait(self, eng, at=None):
        eng.wait_ge(self.h, self.n if at is None else at)


class Builder:
    def __init__(self):
        nc = bass.Bass("TRN2", target_bir_lowering=False, dynamic_dma_scratch_size=8192)
        self.nc = nc
        self.te, self.ve, self.se, self.ge, self.sy = (
            nc.tensor, nc.vector, nc.scalar, nc.gpsimd, nc.sync)
        self.ctx = ExitStack()
        self.sLD = Sem(self.ctx, nc, "sLD", 16)
        self.sLG = Sem(self.ctx, nc, "sLG", 16)
        self.sST = Sem(self.ctx, nc, "sST", 16)
        self.sPE = Sem(self.ctx, nc, "sPE")
        self.sVE = Sem(self.ctx, nc, "sVE")
        self.sAC = Sem(self.ctx, nc, "sAC")
        self.sGE = Sem(self.ctx, nc, "sGE")
        self._dram()
        self.taps = {}
        for name, (shape, dt) in DEV_TAPS.items():
            self.taps[name] = nc.dram_tensor(name, list(shape), dt,
                                             kind="ExternalOutput").ap()

    def sb(self, ctx, name, shape, dtype):
        return ctx.enter_context(self.nc.sbuf_tensor(name, shape, dtype))

    def psum(self, ctx, name, shape, dtype):
        return ctx.enter_context(self.nc.psum_tensor(name, shape, dtype))

    def _dram(self):
        nc = self.nc
        self.X = nc.dram_tensor("x", [T, D], F32, kind="ExternalInput").ap()
        self.WQKVT = nc.dram_tensor("wqkvT", [D, 3 * D], F32, kind="ExternalInput").ap()
        self.BQKV = nc.dram_tensor("bqkv", [1, 3 * D], F32, kind="ExternalInput").ap()
        self.WOT = nc.dram_tensor("woT", [D, D], F32, kind="ExternalInput").ap()
        self.BO = nc.dram_tensor("bo", [1, D], F32, kind="ExternalInput").ap()
        self.WQKROW = nc.dram_tensor("wqkrow", [1, 2 * D], F32, kind="ExternalInput").ap()
        self.IGW = nc.dram_tensor("igw", [NCH, 128, 128], F32, kind="ExternalInput").ap()
        self.IGB = nc.dram_tensor("igb", [NCH, 128], F32, kind="ExternalInput").ap()
        self.AGW = nc.dram_tensor("agw", [NCH, 128, 128], F32, kind="ExternalInput").ap()
        self.AGB = nc.dram_tensor("agb", [NCH, 128], F32, kind="ExternalInput").ap()
        self.APAR = nc.dram_tensor("apar", [NCH, 128], F32, kind="ExternalInput").ap()
        self.GWT = nc.dram_tensor("gwT", [D, NE], F32, kind="ExternalInput").ap()
        self.W13 = nc.dram_tensor("w13", [NE, 128, 2 * NCH * DI], F32, kind="ExternalInput").ap()
        self.W2 = nc.dram_tensor("w2", [NE * 4, 128, D], F32, kind="ExternalInput").ap()
        self.OUT = nc.dram_tensor("out", [T, D], F32, kind="ExternalOutput").ap()
        self.XR = nc.dram_tensor("xres", [T, D], F32, kind="Internal").ap()
        self.QS = nc.dram_tensor("qsc", [NT, 128, D], BF16, kind="Internal").ap()
        self.ZS = nc.dram_tensor("zsc", [NP, 128, T], BF16, kind="Internal").ap()
        self.W13B = nc.dram_tensor("w13bd", [NE, 128, 2 * NCH * DI], BF16, kind="Internal").ap()
        self.W2B = nc.dram_tensor("w2bd", [NE * 4, 128, D], BF16, kind="Internal").ap()
        self.MSC = nc.dram_tensor("msc", [8, 32], F32, kind="Internal").ap()
        self.SFD = nc.dram_tensor("sfd", [NE, T], BF16, kind="Internal").ap()

    def tap(self, name, src_ap):
        if name in self.taps:
            self.sVE.wait(self.sy)
            self.sAC.wait(self.sy)
            self.sPE.wait(self.sy)
            self.sST.inc(self.sy.dma_start(out=self.taps[name], in_=src_ap))

    def prelude(self):
        ctx, nc = self.ctx, self.nc
        te, ve, se, ge, sy = self.te, self.ve, self.se, self.ge, self.sy
        sLD, sVE, sAC, sPE, sGE = self.sLD, self.sVE, self.sAC, self.sPE, self.sGE

        self.ident_b = self.sb(ctx, "ident_b", [128, 128], BF16)
        self.ident_f = self.sb(ctx, "ident_f", [128, 128], F32)
        self.ones_b = self.sb(ctx, "ones_b", [1, 512], BF16)
        self.ones_f = self.sb(ctx, "ones_f", [1, 128], F32)
        self.c_eps_ln = self.sb(ctx, "c_eps_ln", [128, 1], F32)
        self.c_eps_rms = self.sb(ctx, "c_eps_rms", [128, 1], F32)
        self.c_one = self.sb(ctx, "c_one", [128, 1], F32)
        self.sp8 = self.sb(ctx, "sp8", [128, NCH], F32)
        self.lnst = self.sb(ctx, "lnst", [128, 12], F32)
        self.lnmv = self.sb(ctx, "lnmv", [128, 2], F32)
        self.lnr = self.sb(ctx, "lnr", [128, 2], F32)

        with ExitStack() as pctx:
            iscr = self.sb(pctx, "iscr", [128, 128], F32)
            stg1 = self.sb(pctx, "stg1", [1, 3 * D], F32)
            stg2 = self.sb(pctx, "stg2", [128, NCH], F32)
            sp_t = self.sb(pctx, "sp_t", [128, NCH], F32)
            ps_pre = self.psum(pctx, "ps_pre", [128, 512], F32)

            sGE.inc(ge.iota(out=iscr[:], pattern=[[-1, 128]], base=0,
                            channel_multiplier=1,
                            allow_small_or_imprecise_dtypes=True))
            ve.memset(self.ones_b[:], 1.0)
            ve.memset(self.ones_f[:], 1.0)
            ve.memset(self.c_eps_ln[:], 1e-5)
            ve.memset(self.c_eps_rms[:], 1e-6)
            sVE.inc(ve.memset(self.c_one[:], 1.0))
            sGE.wait(ve)
            ve.tensor_scalar(out=self.ident_b[:], in0=iscr[:], scalar1=0.0,
                             scalar2=None, op0=ALU.is_equal)
            sVE.inc(ve.tensor_scalar(out=self.ident_f[:], in0=iscr[:],
                                     scalar1=0.0, scalar2=None, op0=ALU.is_equal))

            for c in range(NCH):
                sLD.inc(sy.dma_start(
                    out=stg2[:, c:c + 1],
                    in_=self.APAR[c:c + 1, :].rearrange("a b -> b a")))
            sLD.wait(se)
            sAC.inc(se.activation(out=sp_t[:], in_=stg2[:], func=AF.Abs))
            sAC.wait(se)
            sAC.inc(se.activation(out=sp_t[:], in_=sp_t[:], func=AF.Exp, scale=-1.0))
            sAC.wait(se)
            sVE.wait(se)
            sAC.inc(se.activation(out=sp_t[:], in_=sp_t[:], func=AF.Ln,
                                  bias=self.c_one[:, 0:1], scale=1.0))
            sLD.wait(ve)
            sAC.wait(ve)
            sVE.inc(ve.tensor_scalar(out=self.sp8[:], in0=stg2[:], scalar1=0.0,
                             scalar2=None, op0=ALU.max))
            sVE.wait(ve)
            sVE.inc(ve.tensor_tensor(out=self.sp8[:], in0=self.sp8[:], in1=sp_t[:], op=ALU.add))
            sVE.wait(ve)
            sVE.inc(ve.tensor_scalar(out=self.sp8[:], in0=self.sp8[:],
                                     scalar1=-8.0, scalar2=None, op0=ALU.mult))
            sVE.wait(te)
            sVE.wait(ge)
            sVE.wait(sy)

    def emit_ln(self, xbuf, out_bf, eps_ap):
        ve, se = self.ve, self.se
        sVE, sAC = self.sVE, self.sAC
        ve.bn_stats(out=self.lnst[:, 0:6], in_=xbuf[:, 0:512])
        sVE.inc(ve.bn_stats(out=self.lnst[:, 6:12], in_=xbuf[:, 512:1024]))
        sVE.wait(ve)
        if getattr(self, "_ln_ac", 0):
            ve.wait_ge(sAC.h, self._ln_ac)
        sVE.inc(ve.bn_aggr(out=self.lnmv[:], in_=self.lnst[:]))
        sVE.wait(ve)
        sVE.wait(se)
        sAC.inc(se.activation(out=self.lnr[:, 0:1], in_=self.lnmv[:, 1:2], func=AF.Ln,
                      bias=eps_ap, scale=1.0))
        self._ln_ac = sAC.n
        sAC.wait(se)
        sAC.inc(se.activation(out=self.lnr[:, 0:1], in_=self.lnr[:, 0:1],
                              func=AF.Exp, scale=-0.5))
        sAC.wait(ve)
        sVE.inc(ve.scalar_tensor_tensor(out=self.lnr[:, 1:2], in0=self.lnmv[:, 0:1],
                                        scalar=-1.0, in1=self.lnr[:, 0:1],
                                        op0=ALU.mult, op1=ALU.mult))
        sVE.wait(ve)
        sVE.inc(ve.tensor_scalar(out=out_bf, in0=xbuf,
                                 scalar1=self.lnr[:, 0:1], scalar2=self.lnr[:, 1:2],
                                 op0=ALU.mult, op1=ALU.add))

    def phase_a(self):
        ctx = self.ctx
        nc = self.nc
        te, ve, se, ge, sy = self.te, self.ve, self.se, self.ge, self.sy
        sLD, sST, sPE, sVE, sAC = self.sLD, self.sST, self.sPE, self.sVE, self.sAC

        with ExitStack() as actx:
            self.bo_b = self.sb(actx, "bo_b", [1, D], BF16)
            kvctx = ExitStack()
            negm_mat = self.sb(kvctx, "negm_mat", [128, 128], F32)
            K_one = self.sb(kvctx, "K_one", [128, NP * T], BF16)
            V_aug = self.sb(kvctx, "V_aug", [128, NT * NH * 65], BF16)
            a1outer = ExitStack()
            self.wqkb = self.sb(a1outer, "wqkb", [128, 2 * D], BF16)
            self.bqkv_b = self.sb(a1outer, "bqkv_b", [1, 3 * D], BF16)
            rctx = ExitStack()
            r_fm = self.sb(rctx, "r_fm", [32, T], BF16)
            with ExitStack() as wctx:
                stg1 = self.sb(wctx, "stg1a", [1, 3 * D], F32)
                ps_pre = self.psum(wctx, "ps_prea", [128, 512], F32)
                sLD.inc(sy.dma_start(out=stg1[0:1, 0:2 * D], in_=self.WQKROW))
                sLD.wait(te)
                self.sVE.wait(te)
                for nb in range(4):
                    if nb:
                        self.sVE.wait(te)
                    sPE.inc(te.matmul(ps_pre[:], self.ones_f[0:1, :],
                                      stg1[0:1, nb * 512:(nb + 1) * 512],
                                      start=True, stop=True))
                    sPE.wait(ve)
                    sVE.inc(ve.tensor_copy(out=self.wqkb[:, nb * 512:(nb + 1) * 512],
                                           in_=ps_pre[:]))
                sVE.wait(sy)
                sLD.inc(sy.dma_start(out=stg1[0:1, :], in_=self.BQKV))
                sLD.wait(ve)
                sVE.inc(ve.tensor_copy(out=self.bqkv_b[:], in_=stg1[0:1, :]))
                sVE.wait(sy)
                sLD.inc(sy.dma_start(out=stg1[0:1, 0:D], in_=self.BO))
                sLD.wait(ve)
                sVE.inc(ve.tensor_copy(out=self.bo_b[:], in_=stg1[0:1, 0:D]))
                sVE.wait(sy)
                sVE.wait(te)

            with ExitStack() as a1:
                xls = self.sb(a1, "xls", [128, 2, D], F32)
                xlnb = self.sb(a1, "xlnb", [128, D], BF16)
                xlf_t = self.sb(a1, "xlf_t", [128, D], BF16)
                wst = self.sb(a1, "wst", [128, 2048], F32)
                q2sb = wst
                s2h = self.sb(a1, "s2h", [128, 32], F32)
                rstdh = self.sb(a1, "rstdh", [128, 32], F32)
                qn1 = self.sb(a1, "qn1", [128, 512], F32)
                qkn = self.sb(a1, "qkn", [128, 2048], BF16)
                rtm = self.sb(a1, "rtm", [128, 32], F32)
                qstg = self.sb(a1, "qstg", [128, 2, D], BF16)
                ps_t = self.psum(a1, "ps_t", [128, 1024], BF16)
                ps_qk = self.psum(a1, "ps_qk", [128, 2048], F32)
                ps_qt = self.psum(a1, "ps_qt", [128, 1024], BF16)
                ps_kt = self.psum(a1, "ps_kt", [128, 1024], BF16)
                ps_r = self.psum(a1, "ps_r", [32, 128], F32)

                with ExitStack() as a1a:
                    wqk_b = self.sb(a1a, "wqk_b", [128, NCH * 2048], BF16)
                    wqk_bv = wqk_b.ap().rearrange("p (c n) -> p c n", c=NCH)
                    cast_done = []
                    for c in range(NCH):
                        if c >= 1:
                            sy.wait_ge(sVE.h, cast_done[c - 1])
                        sLD.inc(sy.dma_start(out=wst[:],
                                             in_=self.WQKVT[c * 128:(c + 1) * 128, 0:2048]))
                        sLD.wait(ve)
                        sVE.inc(ve.tensor_copy(out=wqk_bv[:, c, :], in_=wst[:]))
                        cast_done.append(sVE.n)

                    xld = []
                    sLD.inc(sy.dma_start(out=xls[:, 0, :], in_=self.X[0:128, :]))
                    xld.append(sLD.n)
                    sLD.inc(sy.dma_start(out=xls[:, 1, :], in_=self.X[128:256, :]))
                    xld.append(sLD.n)

                    sn_x = [0] * NT
                    sn_t = [0] * NT
                    sn_qk_ve = [0] * NT
                    sn_qk_ac = [0] * NT
                    sn_r = [0] * NT
                    sn_qt = [0] * NT
                    sn_qst = [0] * NT
                    sn_xtr_pe = [0] * NT
                    sn_qtr_pe = [0] * NT

                    for tt in range(NT):
                        xbuf = xls[:, tt % 2, :]
                        sLD.wait(ve, at=xld[tt])
                        if tt >= 1:
                            ve.wait_ge(sPE.h, sn_xtr_pe[tt - 1])
                        self.emit_ln(xbuf, xlnb[:], self.c_eps_ln[:, 0:1])
                        sn_x[tt] = self.sVE.n
                        if tt + 2 < NT:
                            sy.wait_ge(sVE.h, sn_x[tt])
                            sLD.inc(sy.dma_start(out=xls[:, tt % 2, :],
                                                 in_=self.X[(tt + 2) * 128:(tt + 3) * 128, :]))
                            xld.append(sLD.n)
                        if tt >= 1:
                            te.wait_ge(sVE.h, sn_t[tt - 1])
                        sVE.wait(te)
                        for c in range(NCH):
                            ins = te.transpose(ps_t[:, c * 128:(c + 1) * 128],
                                               xlnb[:, c * 128:(c + 1) * 128],
                                               self.ident_b[:])
                        sPE.inc(ins)
                        sn_xtr_pe[tt] = sPE.n
                        sPE.wait(ve)
                        sVE.inc(ve.tensor_copy(out=xlf_t[:], in_=ps_t[:]))
                        sn_t[tt] = sVE.n
                        xlf_ready = sVE.n

                        for nb in range(4):
                            te.wait_ge(sVE.h, xlf_ready)
                            if tt >= 1:
                                te.wait_ge(sVE.h, sn_qk_ve[tt - 1])
                                te.wait_ge(sAC.h, sn_qk_ac[tt - 1])
                            if nb == 0 and tt >= 1:
                                ve.wait_ge(sPE.h, sn_qtr_pe[tt - 1])
                            bank = ps_qk[:, nb * 512:(nb + 1) * 512]
                            for c in range(NCH):
                                te.matmul(bank, xlf_t[:, c * 128:(c + 1) * 128],
                                          wqk_bv[:, c, nb * 512:(nb + 1) * 512],
                                          start=(c == 0), stop=False)
                            sPE.inc(te.matmul(bank, self.ones_b[0:1, 0:128],
                                              self.bqkv_b[0:1, nb * 512:(nb + 1) * 512],
                                              start=False, stop=True))
                            sPE.wait(se)
                            sAC.inc(se.activation(out=q2sb[:, nb * 512:(nb + 1) * 512],
                                                  in_=bank, func=AF.Square))
                            sAC.wait(ve)
                            sVE.inc(ve.tensor_reduce(
                                out=s2h[:, nb * 8:(nb + 1) * 8],
                                in_=q2sb.ap().rearrange("p (h d) -> p h d", d=DH)[:, nb * 8:(nb + 1) * 8, :],
                                op=ALU.add, axis=AX.X))
                            sVE.wait(se)
                            sAC.inc(se.activation(out=rstdh[:, nb * 8:(nb + 1) * 8],
                                          in_=s2h[:, nb * 8:(nb + 1) * 8], func=AF.Ln,
                                          bias=self.c_eps_rms[:, 0:1], scale=1.0 / DH))
                            sAC.wait(se)
                            sAC.inc(se.activation(out=rstdh[:, nb * 8:(nb + 1) * 8],
                                                  in_=rstdh[:, nb * 8:(nb + 1) * 8],
                                                  func=AF.Exp, scale=-0.5))
                            sAC.wait(ve)
                            ve.tensor_tensor(
                                out=qn1.ap().rearrange("p (h d) -> p h d", d=DH),
                                in0=ps_qk.ap().rearrange("p (h d) -> p h d", d=DH)[:, nb * 8:(nb + 1) * 8, :],
                                in1=rstdh[:, nb * 8:(nb + 1) * 8].unsqueeze(2).to_broadcast([128, 8, DH]),
                                op=ALU.mult)
                            sVE.inc(ve.tensor_tensor(
                                out=qkn[:, nb * 512:(nb + 1) * 512], in0=qn1[:],
                                in1=self.wqkb[:, nb * 512:(nb + 1) * 512], op=ALU.mult))
                        sn_qk_ve[tt] = sVE.n
                        sn_qk_ac[tt] = sAC.n

                        sVE.wait(se)
                        sAC.inc(se.activation(out=q2sb[:], in_=qkn[:], func=AF.Square))
                        sAC.wait(ve)
                        sVE.inc(ve.tensor_reduce(
                            out=rtm[:],
                            in_=q2sb.ap().rearrange("p (h d) -> p h d", d=DH),
                            op=ALU.add, axis=AX.X))
                        sVE.wait(se)
                        sAC.inc(se.activation(out=rtm[:], in_=rtm[:], func=AF.Ln,
                                      bias=self.c_eps_rms[:, 0:1], scale=1.0))
                        sAC.wait(se)
                        sAC.inc(se.activation(out=rtm[:], in_=rtm[:], func=AF.Exp,
                                              scale=0.5))
                        if tt >= 1:
                            te.wait_ge(sVE.h, sn_r[tt - 1])
                        sAC.wait(te)
                        sPE.inc(te.transpose(ps_r[:], rtm[:, 0:32], self.ident_f[:]))
                        sPE.wait(ve)
                        sVE.inc(ve.tensor_copy(out=r_fm[:, tt * 128:(tt + 1) * 128],
                                               in_=ps_r[:]))
                        sn_r[tt] = sVE.n

                        if tt >= 1:
                            te.wait_ge(sVE.h, sn_qt[tt - 1])
                        for p in range(NP):
                            te.transpose(ps_qt[:, p * 128:(p + 1) * 128],
                                         qkn[:, p * 128:(p + 1) * 128], self.ident_b[:])
                            ins = te.transpose(ps_kt[:, p * 128:(p + 1) * 128],
                                               qkn[:, 1024 + p * 128:1024 + (p + 1) * 128],
                                               self.ident_b[:])
                        sPE.inc(ins)
                        sn_qtr_pe[tt] = sPE.n
                        sPE.wait(ve)
                        if tt >= 2:
                            ve.wait_ge(sST.h, sn_qst[tt - 2])
                        ve.tensor_copy(out=qstg[:, tt % 2, :], in_=ps_qt[:])
                        sVE.inc(ve.tensor_copy(
                            out=K_one.ap().rearrange("p (c t) -> p c t", c=NP)[:, :, tt * 128:(tt + 1) * 128],
                            in_=ps_kt.ap().rearrange("p (c t) -> p c t", c=NP)))
                        sn_qt[tt] = sVE.n
                        sVE.wait(sy)
                        sST.inc(sy.dma_start(out=self.QS[tt, :, :],
                                             in_=qstg.ap()[:, tt % 2, :]))
                        sn_qst[tt] = sST.n
                    sVE.wait(sy)
                    sVE.wait(te)
                    sAC.wait(te)

                with ExitStack() as a1b:
                    wv_b = self.sb(a1b, "wv_b", [128, NCH * 1024], BF16)
                    wv_bv = wv_b.ap().rearrange("p (c n) -> p c n", c=NCH)
                    ps_v = ps_qk
                    cast_done = []
                    for c in range(NCH):
                        if c >= 1:
                            sy.wait_ge(sVE.h, cast_done[c - 1])
                        sLD.inc(sy.dma_start(out=wst[:, 0:1024],
                                             in_=self.WQKVT[c * 128:(c + 1) * 128, 2048:3072]))
                        sLD.wait(ve)
                        sVE.inc(ve.tensor_copy(out=wv_bv[:, c, :], in_=wst[:, 0:1024]))
                        cast_done.append(sVE.n)
                    sVE.inc(ve.memset(V_aug[:], 1.0))
                    vones_ready = sVE.n

                    NCK = NE * 4
                    pslot = [wst.ap()[:, 0:512], wst.ap()[:, 512:1024],
                             wst.ap()[:, 1024:1536], wst.ap()[:, 1536:2048]]
                    pwq = qkn.ap()
                    pjobs = []
                    for ee in range(NE):
                        for k8 in range(8):
                            pjobs.append((self.W13[ee][:, k8 * 1024:(k8 + 1) * 1024],
                                          self.W13B[ee][:, k8 * 1024:(k8 + 1) * 1024]))
                    for ck in range(NCK):
                        pjobs.append((self.W2[ck], self.W2B[ck]))
                    pj_cast = {}
                    pj_store = {}
                    a1b_ve0 = sVE.n
                    a1b_ac0 = self.sAC.n

                    def emit_pjob(p):
                        src, dst = pjobs[p]
                        s0 = (2 * p) % 4
                        if p >= 2:
                            sy.wait_ge(sAC.h, pj_cast[p - 2])
                        else:
                            sy.wait_ge(sVE.h, a1b_ve0)
                            sy.wait_ge(sAC.h, a1b_ac0)
                        sLD.inc(sy.dma_start(out=pslot[s0], in_=src[:, 0:512]))
                        sLD.inc(sy.dma_start(out=pslot[s0 + 1], in_=src[:, 512:1024]))
                        ld = sLD.n
                        se.wait_ge(sLD.h, ld)
                        if p >= 2:
                            se.wait_ge(sST.h, pj_store[p - 2])
                        se.activation(out=pwq[:, s0 * 512:(s0 + 1) * 512],
                                      in_=pslot[s0], func=AF.Identity)
                        self.sAC.inc(se.activation(
                            out=pwq[:, (s0 + 1) * 512:(s0 + 2) * 512],
                            in_=pslot[s0 + 1], func=AF.Identity))
                        pj_cast[p] = self.sAC.n
                        sy.wait_ge(sAC.h, pj_cast[p])
                        sST.inc(sy.dma_start(out=dst,
                                             in_=pwq[:, s0 * 512:(s0 + 2) * 512]))
                        pj_store[p] = sST.n

                    xld = []
                    sLD.inc(sy.dma_start(out=xls[:, 0, :], in_=self.X[0:128, :]))
                    xld.append(sLD.n)
                    sLD.inc(sy.dma_start(out=xls[:, 1, :], in_=self.X[128:256, :]))
                    xld.append(sLD.n)
                    sn_x = [0] * NT
                    sn_t = [0] * NT
                    sn_v_ac = [0] * NT
                    for tt in range(NT):
                        xbuf = xls[:, tt % 2, :]
                        sLD.wait(ve, at=xld[tt])
                        self.emit_ln(xbuf, xlnb[:], self.c_eps_ln[:, 0:1])
                        sn_x[tt] = sVE.n
                        if tt + 2 < NT:
                            sy.wait_ge(sVE.h, sn_x[tt])
                            sLD.inc(sy.dma_start(out=xls[:, tt % 2, :],
                                                 in_=self.X[(tt + 2) * 128:(tt + 3) * 128, :]))
                            xld.append(sLD.n)
                        if tt >= 1:
                            te.wait_ge(sVE.h, sn_t[tt - 1])
                        sVE.wait(te)
                        for c in range(NCH):
                            ins = te.transpose(ps_t[:, c * 128:(c + 1) * 128],
                                               xlnb[:, c * 128:(c + 1) * 128],
                                               self.ident_b[:])
                        sPE.inc(ins)
                        sPE.wait(ve)
                        sVE.inc(ve.tensor_copy(out=xlf_t[:], in_=ps_t[:]))
                        sn_t[tt] = sVE.n
                        xlf_ready = sVE.n
                        for nb in range(2):
                            te.wait_ge(sVE.h, xlf_ready)
                            if tt == 0:
                                te.wait_ge(sVE.h, vones_ready)
                            if tt >= 1:
                                te.wait_ge(sVE.h, sn_v_ac[tt - 1])
                            bank = ps_v[:, nb * 512:(nb + 1) * 512]
                            for c in range(NCH):
                                te.matmul(bank, xlf_t[:, c * 128:(c + 1) * 128],
                                          wv_bv[:, c, nb * 512:(nb + 1) * 512],
                                          start=(c == 0), stop=False)
                            sPE.inc(te.matmul(bank, self.ones_b[0:1, 0:128],
                                              self.bqkv_b[0:1, 2048 + nb * 512:2048 + (nb + 1) * 512],
                                              start=False, stop=True))
                            sPE.wait(ve)
                            sVE.inc(ve.tensor_copy(
                                out=V_aug.ap().rearrange("p (t h e) -> p t h e", t=NT, h=NH)[:, tt, nb * 8:(nb + 1) * 8, 0:64],
                                in_=bank.rearrange("p (h e) -> p h e", e=DH)))
                        sn_v_ac[tt] = sVE.n
                        for p in range(6 * tt, 6 * tt + 6):
                            emit_pjob(p)
                    self.wcast_done = True
                    sVE.wait(sy)
                    sVE.wait(te)
                    sAC.wait(te)
                    sAC.wait(sy)
                    sST.wait(sy)
                    sST.wait(se)
                    sST.wait(ve)

            if self.taps:
                self.tap("t_kone", K_one.ap())
                self.tap("t_vaug", V_aug.ap())
                self.tap("t_rfm", r_fm.ap())
                self.tap("t_qs", self.QS)

            with ExitStack() as nmctx:
                rk_col = self.sb(nmctx, "rk_col", [32, 1], F32)
                rqb32 = self.sb(nmctx, "rqb32", [32, 8], F32)
                rqbT = self.sb(nmctx, "rqbT", [8, 32], F32)
                rkT = self.sb(nmctx, "rkT", [1, 32], F32)
                rq_row = self.sb(nmctx, "rq_row", [1, 256], F32)
                negm_row = self.sb(nmctx, "negm_row", [1, 128], F32)
                ps_nm = self.psum(nmctx, "ps_nm", [128, 512], F32)
                sVE.inc(ve.tensor_reduce(out=rk_col[:], in_=r_fm.ap(),
                                         op=ALU.max, axis=AX.X))
                sVE.inc(ve.tensor_reduce(
                    out=rqb32[:],
                    in_=r_fm.ap().rearrange("h (q t) -> h q t", q=8),
                    op=ALU.max, axis=AX.X))
                sVE.wait(te)
                te.transpose(ps_nm[0:8, 0:32], rqb32[:], self.ident_f[0:32, 0:32])
                sPE.inc(te.transpose(ps_nm[0:1, 32:64], rk_col[:],
                                     self.ident_f[0:32, 0:32]))
                sPE.wait(ve)
                ve.tensor_copy(out=rqbT[:], in_=ps_nm[0:8, 0:32])
                sVE.inc(ve.tensor_copy(out=rkT[:], in_=ps_nm[0:1, 32:64]))
                sVE.wait(sy)
                sST.inc(sy.dma_start(out=self.MSC, in_=rqbT[:]))
                sST.wait(sy)
                sLD.inc(sy.dma_start(out=rq_row[:],
                                     in_=self.MSC.rearrange("a b -> (a b)").unsqueeze(0)))
                sLD.wait(ve)
                sVE.inc(ve.scalar_tensor_tensor(
                    out=negm_row.ap().rearrange("a (q h) -> a q h", h=16),
                    in0=rq_row.ap().rearrange("a (q h) -> a q h", h=32)[:, :, 0:16],
                    scalar=-0.5,
                    in1=rkT.ap()[0:1, 16:32].unsqueeze(1).to_broadcast([1, 8, 16]),
                    op0=ALU.mult, op1=ALU.mult))
                sVE.wait(te)
                sPE.inc(te.matmul(ps_nm[:, 0:128], self.ones_f[0:1, :],
                                  negm_row[:], start=True, stop=True))
                sPE.wait(ve)
                sVE.inc(ve.tensor_copy(out=negm_mat[:], in_=ps_nm[:, 0:128]))
                self.tap("t_negm", negm_mat.ap())
                sVE.wait(te)
                sVE.wait(sy)
            rctx.close()
            a1outer.close()

            with ExitStack() as a2:
                P = self.sb(a2, "P", [128, 2 * NT * 512], BF16)
                Qb = self.sb(a2, "Qb", [128, 2, 512], BF16)
                rl = self.sb(a2, "rl", [1, 512], F32)
                bc_sb = self.sb(a2, "bc_sb", [64, 512], F32)
                zstg = self.sb(a2, "zstg", [128, 2, 512], BF16)
                ps_sA = self.psum(a2, "ps_sA", [128, 1536], F32)
                ps_sB = self.psum(a2, "ps_sB", [128, 1536], F32)
                ps_z = self.psum(a2, "ps_z", [128, 512], F32)
                ps_m = self.psum(a2, "ps_m", [128, 512], F32)

                Vv = V_aug.ap().rearrange("p (t h e) -> p t h e", t=NT, h=NH)
                Kv = K_one.ap().rearrange("p (c t) -> p c t", c=NP)

                groups = [(g * 3, min(3, NT - g * 3)) for g in range((NT + 2) // 3)]
                sn_grp = {0: 0, 1: 0}
                sn_z_ve = 0
                sn_bc_ac = 0
                sn_exp_done = {}
                sn_S_pe = {}
                sn_q = {}
                sn_zst = {}
                qbs = range(NTB) if A2_QBS is None else A2_QBS
                heads = [(qb, pc, hh) for qb in qbs for pc in range(NP)
                         for hh in range(2)]
                Pv = P.ap().rearrange("p (b n) -> p b n", b=2)

                def emit_S(i):
                    nonlocal sn_z_ve
                    qb, pc, hh = heads[i]
                    pair = i // 2
                    base = 64 * hh
                    if hh == 0:
                        if pair >= 2:
                            sy.wait_ge(sPE.h, sn_S_pe.get(pair - 2, 0))
                        sLD.inc(sy.dma_start(
                            out=Qb.ap()[:, pc % 2, :].rearrange("p (t d) -> p t d", t=4),
                            in_=self.QS.rearrange("t p d -> p t d")[:, qb * 4:(qb + 1) * 4, pc * 128:(pc + 1) * 128]))
                        sn_q[pair] = sLD.n
                    bias_ap = negm_mat[:, qb * 16 + 2 * pc + hh:qb * 16 + 2 * pc + hh + 1]
                    sLD.wait(te, at=sn_q[pair])
                    for gi, (k0, glen) in enumerate(groups):
                        pss = ps_sA if gi % 2 == 0 else ps_sB
                        te.wait_ge(sAC.h, sn_grp[gi % 2])
                        for kk in range(glen):
                            kt = k0 + kk
                            ins = te.matmul(
                                pss[:, kk * 512:(kk + 1) * 512],
                                Kv[base:base + 64, pc, kt * 128:(kt + 1) * 128],
                                Qb[base:base + 64, pc % 2, :],
                                start=True, stop=True)
                        sPE.inc(ins)
                        sPE.wait(se)
                        sAC.inc(se.activation(
                            out=Pv[:, i % 2, k0 * 512:(k0 + glen) * 512],
                            in_=pss[:, 0:glen * 512],
                            func=AF.Exp, bias=bias_ap, scale=1.0))
                        sn_grp[gi % 2] = sAC.n
                    sn_exp_done[i] = sAC.n
                    sn_S_pe[pair] = sPE.n

                def emit_AV(i):
                    nonlocal sn_z_ve, sn_bc_ac
                    qb, pc, hh = heads[i]
                    h = 2 * pc + hh
                    base = 64 * hh
                    te.wait_ge(sVE.h, sn_z_ve)
                    te.wait_ge(sAC.h, sn_exp_done[i])
                    for kt in range(NT):
                        ins = te.matmul(ps_z[0:65, :],
                                        Vv[:, kt, h, :],
                                        Pv[:, i % 2, kt * 512:(kt + 1) * 512],
                                        start=(kt == 0), stop=(kt == NT - 1))
                    sPE.inc(ins)
                    sPE.wait(ve)
                    sVE.inc(ve.tensor_scalar(out=rl[:], in0=ps_z[64:65, :],
                                             scalar1=1e-30, scalar2=None, op0=ALU.add))
                    sVE.wait(ve)
                    sVE.inc(ve.reciprocal(out=rl[:], in_=rl[:]))
                    sn_rl = sVE.n
                    te.wait_ge(sVE.h, sn_rl)
                    te.wait_ge(sAC.h, sn_bc_ac)
                    sPE.inc(te.matmul(ps_m[0:64, :], self.ones_f[0:1, 0:64],
                                      rl[:], start=True, stop=True))
                    sPE.wait(se)
                    se.wait_ge(sVE.h, sn_z_ve)
                    sAC.inc(se.activation(out=bc_sb[:], in_=ps_m[0:64, :],
                                          func=AF.Identity))
                    sn_bc_ac = sAC.n
                    sAC.wait(ve)
                    pair = i // 2
                    if pair >= 2:
                        ve.wait_ge(sST.h, sn_zst.get(pair - 2, 0))
                    sVE.inc(ve.tensor_tensor(out=zstg[base:base + 64, pc % 2, :],
                                             in0=ps_z[0:64, :], in1=bc_sb[:],
                                             op=ALU.mult))
                    sn_z_ve = sVE.n
                    if hh == 1:
                        sVE.wait(sy)
                        sST.inc(sy.dma_start(
                            out=self.ZS[pc, :, qb * 512:(qb + 1) * 512],
                            in_=zstg.ap()[:, pc % 2, :]))
                        sn_zst[pair] = sST.n

                for i in range(len(heads)):
                    emit_S(i)
                    if i > 0:
                        emit_AV(i - 1)
                emit_AV(len(heads) - 1)
                sVE.wait(te)
                sAC.wait(te)
                sVE.wait(sy)
                self.tap("t_p", P.ap())
                self.tap("t_qb", Qb.ap())

            if self.taps:
                self.tap("t_zs", self.ZS)
            kvctx.close()
            with ExitStack() as a3:
                wo_b = self.sb(a3, "wo_b", [128, NCH * 1024], BF16)
                wo_bv = wo_b.ap().rearrange("p (c n) -> p c n", c=NCH)
                wstA3 = self.sb(a3, "wstA3", [128, 1024], F32)
                zbuf = self.sb(a3, "zbuf", [128, 2, NP, 512], BF16)
                xrt = self.sb(a3, "xrt", [128, 4, D], F32)
                ofm = self.sb(a3, "ofm", [128, 4096], BF16)
                xstg = self.sb(a3, "xstg", [128, 2, D], F32)
                ps_o = self.psum(a3, "ps_o", [128, 2048], F32)
                ps_T = self.psum(a3, "ps_T", [128, 1024], BF16)

                cast_done = []
                for c in range(NCH):
                    if c >= 1:
                        sy.wait_ge(sVE.h, cast_done[c - 1])
                    sLD.inc(sy.dma_start(out=wstA3[:],
                                         in_=self.WOT[c * 128:(c + 1) * 128, :]))
                    sLD.wait(ve)
                    sVE.inc(ve.tensor_copy(out=wo_bv[:, c, :], in_=wstA3[:]))
                    cast_done.append(sVE.n)

                zld = []
                for pc in range(NP):
                    sLD.inc(sy.dma_start(out=zbuf[:, 0, pc, :],
                                         in_=self.ZS[pc, :, 0:512]))
                zld.append(sLD.n)
                sn_o_ac = [0] * NTB
                sn_T_ve = [0] * NTB
                xst_list = []
                for tb in range(NTB):
                    if tb + 1 < NTB:
                        if tb >= 1:
                            sy.wait_ge(sPE.h, self.sPE.n)
                        for pc in range(NP):
                            sLD.inc(sy.dma_start(
                                out=zbuf[:, (tb + 1) % 2, pc, :],
                                in_=self.ZS[pc, :, (tb + 1) * 512:(tb + 2) * 512]))
                        zld.append(sLD.n)
                    if tb >= 1:
                        sy.wait_ge(sVE.h, sn_T_ve[tb - 1])
                    for ts4 in range(4):
                        sLD.inc(sy.dma_start(
                            out=xrt[:, ts4, :],
                            in_=self.X[tb * 512 + ts4 * 128:tb * 512 + (ts4 + 1) * 128, :]))
                    xrld = sLD.n
                    for half in range(2):
                        sLD.wait(te, at=zld[tb])
                        if tb >= 1 or half >= 1:
                            te.wait_ge(sAC.h, sn_o_ac[tb - 1] if half == 0 else self.sAC.n)
                        for oc4 in range(4):
                            oc = half * 4 + oc4
                            bank = ps_o[:, oc4 * 512:(oc4 + 1) * 512]
                            for zc in range(NP):
                                te.matmul(bank, wo_bv[:, zc, oc * 128:(oc + 1) * 128],
                                          zbuf[:, tb % 2, zc, :],
                                          start=(zc == 0), stop=False)
                            sPE.inc(te.matmul(bank, self.bo_b[0:1, oc * 128:(oc + 1) * 128],
                                              self.ones_b[0:1, :],
                                              start=False, stop=True))
                            sPE.wait(se)
                            sAC.inc(se.activation(out=ofm[:, oc * 512:(oc + 1) * 512],
                                                  in_=bank, func=AF.Identity))
                        sn_o_ac[tb] = sAC.n
                    for ts4 in range(4):
                        if tb >= 1 and ts4 == 0:
                            te.wait_ge(sVE.h, sn_T_ve[tb - 1])
                        if ts4 >= 1:
                            te.wait_ge(sVE.h, self.sVE.n)
                        sAC.wait(te)
                        for oc in range(NCH):
                            ins = te.transpose(ps_T[:, oc * 128:(oc + 1) * 128],
                                               ofm[:, oc * 512 + ts4 * 128:oc * 512 + (ts4 + 1) * 128],
                                               self.ident_b[:])
                        sPE.inc(ins)
                        sPE.wait(ve)
                        sLD.wait(ve, at=xrld)
                        if len(xst_list) >= 2:
                            ve.wait_ge(sST.h, xst_list[-2])
                        sVE.inc(ve.tensor_tensor(out=xstg[:, ts4 % 2, :], in0=ps_T[:],
                                                 in1=xrt[:, ts4, :], op=ALU.add))
                        sn_T_ve[tb] = sVE.n
                        sVE.wait(sy)
                        r0 = tb * 512 + ts4 * 128
                        sST.inc(sy.dma_start(out=self.XR[r0:r0 + 128, :],
                                             in_=xstg.ap()[:, ts4 % 2, :]))
                        xst_list.append(sST.n)
                sVE.wait(te)
                sAC.wait(te)
                sVE.wait(sy)
                sVE.wait(ge)

    def emit_ln_pass(self, src_dram, xlf, fstage, xlnb, ps_t):
        te, ve, se, sy = self.te, self.ve, self.se, self.sy
        sLD, sPE, sVE, sAC = self.sLD, self.sPE, self.sVE, self.sAC
        xlf_v = xlf.ap().rearrange("p (c t) -> p c t", c=NCH)
        xld = []
        sVE.wait(sy)
        self.sST.wait(sy)
        sLD.inc(sy.dma_start(out=fstage[:, 0, :], in_=src_dram[0:128, :]))
        xld.append(sLD.n)
        sLD.inc(sy.dma_start(out=fstage[:, 1, :], in_=src_dram[128:256, :]))
        xld.append(sLD.n)
        sn_x = [0] * NT
        sn_t = [0] * NT
        sn_pe = [0] * NT
        for tt in range(NT):
            xbuf = fstage[:, tt % 2, :]
            sLD.wait(ve, at=xld[tt])
            if tt >= 1:
                ve.wait_ge(sPE.h, sn_pe[tt - 1])
            self.emit_ln(xbuf, xlnb[:], self.c_eps_ln[:, 0:1])
            sn_x[tt] = sVE.n
            if tt + 2 < NT:
                sy.wait_ge(sVE.h, sn_x[tt])
                sLD.inc(sy.dma_start(out=fstage[:, tt % 2, :],
                                     in_=src_dram[(tt + 2) * 128:(tt + 3) * 128, :]))
                xld.append(sLD.n)
            if tt >= 1:
                te.wait_ge(sVE.h, sn_t[tt - 1])
            sVE.wait(te)
            for c in range(NCH):
                ins = te.transpose(ps_t[:, c * 128:(c + 1) * 128],
                                   xlnb[:, c * 128:(c + 1) * 128], self.ident_b[:])
            sPE.inc(ins)
            sn_pe[tt] = sPE.n
            sPE.wait(ve)
            sVE.inc(ve.tensor_copy(
                out=xlf_v[:, :, tt * 128:(tt + 1) * 128],
                in_=ps_t.rearrange("p (c t) -> p c t", c=NCH)))
            sn_t[tt] = sVE.n

    def phase_ffn(self, src_dram, dst_dram, first_call):
        sfx = "a" if first_call else "b"
        te, ve, se, ge, sy = self.te, self.ve, self.se, self.ge, self.sy
        sLD, sST, sPE, sVE, sAC = self.sLD, self.sST, self.sPE, self.sVE, self.sAC
        NCK = NE * 4

        with ExitStack() as f:
            xlf = self.sb(f, "xlf" + sfx, [128, NCH * T], BF16)
            act_all = self.sb(f, "act_all" + sfx, [128, NCK * 512], BF16)
            w13b = self.sb(f, "w13b" + sfx, [128, 2, 2 * NCH * 512], BF16)
            w2b = self.sb(f, "w2b" + sfx, [128, 6, D], BF16)
            sbc = self.sb(f, "sbc" + sfx, [128, 512], F32)
            s_t = self.sb(f, "s_t" + sfx, [16, 128], BF16)
            s_row = self.sb(f, "s_row" + sfx, [1, 2, 512], BF16)
            fstage = self.sb(f, "fstage" + sfx, [128, 4, D], F32)
            xlnb = self.sb(f, "xlnb2" + sfx, [128, D], BF16)
            gw_b = self.sb(f, "gw_b" + sfx, [128, NCH * 16], BF16)
            rsc = self.sb(f, "rsc" + sfx, [128, 64], F32)
            ofm = self.sb(f, "ofm2" + sfx, [128, 4096], BF16)
            ps_g = self.psum(f, "ps_g" + sfx, [128, 2048], F32)
            ps_u = self.psum(f, "ps_u" + sfx, [128, 2048], F32)
            ps_t = ps_g.ap()[:, 0:512].bitcast(BF16)
            ps_o_banks = ([ps_g.ap()[:, i * 512:(i + 1) * 512] for i in range(4)]
                          + [ps_u.ap()[:, i * 512:(i + 1) * 512] for i in range(4)])

            if first_call and not getattr(self, "wcast_done", False):
                stgv = act_all.ap().bitcast(F32)
                stg_s = [stgv[:, 0:8192], stgv[:, 8192:16384]]
                wsb_s = [xlf.ap()[:, 0:8192], xlf.ap()[:, 8192:16384]]
                jobs = []
                for e in range(NE):
                    jobs.append((self.W13[e], self.W13B[e], 8192))
                for ck in range(NCK):
                    jobs.append((self.W2[ck], self.W2B[ck], 1024))
                sn_cast = {}
                sn_store = {}
                for j, (src, dst, wdt) in enumerate(jobs):
                    sl = j % 2
                    if j >= 2:
                        sy.wait_ge(sVE.h, sn_cast[j - 2])
                    sLD.inc(sy.dma_start(out=stg_s[sl][:, 0:wdt], in_=src))
                    ld = sLD.n
                    sLD.wait(ve, at=ld)
                    if j >= 2:
                        ve.wait_ge(sST.h, sn_store[j - 2])
                    sVE.inc(ve.tensor_copy(out=wsb_s[sl][:, 0:wdt],
                                           in_=stg_s[sl][:, 0:wdt]))
                    sn_cast[j] = sVE.n
                    sy.wait_ge(sVE.h, sn_cast[j])
                    sST.inc(sy.dma_start(out=dst, in_=wsb_s[sl][:, 0:wdt]))
                    sn_store[j] = sST.n
                sST.wait(sy)
                sST.wait(ve)
                sLD.wait(ve)

            self.emit_ln_pass(src_dram, xlf, fstage, xlnb,
                              ps_t=ps_u.ap()[:, 0:512].bitcast(BF16))
            xlf_v = xlf.ap().rearrange("p (c t) -> p c t", c=NCH)

            with ExitStack() as g:
                gstg = self.sb(g, "gstg" + sfx, [128, NCH * 16], F32)
                sVE.wait(sy)
                sLD.inc(sy.dma_start(
                    out=gstg.ap().rearrange("p (c n) -> p c n", c=NCH),
                    in_=self.GWT.rearrange("(c p) n -> p c n", p=128)))
                sLD.wait(ve)
                sVE.inc(ve.tensor_copy(out=gw_b[:], in_=gstg[:]))
            gw_v = gw_b.ap().rearrange("p (c n) -> p c n", c=NCH)

            ps_rt = ps_g.ap()[:, 0:512]
            lg = rsc[:, 0:16]
            t8 = rsc[:, 16:24]
            neg = rsc[:, 24:25]
            thr = rsc[:, 25:26]
            s4 = rsc[:, 26:27]
            rcp = rsc[:, 27:28]
            es = rsc[:, 32:48]
            sct = rsc[:, 48:64]
            sn_rt = 0
            sn_st1 = sn_st2 = 0
            for tt in range(NT):
                te.wait_ge(sVE.h, self.sVE.n)
                for c in range(NCH):
                    ins = te.matmul(ps_rt[:, 0:16], xlf_v[:, c, tt * 128:(tt + 1) * 128],
                                    gw_v[:, c, :], start=(c == 0), stop=(c == NCH - 1))
                sPE.inc(ins)
                sPE.wait(ve)
                sVE.inc(ve.tensor_copy(out=lg, in_=ps_rt[:, 0:16]))
                sVE.wait(ve)
                sVE.inc(ve.max(out=t8, in_=lg))
                sVE.wait(ve)
                sVE.inc(ve.tensor_scalar_mul(neg, t8[:, 0:1], -1.0))
                sVE.wait(se)
                sAC.inc(se.activation(out=es, in_=lg, func=AF.Exp, bias=neg[:, 0:1],
                                      scale=1.0))
                sAC.wait(se)
                sAC.inc(se.activation(out=thr, in_=t8[:, 3:4], func=AF.Exp,
                                      bias=neg[:, 0:1], scale=1.0))
                sAC.wait(ve)
                sVE.inc(ve.tensor_scalar(out=sct, in0=es, scalar1=thr[:, 0:1],
                                         scalar2=None, op0=ALU.is_ge))
                sVE.wait(ve)
                sVE.inc(ve.tensor_tensor(out=es, in0=es, in1=sct, op=ALU.mult))
                sVE.wait(ve)
                sVE.inc(ve.tensor_reduce(out=s4, in_=es, op=ALU.add, axis=AX.X))
                sVE.wait(ve)
                sVE.inc(ve.reciprocal(out=rcp, in_=s4))
                sVE.wait(ve)
                sVE.inc(ve.tensor_scalar(out=sct, in0=es, scalar1=rcp[:, 0:1],
                                         scalar2=None, op0=ALU.mult))
                te.wait_ge(sVE.h, self.sVE.n)
                if tt >= 1:
                    te.wait_ge(sVE.h, sn_rt)
                if tt >= 2:
                    ve.wait_ge(sST.h, sn_st2)
                sPE.inc(te.transpose(ps_rt[0:16, 16:144], sct, self.ident_f[:]))
                sPE.wait(ve)
                sVE.inc(ve.tensor_copy(out=s_t[:], in_=ps_rt[0:16, 16:144]))
                sn_rt = sVE.n
                sVE.wait(sy)
                sST.inc(sy.dma_start(out=self.SFD[:, tt * 128:(tt + 1) * 128],
                                     in_=s_t[:]))
                sn_st2 = sn_st1
                sn_st1 = sST.n

            w13v = w13b.ap().rearrange("p b (h c n) -> p b h c n", h=2, c=NCH)
            w1v = w13v[:, :, 0]
            w3v = w13v[:, :, 1]
            sn_w13 = {}
            sn_bc_pe = {}
            sn_gu_pe = {}
            sn_act_ve = {}
            sn_silu_ic = {}
            sn_vu_ic = {}
            sn_w2 = {}
            sn_w2_pe = {}
            sn_o_ac = 0
            sn_T_ve = 0
            sn_xst = []
            for gi0 in (0, 1):
                sLD.inc(sy.dma_start(out=w13b.ap()[:, gi0, :],
                                     in_=self.W13B[gi0 % NE]))
                sn_w13[gi0] = sLD.n

            for tb in range(NTB):
                sn_sbc_prev = 0
                for e in range(NE):
                    gi = tb * NE + e
                    if gi >= 2:
                        sy.wait_ge(sPE.h, sn_bc_pe.get(gi - 2, 0))
                    sLD.inc(sy.dma_start(out=s_row.ap()[:, gi % 2, :],
                                         in_=self.SFD[e:e + 1, tb * 512:(tb + 1) * 512]))
                    sn_srow = sLD.n
                    te.wait_ge(sAC.h, max(
                        sn_silu_ic[gi - 1][0] if (gi - 1) in sn_silu_ic else 0,
                        sn_o_ac))
                    te.wait_ge(sVE.h, sn_sbc_prev)
                    sLD.wait(te, at=sn_srow)
                    sPE.inc(te.matmul(ps_g.ap()[:, 0:512], self.ones_b[0:1, 0:128],
                                      s_row.ap()[:, gi % 2, :],
                                      start=True, stop=True))
                    sn_bc_pe[gi] = sPE.n
                    sPE.wait(ve)
                    sVE.inc(ve.tensor_copy(out=sbc[:], in_=ps_g.ap()[:, 0:512]))
                    sn_sbc = sn_sbc_prev = sVE.n

                    te.wait_ge(sLD.h, sn_w13[gi])
                    te.wait_ge(sVE.h, sn_sbc)
                    prev_silu = sn_silu_ic.get(gi - 1)
                    prev_vu = sn_vu_ic.get(gi - 1)
                    cur_silu = [0] * 4
                    cur_vu = [0] * 4
                    for ic in range(4):
                        gbank = ps_g.ap()[:, ic * 512:(ic + 1) * 512]
                        ubank = ps_u.ap()[:, ic * 512:(ic + 1) * 512]
                        need_ac = sn_o_ac if e == 0 else 0
                        if prev_silu is not None:
                            need_ac = max(need_ac, prev_silu[ic])
                            te.wait_ge(sVE.h, prev_vu[ic])
                        if need_ac:
                            te.wait_ge(sAC.h, need_ac)
                        for c in range(NCH):
                            te.matmul(gbank, w1v[:, gi % 2, c, ic * 128:(ic + 1) * 128],
                                      xlf_v[:, c, tb * 512:(tb + 1) * 512],
                                      start=(c == 0), stop=(c == NCH - 1))
                        for c in range(NCH):
                            ins = te.matmul(ubank, w3v[:, gi % 2, c, ic * 128:(ic + 1) * 128],
                                            xlf_v[:, c, tb * 512:(tb + 1) * 512],
                                            start=(c == 0), stop=(c == NCH - 1))
                        sPE.inc(ins)
                        pe_ic = sPE.n
                        se.wait_ge(sPE.h, pe_ic)
                        sAC.inc(se.activation(out=ofm[:, ic * 512:(ic + 1) * 512],
                                              in_=gbank, func=AF.Silu))
                        cur_silu[ic] = sAC.n
                        ve.wait_ge(sPE.h, pe_ic)
                        sVE.inc(ve.tensor_tensor(
                            out=ofm[:, 2048 + ic * 512:2048 + (ic + 1) * 512],
                            in0=ubank, in1=sbc[:], op=ALU.mult))
                        cur_vu[ic] = sVE.n
                    sn_silu_ic[gi] = cur_silu
                    sn_vu_ic[gi] = cur_vu
                    sn_gu_pe[gi] = sPE.n
                    ni = gi + 2
                    if ni < NTB * NE:
                        sy.wait_ge(sPE.h, sn_gu_pe[gi])
                        sLD.inc(sy.dma_start(out=w13b.ap()[:, ni % 2, :],
                                             in_=self.W13B[ni % NE]))
                        sn_w13[ni] = sLD.n
                    ve.wait_ge(sAC.h, cur_silu[3])
                    sVE.inc(ve.tensor_tensor(
                        out=act_all[:, e * 2048:(e + 1) * 2048],
                        in0=ofm[:, 0:2048], in1=ofm[:, 2048:4096], op=ALU.mult))
                    sn_act_ve[gi] = sVE.n

                if tb >= 1:
                    sy.wait_ge(sPE.h, self.sPE.n)
                for pp in range(3):
                    sLD.inc(sy.dma_start(
                        out=w2b[:, 2 * pp:2 * pp + 2, :],
                        in_=self.W2B[2 * pp:2 * pp + 2].rearrange("k p d -> p k d")))
                    sn_w2[pp] = sLD.n
                for ck in range(NCK):
                    te.wait_ge(sLD.h, sn_w2[ck // 2])
                    if ck == 0:
                        te.wait_ge(sVE.h, sn_act_ve[tb * NE + NE - 1])
                        te.wait_ge(sAC.h, sn_silu_ic[tb * NE + NE - 1][3])
                        te.wait_ge(sVE.h, sn_sbc_prev)
                    for oc in range(NCH):
                        ins = te.matmul(ps_o_banks[oc],
                                        w2b[:, ck % 6, oc * 128:(oc + 1) * 128],
                                        act_all[:, ck * 512:(ck + 1) * 512],
                                        start=(ck == 0), stop=(ck == NCK - 1))
                    sPE.inc(ins)
                    sn_w2_pe[ck] = sPE.n
                    if ck % 2 == 1:
                        npair = ck // 2 + 3
                        if 2 * npair < NCK:
                            s0 = (2 * npair) % 6
                            sy.wait_ge(sPE.h, sn_w2_pe[ck])
                            sLD.inc(sy.dma_start(
                                out=w2b[:, s0:s0 + 2, :],
                                in_=self.W2B[2 * npair:2 * npair + 2]
                                .rearrange("k p d -> p k d")))
                            sn_w2[npair] = sLD.n
                sPE.wait(se)
                for oc in range(NCH):
                    ins = se.activation(out=ofm[:, oc * 512:(oc + 1) * 512],
                                        in_=ps_o_banks[oc], func=AF.Identity)
                sAC.inc(ins)
                sn_o_ac = self.sAC.n
                for ts4 in range(4):
                    r0 = tb * 512 + ts4 * 128
                    sy.wait_ge(sVE.h, sn_T_ve)
                    sLD.inc(sy.dma_start(out=fstage[:, ts4 % 2, :],
                                         in_=src_dram[r0:r0 + 128, :]))
                    xrld = sLD.n
                    sAC.wait(te)
                    te.wait_ge(sVE.h, sn_T_ve)
                    for oc in range(NCH):
                        ins = te.transpose(ps_t[:, oc * 128:(oc + 1) * 128],
                                           ofm[:, oc * 512 + ts4 * 128:oc * 512 + (ts4 + 1) * 128],
                                           self.ident_b[:])
                    sPE.inc(ins)
                    sPE.wait(ve)
                    sLD.wait(ve, at=xrld)
                    if len(sn_xst) >= 2:
                        ve.wait_ge(sST.h, sn_xst[-2])
                    sVE.inc(ve.tensor_tensor(out=fstage[:, 2 + ts4 % 2, :],
                                             in0=ps_t, in1=fstage[:, ts4 % 2, :],
                                             op=ALU.add))
                    sn_T_ve = sVE.n
                    sVE.wait(sy)
                    sST.inc(sy.dma_start(out=dst_dram[r0:r0 + 128, :],
                                         in_=fstage.ap()[:, 2 + ts4 % 2, :]))
                    sn_xst.append(sST.n)
            sVE.wait(te)
            sAC.wait(te)
            sVE.wait(sy)
            sVE.wait(ge)
            self.tap("t_sfd", self.SFD)
            self.tap("t_act", act_all.ap())
            self.tap("t_xlf", xlf.ap())
            self.tap("t_w13b", self.W13B)
            self.tap("t_w2b", self.W2B)

    def phase_b(self, src_dram, dst_dram):
        te, ve, se, ge, sy = self.te, self.ve, self.se, self.ge, self.sy
        sLD, sST, sPE, sVE, sAC = self.sLD, self.sST, self.sPE, self.sVE, self.sAC

        with ExitStack() as b:
            xlf = self.sb(b, "xlf3", [128, NCH * T], BF16)
            B1 = self.sb(b, "B1", [128, T], F32)
            B2 = self.sb(b, "B2", [128, T], F32)
            B3 = self.sb(b, "B3", [128, T], F32)
            B4 = self.sb(b, "B4", [128, T], F32)
            fstage = self.sb(b, "fstage3", [128, 4, D], F32)
            xlnb = self.sb(b, "xlnb3", [128, D], BF16)
            wg_b = self.sb(b, "wg_b", [128, 2, 128], BF16)
            wgstg = self.sb(b, "wgstg", [128, 128], F32)
            bcol = self.sb(b, "bcol", [128, 2], F32)
            hstage = self.sb(b, "hstage", [128, 2, 512], F32)
            xq = self.sb(b, "xq", [128, 2, 512], F32)
            ps_gx = self.psum(b, "ps_gx", [128, 2048], F32)
            ps_ht = self.psum(b, "ps_ht", [128, 512], F32)

            self.emit_ln_pass(src_dram, xlf, fstage, xlnb,
                              ps_t=ps_gx.ap()[:, 0:512].bitcast(BF16))
            xlf_v = xlf.ap().rearrange("p (c t) -> p c t", c=NCH)

            for c in range(NCH):
                for i, (WSRC, BSRC) in enumerate(((self.IGW, self.IGB),
                                                  (self.AGW, self.AGB))):
                    sVE.wait(sy)
                    sLD.inc(sy.dma_start(out=wgstg[:], in_=WSRC[c]))
                    sLD.inc(sy.dma_start(out=bcol[:, i:i + 1],
                                         in_=BSRC[c:c + 1, :].rearrange("a b -> b a")))
                    sLD.wait(ve)
                    sVE.inc(ve.tensor_copy(out=wg_b[:, i, :], in_=wgstg[:]))
                sn_wg = sVE.n
                for i, dstbuf in ((0, B1), (1, B2)):
                    for tb in range(NTB):
                        bank = ps_gx.ap()[:, (tb % 2) * 512:(tb % 2 + 1) * 512]
                        te.wait_ge(sVE.h, sn_wg)
                        te.wait_ge(sAC.h, self.sAC.n)
                        sPE.inc(te.matmul(bank, wg_b[:, i, :],
                                          xlf_v[:, c, tb * 512:(tb + 1) * 512],
                                          start=True, stop=True))
                        sPE.wait(se)
                        sAC.inc(se.activation(out=dstbuf[:, tb * 512:(tb + 1) * 512],
                                              in_=bank, func=AF.Sigmoid,
                                              bias=bcol[:, i:i + 1], scale=1.0))
                sAC.wait(ve)
                sVE.inc(ve.tensor_scalar(out=B2[:], in0=B2[:],
                                         scalar1=self.sp8[:, c:c + 1], scalar2=None,
                                         op0=ALU.mult))
                sVE.inc(ve.tensor_tensor_scan(out=B3[:], data0=B2[:], data1=B2[:],
                                              initial=0.0, op0=ALU.add, op1=ALU.bypass))
                sVE.inc(ve.tensor_scalar(out=B3[:], in0=B3[:], scalar1=LCLIP,
                                         scalar2=None, op0=ALU.max))
                ve.memset(B4[:, 0:1], 0.0)
                ve.tensor_copy(out=B4[:, 1:2], in_=B3[:, 0:1])
                sVE.inc(ve.tensor_tensor(out=B4[:, 2:T], in0=B3[:, 1:T - 1],
                                         in1=B3[:, 0:T - 2], op=ALU.subtract))
                sVE.wait(se)
                sAC.inc(se.activation(out=B4[:], in_=B4[:], func=AF.Exp))
                sAC.inc(se.activation(out=B2[:], in_=B2[:], func=AF.Exp, scale=2.0))
                sAC.wait(ve)
                ve.tensor_scalar(out=B2[:], in0=B2[:], scalar1=-1.0, scalar2=1.0,
                                 op0=ALU.mult, op1=ALU.add)
                sVE.inc(ve.tensor_scalar(out=B2[:], in0=B2[:], scalar1=1e-7,
                                         scalar2=None, op0=ALU.max))
                sVE.wait(se)
                se.activation(out=B2[:], in_=B2[:], func=AF.Ln)
                sAC.inc(se.activation(out=B2[:], in_=B2[:], func=AF.Exp, scale=0.5))
                sAC.wait(ve)
                ve.tensor_tensor(out=B1[:], in0=xlf_v[:, c, :], in1=B1[:], op=ALU.mult)
                sVE.inc(ve.tensor_tensor(out=B1[:], in0=B1[:], in1=B2[:], op=ALU.mult))
                sVE.inc(ve.tensor_tensor_scan(out=B3[:], data0=B4[:], data1=B1[:],
                                              initial=0.0, op0=ALU.mult, op1=ALU.add))
                sn_hst = []
                sn_hT = 0
                xqld = {}
                for q4 in range(NCH):
                    r0 = q4 * 512
                    dcol = dst_dram[r0:r0 + 512, c * 128:(c + 1) * 128] \
                        .rearrange("(q p) d -> p q d", p=128)
                    if q4 >= 2:
                        sy.wait_ge(sVE.h, sn_hT)
                    sLD.inc(sy.dma_start(out=xq.ap()[:, q4 % 2, :]
                                         .rearrange("p (q d) -> p q d", q=4),
                                         in_=dcol))
                    xqld[q4] = sLD.n
                    if q4 >= 1:
                        te.wait_ge(sVE.h, sn_hT)
                    sVE.wait(te)
                    for j in range(4):
                        tt = q4 * 4 + j
                        ins = te.transpose(ps_ht[:, j * 128:(j + 1) * 128],
                                           B3[:, tt * 128:(tt + 1) * 128],
                                           self.ident_f[:])
                    sPE.inc(ins)
                    sPE.wait(ve)
                    sLD.wait(ve, at=xqld[q4])
                    if len(sn_hst) >= 2:
                        ve.wait_ge(sST.h, sn_hst[-2])
                    sVE.inc(ve.tensor_tensor(out=hstage.ap()[:, q4 % 2, :],
                                             in0=ps_ht.ap(),
                                             in1=xq.ap()[:, q4 % 2, :], op=ALU.add))
                    sn_hT = sVE.n
                    sVE.wait(sy)
                    sST.inc(sy.dma_start(
                        out=dcol,
                        in_=hstage.ap()[:, q4 % 2, :].rearrange("p (q d) -> p q d", q=4)))
                    sn_hst.append(sST.n)
                sST.wait(sy)
            sVE.wait(te)
            sAC.wait(te)
            sVE.wait(sy)
            self.tap("t_B1", B1.ap())
            self.tap("t_B2", B2.ap())
            self.tap("t_B3", B3.ap())
            self.tap("t_B4", B4.ap())
            self.tap("t_xlf3", xlf.ap())

    def finish(self, copy_xr_to_out=False):
        ge, sy, ve = self.ge, self.sy, self.ve
        if copy_xr_to_out:
            self.sST.wait(sy)
            self.sST.inc(sy.dma_start(out=self.OUT, in_=self.XR))
        self.sST.wait(sy)
        self.sLD.wait(sy)
        self.sLG.wait(sy)
        self.sST.wait(ge)

    def build(self):
        self.prelude()
        if "A" in PHASES:
            self.phase_a()
        first = True
        if "F1" in PHASES:
            self.phase_ffn(self.XR, self.XR, first_call=first)
            first = False
        if "B" in PHASES:
            self.phase_b(self.XR, self.XR)
        if "F2" in PHASES:
            last = ("A" in PHASES and "F1" in PHASES and "B" in PHASES)
            self.phase_ffn(self.XR, self.OUT if last else self.XR,
                           first_call=first)
            first = False
            done_full = last
        else:
            done_full = False
        self.finish(copy_xr_to_out=not done_full)
        self.ctx.close()
        return self.nc


def _shard_inputs(inputs):
    f = np.ascontiguousarray
    x = inputs["L0_attn_a_x"]
    wq = np.asarray(inputs["L0_attn_a_wq"])
    wk = np.asarray(inputs["L0_attn_a_wk"])
    common = {
        "wqkvT": f(np.asarray(inputs["L0_attn_a_Wqkv"]).T),
        "bqkv": f(np.asarray(inputs["L0_attn_a_bqkv"]).reshape(1, 3 * D)),
        "woT": f(np.asarray(inputs["L0_attn_a_Wo"]).T),
        "bo": f(np.asarray(inputs["L0_attn_a_bo"]).reshape(1, D)),
        "wqkrow": f(np.concatenate([np.tile(wq, NH), np.tile(wk, NH)]).reshape(1, 2 * D)),
        "igw": f(np.asarray(inputs["L0_attn_b_ig_w"])),
        "igb": f(np.asarray(inputs["L0_attn_b_ig_b"])),
        "agw": f(np.asarray(inputs["L0_attn_b_ag_w"])),
        "agb": f(np.asarray(inputs["L0_attn_b_ag_b"])),
        "apar": f(np.asarray(inputs["L0_attn_b_a_param"]).reshape(NCH, 128)),
        "gwT": f(np.asarray(inputs["L0_ffn_gate_w"]).T),
        "w13": f(np.concatenate(
            [np.asarray(inputs["L0_ffn_w1"]).reshape(NE, NCH, 128, DI)
             .transpose(0, 2, 1, 3).reshape(NE, 128, NCH * DI),
             np.asarray(inputs["L0_ffn_w3"]).reshape(NE, NCH, 128, DI)
             .transpose(0, 2, 1, 3).reshape(NE, 128, NCH * DI)], axis=2)),
        "w2": f(np.asarray(inputs["L0_ffn_w2"]).reshape(NE * 4, 128, D)),
    }
    in_maps = []
    for i in range(8):
        m = dict(common)
        m["x"] = f(np.asarray(x[i]))
        in_maps.append(m)
    return in_maps


def kernel(**inputs):
    in_maps = _shard_inputs(inputs)
    nc = Builder().build()
    res = run_bass_kernel_spmd(nc, in_maps, core_ids=list(range(8)))
    out = np.stack([np.asarray(r["out"]) for r in res.results], axis=0)
    return out.astype(np.float32)
